# version 05
# speedup vs baseline: 1.2213x; 1.0541x over previous
import contextlib
import numpy as np
import ml_dtypes
import concourse.bass as bass
import concourse.mybir as mybir
from concourse.bass_utils import run_bass_kernel_spmd


"""Build kit: program-order recorder with per-engine + per-DMA-channel sems.

Soundness model (validated against the bass simulator's race detector):
- Compute engines complete in order; per-engine semaphore prefix waits are
  sound. Every compute op publishes +1 and all dependencies (same-engine
  included) are enforced with wait_ge (same-engine writeback ordering is
  NOT guaranteed otherwise — probe4).
- DMA completions are NOT ordered within a queue: a prefix wait on a
  shared DMA semaphore is unsound. Every DMA therefore goes to a named
  channel (own semaphore, +16 per DMA) under the discipline that a channel
  has at most one outstanding DMA (enforced structurally: the issuing
  engine waits for the previous consumer first), or is only ever waited
  at its full total with no later issues (init/store channels).
"""
import concourse.bass as bass
import concourse.mybir as mybir

F32 = mybir.dt.float32
BF16 = mybir.dt.bfloat16
I32 = mybir.dt.int32
AF = mybir.ActivationFunctionType
ALU = mybir.AluOpType
AX = mybir.AxisListType

ENGINES = ("sync", "gpsimd", "tensor", "scalar", "vector")
SERIAL = ("gpsimd", "scalar", "vector")


class Prog:
    def __init__(self, nc):
        self.nc = nc
        self.ops = {e: [] for e in ENGINES}
        self.count = {e: 0 for e in ENGINES}
        self.chan_count = {}
        self.sems = {}
        self.last_wait = {e: {} for e in ENGINES}
        self.prev_compute = {e: None for e in ENGINES}
        self._sem_ctxs = []

    def _wait(self, eng, key, val, tag):
        lw = self.last_wait[eng]
        if lw.get(key, 0) < val:
            self.ops[eng].append(("wait", key, val, tag))
            lw[key] = val

    def on(self, eng, fn, after=(), inc=None, tag=None, dma_chan=None):
        if dma_chan is not None:
            inc = 16
        elif inc is None:
            inc = 1
        deps = [t for t in after if t is not None]
        if dma_chan is None and eng in SERIAL and self.prev_compute[eng] is not None:
            deps.append(self.prev_compute[eng])
        if dma_chan is not None:
            prev = self.chan_count.get("c:" + dma_chan, 0)
            if prev:
                deps.append(("c:" + dma_chan, prev, True))
        for key, val, _async in deps:
            if key == eng and self.last_wait[eng].get(key, 0) >= val:
                continue
            self._wait(eng, key, val, tag)
        self.ops[eng].append(("emit", fn, inc, tag, dma_chan))
        if dma_chan is not None:
            key = "c:" + dma_chan
            self.chan_count[key] = self.chan_count.get(key, 0) + 16
            return (key, self.chan_count[key], True)
        if inc:
            self.count[eng] += inc
        tok = (eng, self.count[eng], False)
        if inc:
            self.prev_compute[eng] = tok
        return tok

    def barrier_all(self):
        toks = [(e, self.count[e], False) for e in ENGINES if self.count[e]]
        toks += [(k, v, True) for k, v in self.chan_count.items()]
        for e in ENGINES:
            if self.ops[e]:
                self.wait_all(e, toks, tag="barrier")

    def barrier_token(self, eng):
        return (eng, self.count[eng], False)

    def chan_total(self, chan):
        key = "c:" + chan
        return (key, self.chan_count.get(key, 0), True)

    def wait_all(self, eng, toks, tag=None):
        for tok in toks:
            if tok is None:
                continue
            key, val, _ = tok
            self._wait(eng, key, val, tag)

    def finalize(self, eng="sync"):
        for key, val in sorted(self.chan_count.items()):
            self._wait(eng, key, val, "final")

    def make_sems(self):
        for e in ENGINES:
            ctx = self.nc.semaphore("sem_" + e)
            h = ctx.__enter__()
            self._sem_ctxs.append(ctx)
            self.sems[e] = h
        for key in self.chan_count:
            ctx = self.nc.semaphore("sem_" + key[2:])
            h = ctx.__enter__()
            self._sem_ctxs.append(ctx)
            self.sems[key] = h

    def emit(self, blk):
        def mk(e):
            def body(engobj):
                for item in self.ops[e]:
                    if item[0] == "wait":
                        _, key, val, tag = item
                        engobj.wait_ge(self.sems[key], val)
                    else:
                        _, fn, inc, tag, chan = item
                        inst = fn(engobj)
                        if chan is not None:
                            inst.then_inc(self.sems["c:" + chan], 16)
                        elif inc:
                            assert inst is not None, f"inc={inc} tag={tag} returned None"
                            inst.then_inc(self.sems[e], inc)
            return body
        for e in ENGINES:
            if self.ops[e]:
                getattr(blk, e)(mk(e))

    def stats(self):
        return {e: len(self.ops[e]) for e in ENGINES}

"""Block program builders for the macaron-d2 megakernel (raw Bass).

Home layout: token-major [tokens on partitions, features on free].
Matmul inputs are feature-major bf16 (produced by PE transposes).

Transitive-safety rule used throughout: PE executes in order, so a writer
that waits on a PE token issued after all PE reads of its destination is
WAR-safe without an explicit edge. Same-engine ops are in order.
"""
import contextlib
import numpy as np
import concourse.bass as bass
import concourse.mybir as mybir

D = 1024


class St:
    pass


_UNIQ = [0]


def alloc(ctx, nc, name, shape, dtype):
    _UNIQ[0] += 1
    return ctx.enter_context(nc.sbuf_tensor(f"{name}_u{_UNIQ[0]}", shape, dtype))


def alloc_ps(ctx, nc, name, shape, dtype=F32):
    _UNIQ[0] += 1
    return ctx.enter_context(nc.psum_tensor(f"{name}_u{_UNIQ[0]}", shape, dtype))


class PsumRing:
    def __init__(self, ctx, nc, name, nbanks, width=512, dtype=F32):
        self.tiles = [alloc_ps(ctx, nc, f"{name}{i}", [128, width], dtype)
                      for i in range(nbanks)]
        self.war = [None] * nbanks
        self.i = 0

    def get(self):
        b = self.i % len(self.tiles)
        self.i += 1
        return self.tiles[b], self.war[b], b

    def consumed(self, b, token):
        self.war[b] = token


def emit_ln_tile(p, st, xt, ut, col, tag="ln", gb=None, bb=None):
    s1c = st.s1[:, col:col + 1]
    s2c = st.s2[:, col:col + 1]
    mc = st.mcol[:, col:col + 1]
    vc = st.vcol[:, col:col + 1]
    rc = st.rcol[:, col:col + 1]
    t1 = p.on("vector", lambda e: e.reduce_sum(s1c, xt, AX.X), after=st.ln_after,
              tag=tag + "_s1")
    t2 = p.on("scalar", lambda e: e.activation(st.dump2[:, :D], xt, AF.Square,
                                               accum_out=s2c), after=st.ln_after,
              tag=tag + "_s2")
    t3 = p.on("vector", lambda e: e.tensor_scalar(mc, s1c, 1.0 / D, None, ALU.mult),
              after=[t1], tag=tag + "_m")
    t4 = p.on("scalar", lambda e: e.activation(vc, mc, AF.Square), after=[t2, t3],
              tag=tag + "_m2")
    t5 = p.on("vector", lambda e: e.tensor_scalar(vc, s2c, 1.0 / D, vc, ALU.mult,
                                                  ALU.subtract), after=[t4],
              tag=tag + "_v")
    t6 = p.on("scalar", lambda e: e.activation(vc, vc, AF.Sqrt, bias=st.epscol[:, 0:1]),
              after=[t5], tag=tag + "_sd")
    t7 = p.on("vector", lambda e: e.reciprocal(rc, vc), after=[t6], tag=tag + "_r")
    if gb is None:
        t8 = p.on("vector", lambda e: e.tensor_scalar(ut, xt, mc, rc, ALU.subtract,
                                                      ALU.mult), after=[t7], tag=tag + "_ap")
        return t8
    tmp = st.lntmp[:, :xt.shape[1]]
    t8 = p.on("vector", lambda e: e.tensor_scalar(tmp, xt, mc, rc, ALU.subtract,
                                                  ALU.mult), after=[t7], tag=tag + "_ap")
    t9 = p.on("vector", lambda e: e.tensor_mul(tmp, tmp, gb), after=[t8], tag=tag + "_g")
    ta = p.on("vector", lambda e: e.tensor_add(ut, tmp, bb), after=[t9], tag=tag + "_b")
    return ta


def emit_transpose(p, st, src_tile, dst_fm, ntile, dchunks, after, ps_tr, tag="tr"):
    toks = []
    for k in range(dchunks):
        for i0 in range(0, ntile, 4):
            nb = min(4, ntile - i0)
            ps, war, b = ps_tr.get()
            last = None
            for j in range(nb):
                i = i0 + j
                aft = list(after) + ([war] if (war and j == 0) else [])
                last = p.on("tensor", lambda e, i=i, j=j, k=k, ps=ps: e.transpose(
                    ps[:, j * 128:(j + 1) * 128],
                    src_tile(i)[:, k * 128:(k + 1) * 128],
                    st.idbf[:, :]),
                    after=(aft if j == 0 else []), inc=(1 if j == nb - 1 else 0),
                    tag=f"{tag}_t{k}_{i}")
            tv = p.on("vector", lambda e, k=k, i0=i0, nb=nb, ps=ps: e.tensor_copy(
                dst_fm[k][:, i0 * 128:(i0 + nb) * 128], ps[:, :nb * 128]),
                after=[last], tag=f"{tag}_e{k}_{i0}")
            ps_tr.consumed(b, tv)
            toks.append(tv)
    return toks


L1FFN_W = ["rT", "gT", "uT", "dT", "idn"]


def emit_l1ffn(p, nc, ctx, Xd, XOd, wd, T=4096, TG=1024, debug=False):
    E, FF = 8, 2048
    KC = D // 128
    FC = FF // 128
    NT = TG // 128
    NG = T // TG
    dbg = {}
    rTd, gTd, uTd, dTd, idn = (wd[k] for k in L1FFN_W)

    st = St()
    st.idbf = alloc(ctx, nc, "idbf", [128, 128], BF16)
    st.dump = alloc(ctx, nc, "dump", [128, D], F32)
    st.dump2 = alloc(ctx, nc, "dump2", [128, D], F32)
    for nm in ("s1", "s2", "mcol", "vcol", "rcol"):
        setattr(st, nm, alloc(ctx, nc, nm, [128, NT], F32))

    xt = [alloc(ctx, nc, f"xt{i}", [128, D], F32) for i in range(2)]
    u_tm = alloc(ctx, nc, "u_tm", [128, NT * D], BF16)
    u_fm = [alloc(ctx, nc, f"ufm{k}", [128, TG], BF16) for k in range(KC)]
    h_fm = [alloc(ctx, nc, f"hfm{f}", [128, TG], BF16) for f in range(FC)]
    acc = alloc(ctx, nc, "acc", [128, NT * D], F32)
    wslot = [alloc(ctx, nc, f"w{i}", [128, 16384], BF16) for i in range(2)]
    rbf = alloc(ctx, nc, "rbf", [128, KC * E], BF16)
    zer8 = alloc(ctx, nc, "zer8", [128, E], F32)
    prob = alloc(ctx, nc, "prob", [128, E], F32)
    eq = alloc(ctx, nc, "eq", [128, E], F32)
    cum = alloc(ctx, nc, "cum", [128, E], F32)
    oh1 = alloc(ctx, nc, "oh1", [128, E], F32)
    oh2 = alloc(ctx, nc, "oh2", [128, E], F32)
    wecols = alloc(ctx, nc, "wecols", [128, NT * E], F32)
    c1 = alloc(ctx, nc, "c1", [128, 8], F32)

    ps_tr = PsumRing(ctx, nc, "pstr", 2, 512, BF16)
    ps_r = PsumRing(ctx, nc, "psr", 1, E, F32)
    ps_gu = PsumRing(ctx, nc, "psgu", 2, 512, F32)
    ps_d = PsumRing(ctx, nc, "psd", 2, 512, F32)

    st.epscol = alloc(ctx, nc, "epscol", [128, 1], F32)
    tid0 = p.on("sync", lambda e: e.dma_start(out=st.idbf[:, :], in_=idn),
                dma_chan="init")
    tz = p.on("vector", lambda e: e.memset(zer8[:, :], 0.0))
    p.on("vector", lambda e: e.memset(st.epscol[:, :], 1e-5))
    p.on("gpsimd", lambda e: e.dma_start(
        out=rbf[:, :].rearrange("p (k e) -> p k e", k=KC),
        in_=rTd.rearrange("(k p) e -> p k e", p=128)), dma_chan="initg")
    tid = p.chan_total("init")
    trw = p.chan_total("initg")

    wslot_war = {0: [], 1: []}
    wtok = {}

    def wload(slot, e, kind, tag=""):
        src = {"g": gTd, "u": uTd, "d": dTd}[kind][e]
        rows, cols = src.shape
        ch = rows // 128
        return p.on("gpsimd", lambda _e: _e.dma_start(
            out=wslot[slot][:, :ch * cols].rearrange("p (c n) -> p c n", c=ch),
            in_=src.rearrange("(c p) n -> p c n", p=128)), dma_chan=f"w{slot}",
            after=wslot_war[slot], tag=f"wl{tag}_{e}{kind}")

    xt_war = [None, None]
    prev_router_tail = None

    for g in range(NG):
        ut_toks = []
        for i in range(NT):
            sl = i % 2
            tx = p.on("sync", lambda e, i=i, sl=sl, g=g: e.dma_start(
                out=xt[sl][:, :], in_=Xd[(g * NT + i) * 128:(g * NT + i + 1) * 128, :]),
                after=([xt_war[sl]] if xt_war[sl] else []), dma_chan=f"xt{sl}",
                tag=f"ldx{g}_{i}")
            st.ln_after = [tx]
            tu = emit_ln_tile(p, st, xt[sl][:, :], u_tm[:, i * D:(i + 1) * D], i,
                              tag=f"ln{g}_{i}")
            xt_war[sl] = tu
            ut_toks.append(tu)

        ufm_done = emit_transpose(p, st, lambda i: u_tm[:, i * D:(i + 1) * D],
                                  u_fm, NT, KC, ut_toks + [tid], ps_tr, tag=f"tr{g}")

        if debug and g == NG - 1:
            p.on("sync", lambda e: e.dma_start(out=dbg["dbg_u"], in_=u_tm[:, :]),
                 after=ut_toks, inc=16)
            p.on("sync", lambda e: e.dma_start(out=dbg["dbg_s1"], in_=st.s1[:, :]),
                 after=ut_toks, inc=16)
            p.on("sync", lambda e: e.dma_start(out=dbg["dbg_s2"], in_=st.s2[:, :]),
                 after=ut_toks, inc=16)
            p.on("sync", lambda e: e.dma_start(out=dbg["dbg_m"], in_=st.mcol[:, :]),
                 after=ut_toks, inc=16)
            p.on("sync", lambda e: e.dma_start(out=dbg["dbg_r"], in_=st.rcol[:, :]),
                 after=ut_toks, inc=16)
            p.on("sync", lambda e: e.dma_start(out=dbg["dbg_ufm0"], in_=u_fm[0][:, :]),
                 after=ufm_done, inc=16)

        we_done = []
        for i in range(NT):
            ps, war, b = ps_r.get()
            last = None
            for k in range(KC):
                aft = ufm_done + [trw] + ([war] if war and k == 0 else [])
                last = p.on("tensor", lambda e, k=k, i=i, ps=ps: e.matmul(
                    ps[:, :], u_fm[k][:, i * 128:(i + 1) * 128],
                    rbf[:, k * E:(k + 1) * E],
                    start=(k == 0), stop=(k == KC - 1)),
                    after=(aft if k == 0 else []), inc=(1 if k == KC - 1 else 0),
                    tag=f"rt{g}_{i}")
            tm_ = p.on("vector", lambda e, ps=ps: e.reduce_max(
                c1[:, 0:1], ps[:, :], AX.X, negate=True), after=[last])
            te = p.on("scalar", lambda e, ps=ps: e.activation(
                prob[:, :], ps[:, :], AF.Exp, bias=c1[:, 0:1]),
                after=[tm_] + ([prev_router_tail] if prev_router_tail else []))
            ps_r.consumed(b, te)
            t1_ = p.on("vector", lambda e: e.reduce_max(c1[:, 1:2], prob[:, :], AX.X),
                       after=[te])
            p.on("vector", lambda e: e.tensor_scalar(eq[:, :], prob[:, :], c1[:, 1:2],
                                                     None, ALU.is_ge), after=[t1_])
            p.on("vector", lambda e: e.tensor_tensor_scan(cum[:, :], eq[:, :],
                                                          zer8[:, :], 0.0, ALU.add,
                                                          ALU.add), after=[tz])
            p.on("vector", lambda e: e.tensor_scalar(cum[:, :], cum[:, :], 1.0, None,
                                                     ALU.is_equal))
            p.on("vector", lambda e: e.tensor_mul(oh1[:, :], eq[:, :], cum[:, :]))
            p.on("vector", lambda e: e.tensor_scalar(cum[:, :], oh1[:, :], -1.0, 1.0,
                                                     ALU.mult, ALU.add))
            p.on("vector", lambda e: e.tensor_mul(eq[:, :], prob[:, :], cum[:, :]))
            p.on("vector", lambda e: e.reduce_max(c1[:, 2:3], eq[:, :], AX.X))
            p.on("vector", lambda e: e.tensor_scalar(cum[:, :], eq[:, :], c1[:, 2:3],
                                                     None, ALU.is_ge))
            p.on("vector", lambda e: e.tensor_tensor_scan(oh2[:, :], cum[:, :],
                                                          zer8[:, :], 0.0, ALU.add,
                                                          ALU.add))
            p.on("vector", lambda e: e.tensor_scalar(oh2[:, :], oh2[:, :], 1.0, None,
                                                     ALU.is_equal))
            p.on("vector", lambda e: e.tensor_mul(oh2[:, :], cum[:, :], oh2[:, :]))
            p.on("vector", lambda e: e.tensor_add(c1[:, 3:4], c1[:, 1:2], c1[:, 2:3]))
            p.on("vector", lambda e: e.reciprocal(c1[:, 4:5], c1[:, 3:4]))
            p.on("vector", lambda e, i=i: e.tensor_scalar(
                wecols[:, i * E:(i + 1) * E], oh1[:, :], c1[:, 1:2], None, ALU.mult))
            p.on("vector", lambda e, i=i: e.scalar_tensor_tensor(
                cum[:, :], oh2[:, :], c1[:, 2:3], wecols[:, i * E:(i + 1) * E],
                ALU.mult, ALU.add))
            tq = p.on("vector", lambda e, i=i: e.tensor_scalar(
                wecols[:, i * E:(i + 1) * E], cum[:, :], c1[:, 4:5], None, ALU.mult),
                tag=f"we{g}_{i}")
            prev_router_tail = tq
            we_done.append(tq)

        if debug and g == NG - 1:
            p.on("sync", lambda e: e.dma_start(out=dbg["dbg_we"], in_=wecols[:, :]),
                 after=we_done, inc=16)

        p.on("vector", lambda e: e.memset(acc[:, :], 0.0), tag=f"acc0{g}")

        phases = [(e, k) for e in range(E) for k in ("g", "u", "d")]
        h_g = []
        h_done = []
        for pi, (e, kind) in enumerate(phases):
            key = (g, e, kind)
            if key not in wtok:
                wtok[key] = wload(pi % 2, e, kind, tag=str(g))
            nxt = pi + 1
            if nxt < len(phases):
                en, kn = phases[nxt]
                if (g, en, kn) not in wtok:
                    wtok[(g, en, kn)] = wload(nxt % 2, en, kn, tag=str(g))
            elif g + 1 < NG:
                wtok[(g + 1, 0, "g")] = wload(nxt % 2, 0, "g", tag=str(g + 1))
            slot = pi % 2
            w = wslot[slot]
            wt = wtok[key]
            cons = []
            if debug and g == NG - 1 and e == 0 and kind == "g":
                p.on("sync", lambda _e, w=w: _e.dma_start(out=dbg["dbg_w"],
                     in_=w[:, :]), after=[wt], dma_chan="dbg")
            if kind in ("g", "u"):
                for f in range(FC):
                    for t4 in range(TG // 512):
                        ps, war, b = ps_gu.get()
                        last = None
                        for k in range(KC):
                            aft = [wt] + ufm_done + ([war] if war and k == 0 else [])
                            last = p.on("tensor", lambda _e, k=k, f=f, t4=t4, ps=ps, w=w: _e.matmul(
                                ps[:, :], w[:, k * 2048 + f * 128:k * 2048 + (f + 1) * 128],
                                u_fm[k][:, t4 * 512:(t4 + 1) * 512],
                                start=(k == 0), stop=(k == KC - 1)),
                                after=(aft if k == 0 else []),
                                inc=(1 if k == KC - 1 else 0),
                                tag=f"{kind}{g}_{e}_{f}_{t4}")
                        if kind == "g":
                            tv = p.on("scalar", lambda _e, f=f, t4=t4, ps=ps: _e.activation(
                                h_fm[f][:, t4 * 512:(t4 + 1) * 512], ps[:, :], AF.Silu),
                                after=[last], tag=f"sl{g}_{e}_{f}_{t4}")
                        else:
                            tv = p.on("vector", lambda _e, f=f, t4=t4, ps=ps: _e.tensor_tensor(
                                h_fm[f][:, t4 * 512:(t4 + 1) * 512],
                                h_fm[f][:, t4 * 512:(t4 + 1) * 512], ps[:, :], ALU.mult),
                                after=[last] + h_g, tag=f"h{g}_{e}_{f}_{t4}")
                        ps_gu.consumed(b, tv)
                        cons.append(tv)
                if kind == "g":
                    h_g = list(cons)
                    if debug and g == NG - 1 and e == 0:
                        p.on("sync", lambda _e: _e.dma_start(out=dbg["dbg_g0"],
                             in_=h_fm[0][:, :]), after=h_g, inc=16)
                else:
                    h_done = list(cons)
                    if debug and g == NG - 1 and e == 0:
                        p.on("sync", lambda _e: _e.dma_start(out=dbg["dbg_h0"],
                             in_=h_fm[0][:, :]), after=h_done, inc=16)
            else:
                for i in range(NT):
                    for dg in range(D // 512):
                        ps, war, b = ps_d.get()
                        last = None
                        for fk in range(FC):
                            aft = [wt] + h_done + ([war] if war and fk == 0 else [])
                            last = p.on("tensor", lambda _e, fk=fk, i=i, dg=dg, ps=ps, w=w: _e.matmul(
                                ps[:, :], h_fm[fk][:, i * 128:(i + 1) * 128],
                                w[:, fk * 1024 + dg * 512:fk * 1024 + (dg + 1) * 512],
                                start=(fk == 0), stop=(fk == FC - 1)),
                                after=(aft if fk == 0 else []),
                                inc=(1 if fk == FC - 1 else 0), tag=f"d{g}_{e}_{i}_{dg}")
                        tv = p.on("vector", lambda _e, i=i, dg=dg, e_=e, ps=ps: _e.scalar_tensor_tensor(
                            acc[:, i * D + dg * 512:i * D + (dg + 1) * 512], ps[:, :],
                            wecols[:, i * E + e_:i * E + e_ + 1],
                            acc[:, i * D + dg * 512:i * D + (dg + 1) * 512],
                            ALU.mult, ALU.add), after=[last], tag=f"ac{g}_{e}_{i}_{dg}")
                        ps_d.consumed(b, tv)
                        cons.append(tv)
            wslot_war[slot] = cons

        acc_done = [cons[-1]]
        if debug and g == NG - 1:
            p.on("sync", lambda e: e.dma_start(out=dbg["dbg_ufm0b"], in_=u_fm[0][:, :]),
                 after=acc_done, inc=16)
            p.on("sync", lambda e: e.dma_start(out=dbg["dbg_acc"], in_=acc[:, :]),
                 after=acc_done, inc=16)
        for i in range(NT):
            sl = i % 2
            tx = p.on("sync", lambda e, i=i, sl=sl, g=g: e.dma_start(
                out=xt[sl][:, :], in_=Xd[(g * NT + i) * 128:(g * NT + i + 1) * 128, :]),
                after=[xt_war[sl]] if xt_war[sl] else [], dma_chan=f"xt{sl}",
                tag=f"ldx2{g}_{i}")
            ty = p.on("vector", lambda e, i=i, sl=sl: e.scalar_tensor_tensor(
                xt[sl][:, :], acc[:, i * D:(i + 1) * D], 0.5, xt[sl][:, :],
                ALU.mult, ALU.add), after=[tx] + acc_done, tag=f"xo{g}_{i}")
            ts_ = p.on("sync", lambda e, i=i, sl=sl, g=g: e.dma_start(
                out=XOd[(g * NT + i) * 128:(g * NT + i + 1) * 128, :], in_=xt[sl][:, :]),
                after=[ty], dma_chan=f"st{sl}", tag=f"stx{g}_{i}")
            xt_war[sl] = ts_

    return p


L1ATTN_W = ["Wq", "Wk", "Wv", "Wo", "W1", "W2", "bqc", "bkr", "bvr", "bor",
            "b1c", "b2t", "g1t", "be1t", "g2t", "be2t", "idn"]


def emit_l1attn(p, nc, ctx, Xd, XOd, wd, T=4096, S=2048, stop_after=None):
    H, DH, FF = 16, 64, 4096
    KC = D // 128
    NT = S // 128
    NSEQ = T // S
    NSL = 8
    FSL = FF // NSL
    NHALF = NT // 2
    (Wqd, Wkd, Wvd, Wod, W1d, W2d, bqc, bkr, bvr, bor, b1c, b2t, g1t, be1t,
     g2t, be2t, idn) = (wd[k] for k in L1ATTN_W)

    st = St()
    st.idbf = alloc(ctx, nc, "idbf", [128, 128], BF16)
    st.dump = alloc(ctx, nc, "dump", [128, D], F32)
    st.dump2 = alloc(ctx, nc, "dump2", [128, D], F32)
    st.lntmp = st.dump2
    st.epscol = alloc(ctx, nc, "epscol", [128, 1], F32)
    for nm in ("s1", "s2", "mcol", "vcol", "rcol"):
        setattr(st, nm, alloc(ctx, nc, nm, [128, NT], F32))

    S0 = alloc(ctx, nc, "S0", [128, NT * D], BF16)
    S1 = alloc(ctx, nc, "S1", [128, NT * D], BF16)
    S2 = alloc(ctx, nc, "S2", [128, NT * D], BF16)
    S3 = alloc(ctx, nc, "S3", [128, NT * D], BF16)
    y_acc = S3[:, :].bitcast(F32)
    wsl = [alloc(ctx, nc, f"wsl{i}", [128, KC * D], BF16) for i in range(2)]
    xt = [alloc(ctx, nc, f"xt{i}", [128, D], F32) for i in range(2)]
    scr = [alloc(ctx, nc, "scr0", [128, 512], F32), st.dump2[:, 0:512]]
    kt = alloc(ctx, nc, "kt", [128, D], BF16)
    outt = kt
    vt = alloc(ctx, nc, "vt", [128, D], BF16)
    kv_acc = alloc(ctx, nc, "kv_acc", [128, 512], F32)
    ks_acc = alloc(ctx, nc, "ks_acc", [128, 8], F32)
    ztile = alloc(ctx, nc, "ztile", [128, 16], F32)
    kvsb = alloc(ctx, nc, "kvsb", [128, 1024], BF16)
    ksb = alloc(ctx, nc, "ksb", [128, 16], BF16)
    bq1 = alloc(ctx, nc, "bq1", [128, KC], F32)
    b1cs = alloc(ctx, nc, "b1cs", [128, FF // 128], F32)
    rows = alloc(ctx, nc, "rows", [1, 3 * D], BF16)
    onec = alloc(ctx, nc, "onec", [128, 1], BF16)
    oner = alloc(ctx, nc, "oner", [1, 128], BF16)
    neg1 = alloc(ctx, nc, "neg1", [128, 1], F32)
    b2b = alloc(ctx, nc, "b2b", [128, D], BF16)
    g1b = alloc(ctx, nc, "g1b", [128, D], F32)
    be1b = alloc(ctx, nc, "be1b", [128, D], BF16)
    g2b = alloc(ctx, nc, "g2b", [128, D], F32)
    be2b = alloc(ctx, nc, "be2b", [128, D], BF16)

    ps_tr = PsumRing(ctx, nc, "pstr", 2, 512, BF16)
    ps_a = PsumRing(ctx, nc, "psa", 2, 512, F32)
    ps_b = PsumRing(ctx, nc, "psb", 2, 512, F32)
    kv_ps = alloc_ps(ctx, nc, "kvps", [128, 512], F32)
    ks_ps = alloc_ps(ctx, nc, "ksps", [128, 8], F32)

    p.on("sync", lambda e: e.dma_start(out=st.idbf[:, :], in_=idn), dma_chan="init")
    for dst, srcd in [(g1b, g1t), (g2b, g2t)]:
        p.on("sync", lambda e, dst=dst, srcd=srcd: e.dma_start(out=dst[:, :], in_=srcd),
             dma_chan="init")
    for dst, srcd in [(b2b, b2t), (be1b, be1t), (be2b, be2t)]:
        p.on("gpsimd", lambda e, dst=dst, srcd=srcd: e.dma_start(out=dst[:, :], in_=srcd),
             dma_chan="initg")
    p.on("sync", lambda e: e.dma_start(out=bq1[:, :], in_=bqc), dma_chan="init")
    p.on("sync", lambda e: e.dma_start(out=b1cs[:, :], in_=b1c), dma_chan="init")
    p.on("sync", lambda e: e.dma_start(out=st.dump[0:1, :], in_=bkr), dma_chan="init")
    p.on("gpsimd", lambda e: e.dma_start(out=rows[:, D:2 * D], in_=bvr),
         dma_chan="initg")
    p.on("gpsimd", lambda e: e.dma_start(out=rows[:, 2 * D:3 * D], in_=bor),
         dma_chan="initg")
    init_done = p.chan_total("init")
    initg_done = p.chan_total("initg")
    tid = tb0 = tb1 = init_done
    p.on("vector", lambda e: e.memset(st.epscol[:, :], 1e-5))
    p.on("vector", lambda e: e.memset(onec[:, :], 1.0))
    p.on("vector", lambda e: e.memset(oner[:, :], 1.0))
    tn1 = p.on("vector", lambda e: e.memset(neg1[:, :], -1.0))
    tbq = p.on("vector", lambda e: e.tensor_scalar_add(bq1[:, :], bq1[:, :], 1.0),
               after=[init_done])
    tk0 = p.on("vector", lambda e: e.tensor_scalar_add(rows[:, 0:D], st.dump[0:1, :], 1.0),
               after=[init_done])
    rows_done = [tk0, init_done, initg_done]

    wwar = {0: [], 1: []}

    def wload(slot, srcd, r0, rch, c0, cn, tg):
        return p.on("gpsimd", lambda e, slot=slot, srcd=srcd, r0=r0, rch=rch,
                    c0=c0, cn=cn: e.dma_start(
            out=wsl[slot][:, :rch * cn].rearrange("p (c n) -> p c n", c=rch),
            in_=srcd[r0 * 128:(r0 + rch) * 128, c0:c0 + cn].rearrange(
                "(c p) n -> p c n", p=128)), dma_chan=f"w{slot}",
            after=wwar[slot], tag=tg)

    xt_war = [None, None]
    kvsb_war = None

    for s in range(NSEQ):
        ut_toks = []
        for i in range(NT):
            sl = i % 2
            tx = p.on("sync", lambda e, s=s, i=i, sl=sl: e.dma_start(
                out=xt[sl][:, :], in_=Xd[(s * NT + i) * 128:(s * NT + i + 1) * 128, :]),
                after=([xt_war[sl]] if xt_war[sl] else []), dma_chan=f"xt{sl}",
                tag=f"lx{s}_{i}")
            st.ln_after = [tx]
            tu = emit_ln_tile(p, st, xt[sl][:, :], S0[:, i * D:(i + 1) * D], i,
                              tag=f"ln{s}_{i}")
            xt_war[sl] = tu
            ut_toks.append(tu)

        ufm = [S1[:, k * NT * 128:(k + 1) * NT * 128] for k in range(KC)]
        ufm_done = emit_transpose(p, st, lambda i: S0[:, i * D:(i + 1) * D],
                                  ufm, NT, KC, ut_toks + [tid], ps_tr, tag=f"utr{s}")

        if stop_after == "tr":
            break
        wq = wload(0, Wqd, 0, KC, 0, D, f"wq{s}")
        wk = wload(1, Wkd, 0, KC, 0, D, f"wk{s}")
        qfm = [S2[:, k * NT * 128:(k + 1) * NT * 128] for k in range(KC)]
        cons_q = []
        prev_t4 = None
        for f in range(KC):
            for tg in range(NT * 128 // 512):
                ps, war, b = ps_a.get()
                last = None
                for k in range(KC):
                    aft = [wq] + ufm_done + ([war] if war and k == 0 else [])
                    last = p.on("tensor", lambda e, k=k, f=f, tg=tg, ps=ps: e.matmul(
                        ps[:, :], wsl[0][:, k * D + f * 128:k * D + (f + 1) * 128],
                        ufm[k][:, tg * 512:(tg + 1) * 512],
                        start=(k == 0), stop=(k == KC - 1)),
                        after=(aft if k == 0 else []), inc=(1 if k == KC - 1 else 0),
                        tag=f"q{s}_{f}_{tg}")
                t1 = p.on("vector", lambda e, f=f, ps=ps: e.tensor_scalar_add(
                    scr[0][:, :], ps[:, :], bq1[:, f:f + 1]),
                    after=[last, tbq] + ([prev_t4] if prev_t4 else []),
                    tag=f"qa{s}_{f}_{tg}")
                t2 = p.on("vector", lambda e: e.tensor_scalar_min(
                    scr[1][:, :], scr[0][:, :], 1.0), after=[t1])
                t3 = p.on("scalar", lambda e: e.activation(
                    scr[1][:, :], scr[1][:, :], AF.Exp, bias=neg1[:, 0:1]),
                    after=[t2, tn1])
                t4 = p.on("vector", lambda e, f=f, tg=tg: e.tensor_max(
                    qfm[f][:, tg * 512:(tg + 1) * 512], scr[0][:, :], scr[1][:, :]),
                    after=[t3], tag=f"qm{s}_{f}_{tg}")
                ps_a.consumed(b, t4)
                prev_t4 = t4
                cons_q.append(t4)
        wwar[0] = cons_q

        if stop_after == "q":
            break
        wv = wload(0, Wvd, 0, KC, 0, D, f"wv{s}")
        kv_zero = p.on("vector", lambda e: e.memset(kv_acc[:, :], 0.0), tag=f"kvz{s}")
        ks_zero = p.on("vector", lambda e: e.memset(ks_acc[:, :], 0.0), tag=f"ksz{s}")
        last_kv = None
        last_ks = None
        kv_evac = None
        ks_evac = None
        kt_war = None
        vt_war = None
        for i in range(NT):
            k_toks = []
            for fg in range(2):
                ps, war, b = ps_b.get()
                last = None
                for k in range(KC):
                    aft = [wk] + ufm_done + rows_done + ([war] if war and k == 0 else [])
                    last = p.on("tensor", lambda e, k=k, i=i, fg=fg, ps=ps: e.matmul(
                        ps[:, :], ufm[k][:, i * 128:(i + 1) * 128],
                        wsl[1][:, k * D + fg * 512:k * D + (fg + 1) * 512],
                        start=(k == 0), stop=False),
                        after=(aft if k == 0 else []), inc=0, tag=f"k{s}_{i}_{fg}")
                last = p.on("tensor", lambda e, fg=fg, ps=ps: e.matmul(
                    ps[:, :], oner[0:1, :], rows[:, fg * 512:(fg + 1) * 512],
                    start=False, stop=True), inc=1, tag=f"kb{s}_{i}_{fg}")
                t2 = p.on("vector", lambda e, ps=ps: e.tensor_scalar_min(
                    scr[0][:, :], ps[:, :], 1.0),
                    after=[last] + ([kt_war] if kt_war and fg == 0 else []))
                t3 = p.on("scalar", lambda e: e.activation(
                    scr[0][:, :], scr[0][:, :], AF.Exp, bias=neg1[:, 0:1]),
                    after=[t2, tn1])
                t4 = p.on("vector", lambda e, fg=fg, ps=ps: e.tensor_tensor(
                    kt[:, fg * 512:(fg + 1) * 512], ps[:, :], scr[0][:, :], ALU.max),
                    after=[t3], tag=f"km{s}_{i}_{fg}")
                ps_b.consumed(b, t4)
                k_toks.append(t4)
            v_toks = []
            for fg in range(2):
                ps, war, b = ps_b.get()
                last = None
                for k in range(KC):
                    aft = [wv] + ufm_done + rows_done + ([war] if war and k == 0 else [])
                    last = p.on("tensor", lambda e, k=k, i=i, fg=fg, ps=ps: e.matmul(
                        ps[:, :], ufm[k][:, i * 128:(i + 1) * 128],
                        wsl[0][:, k * D + fg * 512:k * D + (fg + 1) * 512],
                        start=(k == 0), stop=False),
                        after=(aft if k == 0 else []), inc=0, tag=f"v{s}_{i}_{fg}")
                last = p.on("tensor", lambda e, fg=fg, ps=ps: e.matmul(
                    ps[:, :], oner[0:1, :], rows[:, D + fg * 512:D + (fg + 1) * 512],
                    start=False, stop=True), inc=1, tag=f"vb{s}_{i}_{fg}")
                t4 = p.on("scalar", lambda e, fg=fg, ps=ps: e.activation(
                    vt[:, fg * 512:(fg + 1) * 512], ps[:, :], AF.Copy),
                    after=[last] + ([vt_war] if vt_war and fg == 0 else []),
                    tag=f"vc{s}_{i}_{fg}")
                ps_b.consumed(b, t4)
                v_toks.append(t4)
            for h in range(H):
                hc, hp = h // 2, (h % 2) * 64
                aft = (k_toks + v_toks + ([kv_evac] if i else [kv_zero])) if h == 0 else []
                last_kv = p.on("tensor", lambda e, h=h, hc=hc, hp=hp: e.matmul(
                    kv_ps[hp:hp + 64, hc * 64:(hc + 1) * 64],
                    kt[:, h * 64:(h + 1) * 64], vt[:, h * 64:(h + 1) * 64],
                    start=True, stop=True),
                    after=aft, inc=(1 if h == H - 1 else 0), tag=f"kv{s}_{i}_{h}")
            for hc in range(KC):
                aft = (k_toks + ([ks_evac] if i else [ks_zero])) if hc == 0 else []
                last_ks = p.on("tensor", lambda e, hc=hc: e.matmul(
                    ks_ps[:, hc:hc + 1], kt[:, hc * 128:(hc + 1) * 128], onec[:, 0:1],
                    start=True, stop=True),
                    after=aft, inc=(1 if hc == KC - 1 else 0), tag=f"ks{s}_{i}_{hc}")
            kv_evac = p.on("vector", lambda e: e.tensor_add(
                kv_acc[:, :], kv_acc[:, :], kv_ps[:, :]), after=[last_kv],
                tag=f"kva{s}_{i}")
            ks_evac = p.on("vector", lambda e: e.tensor_add(
                ks_acc[:, :], ks_acc[:, :], ks_ps[:, :]), after=[last_ks],
                tag=f"ksa{s}_{i}")
            kt_war = last_ks
            vt_war = last_kv
        p.on("vector", lambda e: e.memset(kvsb[:, :], 0.0),
             after=[kvsb_war] if kvsb_war else [], tag=f"kvz2{s}")
        p.on("vector", lambda e: e.memset(ksb[:, :], 0.0), tag=f"ksz2{s}")
        kv_cp = []
        for hc in range(KC):
            kv_cp.append(p.on("vector", lambda e, hc=hc: e.tensor_copy(
                kvsb[0:64, hc * 128:hc * 128 + 64],
                kv_acc[0:64, hc * 64:(hc + 1) * 64]), after=[kv_evac], tag=f"kbd{s}_{hc}"))
            kv_cp.append(p.on("vector", lambda e, hc=hc: e.tensor_copy(
                kvsb[64:128, hc * 128 + 64:(hc + 1) * 128],
                kv_acc[64:128, hc * 64:(hc + 1) * 64]), tag=f"kbd2{s}_{hc}"))
            kv_cp.append(p.on("vector", lambda e, hc=hc: e.tensor_copy(
                ksb[0:64, 2 * hc:2 * hc + 1], ks_acc[0:64, hc:hc + 1]),
                after=[ks_evac], tag=f"sbd{s}_{hc}"))
            kv_cp.append(p.on("vector", lambda e, hc=hc: e.tensor_copy(
                ksb[64:128, 2 * hc + 1:2 * hc + 2], ks_acc[64:128, hc:hc + 1]),
                tag=f"sbd2{s}_{hc}"))
        tkv = tks = kv_cp[-1]

        if stop_after == "kv":
            break
        zinv_ps = ps_a.tiles[0]
        z_war = ps_a.war[0]
        pso = [ps_b.tiles[0], ps_b.tiles[1]]
        pso_war = [ps_b.war[0], ps_b.war[1]]
        osc_toks = []
        for i in range(NT):
            lastz = None
            for hc in range(KC):
                aft = (cons_q + [tks] + ([z_war] if z_war else [])) if hc == 0 else []
                lastz = p.on("tensor", lambda e, i=i, hc=hc: e.matmul(
                    zinv_ps[:, 2 * hc:2 * hc + 2],
                    qfm[hc][:, i * 128:(i + 1) * 128],
                    ksb[:, 2 * hc:2 * hc + 2],
                    start=True, stop=True),
                    after=aft, inc=(1 if hc == KC - 1 else 0), tag=f"zi{s}_{i}_{hc}")
            tz1 = p.on("vector", lambda e: e.tensor_scalar_add(
                ztile[:, :], zinv_ps[:, :16], 1e-6), after=[lastz], tag=f"z1{s}_{i}")
            tz2 = p.on("vector", lambda e: e.reciprocal(ztile[:, :], ztile[:, :]),
                       after=[tz1], tag=f"z2{s}_{i}")
            z_war = tz2
            lasto = [None, None]
            for hc in range(KC):
                bk_ = hc // 4
                aft = []
                if hc == 0:
                    aft = [tkv] + cons_q + [t for t in pso_war if t]
                lasto[bk_] = p.on("tensor", lambda e, i=i, hc=hc, bk_=bk_: e.matmul(
                    pso[bk_][:, (hc % 4) * 128:(hc % 4 + 1) * 128],
                    qfm[hc][:, i * 128:(i + 1) * 128],
                    kvsb[:, hc * 128:(hc + 1) * 128],
                    start=True, stop=True),
                    after=aft, inc=(1 if hc % 4 == 3 else 0), tag=f"o{s}_{i}_{hc}")
            tsc = None
            for bk_ in range(2):
                tsc = p.on("vector", lambda e, i=i, bk_=bk_: e.tensor_mul(
                    S1[:, i * D + bk_ * 512:i * D + (bk_ + 1) * 512].rearrange(
                        "p (h m) -> p h m", m=64),
                    pso[bk_][:, :].rearrange("p (h m) -> p h m", m=64),
                    ztile[:, bk_ * 8:(bk_ + 1) * 8].unsqueeze(2).broadcast_to(
                        [128, 8, 64])), after=[lasto[bk_], tz2], tag=f"os{s}_{i}_{bk_}")
                pso_war[bk_] = tsc
            osc_toks.append(tsc)
        ps_a.war[0] = z_war
        ps_b.war[0] = pso_war[0]
        ps_b.war[1] = pso_war[1]
        kvsb_war = osc_toks[-1]

        if stop_after == "attn":
            break
        wwar[0] = [tkv]
        wwar[1] = [tkv, tks]
        wo = wload(1, Wod, 0, KC, 0, D, f"wo{s}")
        ofm = [S2[:, k * NT * 128:(k + 1) * NT * 128] for k in range(KC)]
        ofm_done = emit_transpose(p, st, lambda i: S1[:, i * D:(i + 1) * D],
                                  ofm, NT, KC, osc_toks + [tid], ps_tr, tag=f"otr{s}")
        if stop_after == "ofm":
            break
        x1_toks = []
        for i in range(NT):
            evs = []
            for fg in range(D // 512):
                ps, war, b = ps_a.get()
                last = None
                for k in range(KC):
                    aft = ofm_done + [wo] + ([war] if war and k == 0 else [])
                    last = p.on("tensor", lambda e, k=k, i=i, fg=fg, ps=ps: e.matmul(
                        ps[:, :], ofm[k][:, i * 128:(i + 1) * 128],
                        wsl[1][:, k * D + fg * 512:k * D + (fg + 1) * 512],
                        start=(k == 0), stop=False),
                        after=(aft if k == 0 else []), inc=0, tag=f"wo{s}_{i}_{fg}")
                last = p.on("tensor", lambda e, i=i, fg=fg, ps=ps: e.matmul(
                    ps[:, :], oner[0:1, :], rows[:, 2 * D + fg * 512:2 * D + (fg + 1) * 512],
                    start=False, stop=True), inc=1, tag=f"wob{s}_{i}_{fg}")
                ev = p.on("vector", lambda e, i=i, fg=fg, ps=ps: e.tensor_add(
                    st.dump[:, fg * 512:(fg + 1) * 512], ps[:, :],
                    S0[:, i * D + fg * 512:i * D + (fg + 1) * 512]),
                    after=[last], tag=f"x1r{s}_{i}_{fg}")
                ps_a.consumed(b, ev)
                evs.append(ev)
            st.ln_after = evs
            tu = emit_ln_tile(p, st, st.dump[:, :D], S0[:, i * D:(i + 1) * D], i,
                              tag=f"l1n{s}_{i}", gb=g1b[:, :], bb=be1b[:, :])
            x1_toks.append(tu)
        wwar[1] = x1_toks

        if stop_after == "x1":
            break
        x1fm = [S2[:, k * NT * 128:(k + 1) * NT * 128] for k in range(KC)]
        x1fm_done = emit_transpose(p, st, lambda i: S0[:, i * D:(i + 1) * D],
                                   x1fm, NT, KC, x1_toks + [tid], ps_tr, tag=f"x1tr{s}")

        if stop_after == "x1fm":
            break
        h1ch = FSL // 128
        HT = NHALF * 128
        for hf in range(2):
            i0 = hf * NHALF
            tza = p.on("vector", lambda e: e.memset(y_acc[:, :], 0.0), tag=f"ya{s}_{hf}")
            yacc_done = [tza]
            def wpair(slot, sl_, tg):
                t1_ = p.on("gpsimd", lambda e, slot=slot, sl_=sl_: e.dma_start(
                    out=wsl[slot][:, :KC * FSL].rearrange("p (c n) -> p c n", c=KC),
                    in_=W1d[:, sl_ * FSL:(sl_ + 1) * FSL].rearrange(
                        "(c p) n -> p c n", p=128)), dma_chan=f"w{slot}",
                    after=wwar[slot], tag=f"w1{tg}")
                t2_ = p.on("gpsimd", lambda e, slot=slot, sl_=sl_: e.dma_start(
                    out=wsl[slot][:, KC * FSL:KC * FSL + h1ch * D].rearrange(
                        "p (c n) -> p c n", c=h1ch),
                    in_=W2d[sl_ * h1ch * 128:(sl_ + 1) * h1ch * 128, :].rearrange(
                        "(c p) n -> p c n", p=128)), dma_chan=f"w{slot}", tag=f"w2{tg}")
                return (t1_, t2_)

            prev_w = wpair(0, 0, f"{s}{hf}_0")
            y_toks = []
            for sl_ in range(NSL):
                slot = sl_ % 2
                wt_ = prev_w
                if sl_ + 1 < NSL:
                    prev_w = wpair((sl_ + 1) % 2, sl_ + 1, f"{s}{hf}_{sl_+1}")
                cons_h = []
                for fc in range(h1ch):
                    for tg in range(NHALF * 128 // 512):
                        ps, war, b = ps_a.get()
                        last = None
                        for k in range(KC):
                            aft = [wt_[0]] + x1fm_done + ([war] if war and k == 0 else [])
                            last = p.on("tensor", lambda e, k=k, fc=fc, tg=tg, ps=ps,
                                        i0=i0, slot=slot: e.matmul(
                                ps[:, :],
                                wsl[slot][:, k * FSL + fc * 128:k * FSL + (fc + 1) * 128],
                                x1fm[k][:, i0 * 128 + tg * 512:i0 * 128 + (tg + 1) * 512],
                                start=(k == 0), stop=(k == KC - 1)),
                                after=(aft if k == 0 else []),
                                inc=(1 if k == KC - 1 else 0),
                                tag=f"h1{s}_{hf}_{sl_}_{fc}_{tg}")
                        ev = p.on("scalar", lambda e, fc=fc, tg=tg, sl_=sl_, ps=ps,
                                  HT=HT: e.activation(
                            S1[:, fc * HT + tg * 512:fc * HT + (tg + 1) * 512],
                            ps[:, :], AF.Gelu,
                            bias=b1cs[:, sl_ * h1ch + fc:sl_ * h1ch + fc + 1]),
                            after=[last, tb1], tag=f"ge{s}_{hf}_{sl_}_{fc}_{tg}")
                        ps_a.consumed(b, ev)
                        cons_h.append(ev)
                cons_y = []
                for ii in range(NHALF):
                    for dg in range(D // 512):
                        ps, war, b = ps_b.get()
                        last = None
                        for fk in range(h1ch):
                            aft = [wt_[1]] + cons_h + ([war] if war and fk == 0 else [])
                            last = p.on("tensor", lambda e, fk=fk, ii=ii, dg=dg, ps=ps,
                                        HT=HT, slot=slot: e.matmul(
                                ps[:, :], S1[:, fk * HT + ii * 128:fk * HT + (ii + 1) * 128],
                                wsl[slot][:, KC * FSL + fk * D + dg * 512:
                                          KC * FSL + fk * D + (dg + 1) * 512],
                                start=(fk == 0), stop=(fk == h1ch - 1)),
                                after=(aft if fk == 0 else []),
                                inc=(1 if fk == h1ch - 1 else 0),
                                tag=f"y{s}_{hf}_{sl_}_{ii}_{dg}")
                        ev = p.on("vector", lambda e, ii=ii, dg=dg, ps=ps: e.tensor_add(
                            y_acc[:, ii * D + dg * 512:ii * D + (dg + 1) * 512],
                            y_acc[:, ii * D + dg * 512:ii * D + (dg + 1) * 512],
                            ps[:, :]), after=[last] + yacc_done,
                            tag=f"ya{s}_{hf}_{sl_}_{ii}_{dg}")
                        ps_b.consumed(b, ev)
                        cons_y.append(ev)
                wwar[slot] = cons_h + cons_y
                y_toks = cons_y

            for ii in range(NHALF):
                i = i0 + ii
                sl = i % 2
                t1 = p.on("vector", lambda e, ii=ii: e.tensor_add(
                    st.dump[:, :512], y_acc[:, ii * D:ii * D + 512], b2b[:, :512]),
                    after=y_toks, tag=f"x2a{s}_{i}")
                t1b = p.on("vector", lambda e, ii=ii: e.tensor_add(
                    st.dump[:, 512:1024], y_acc[:, ii * D + 512:(ii + 1) * D],
                    b2b[:, 512:]), tag=f"x2b{s}_{i}")
                t2 = p.on("vector", lambda e, i=i: e.tensor_add(
                    st.dump[:, :D], st.dump[:, :D], S0[:, i * D:(i + 1) * D]),
                    after=[t1, t1b], tag=f"x2c{s}_{i}")
                st.ln_after = [t2]
                tu = emit_ln_tile(p, st, st.dump[:, :D], outt[:, :], i,
                                  tag=f"l2n{s}_{i}", gb=g2b[:, :], bb=be2b[:, :])
                tx = p.on("sync", lambda e, s=s, i=i, sl=sl: e.dma_start(
                    out=xt[sl][:, :], in_=Xd[(s * NT + i) * 128:(s * NT + i + 1) * 128, :]),
                    after=([xt_war[sl]] if xt_war[sl] else []), dma_chan=f"xt{sl}",
                    tag=f"lx2{s}_{i}")
                ty = p.on("vector", lambda e, sl=sl: e.tensor_add(
                    xt[sl][:, :], xt[sl][:, :], outt[:, :]),
                    after=[tx, tu], tag=f"xo{s}_{i}")
                ts_ = p.on("sync", lambda e, s=s, i=i, sl=sl: e.dma_start(
                    out=XOd[(s * NT + i) * 128:(s * NT + i + 1) * 128, :], in_=xt[sl][:, :]),
                    after=[ty], dma_chan=f"st{sl}", tag=f"sx{s}_{i}")
                xt_war[sl] = ts_

    return p


L0FFN_W = ["rT", "br", "idn", "idf", "W1", "W2", "b1c", "b2"]


def emit_l0ffn(p, nc, ctx, Xd, XOd, wd, T=4096, TG=1024, mode="full", C=4096,
               CNTd=None, cntd=None, prefd=None):
    E = 8
    KC = D // 128
    NT = TG // 128
    NG = T // TG
    NTT = T // 128
    rTd, brd, idnd, idfd = wd["rT"], wd["br"], wd["idn"], wd["idf"]
    if mode == "full":
        W1d, W2d, b1cd, b2d = wd["W1"], wd["W2"], wd["b1c"], wd["b2"]

    st = St()
    st.idbf = alloc(ctx, nc, "idbf", [128, 128], BF16)
    idf = alloc(ctx, nc, "idf_sb", [128, 128], F32)
    st.dump = alloc(ctx, nc, "dump", [128, D], F32)
    st.dump2 = alloc(ctx, nc, "dump2", [128, D], F32)
    st.epscol = alloc(ctx, nc, "epscol", [128, 1], F32)
    for nm in ("s1", "s2", "mcol", "vcol", "rcol"):
        setattr(st, nm, alloc(ctx, nc, nm, [128, NT], F32))

    xt = [alloc(ctx, nc, f"xt{i}", [128, D], F32) for i in range(2)]
    u_tm = alloc(ctx, nc, "u_tm", [128, NT * D], BF16)
    u_fm = [alloc(ctx, nc, f"ufm{k}", [128, TG], BF16) for k in range(KC)]
    rbf = alloc(ctx, nc, "rbf", [128, KC * E], BF16)
    brr = alloc(ctx, nc, "brr", [1, E], BF16)
    oh_fm = alloc(ctx, nc, "oh_fm", [8, NTT * 128], F32)
    onehot_tm = alloc(ctx, nc, "onehot_tm", [128, NTT * E], F32)
    gates = alloc(ctx, nc, "gates", [128, NTT], F32)
    cntloc = alloc(ctx, nc, "cntloc", [8, 1], F32)
    oner8 = alloc(ctx, nc, "oner8", [1, 128], BF16)
    prob = alloc(ctx, nc, "prob", [128, E], F32)
    eq = alloc(ctx, nc, "eq", [128, E], F32)
    cum = alloc(ctx, nc, "cum", [128, E], F32)
    zer8 = alloc(ctx, nc, "zer8", [128, E], F32)
    c1 = alloc(ctx, nc, "c1", [128, 4], F32)
    if mode == "full":
        h_fm = [alloc(ctx, nc, f"hfm{f}", [128, TG], BF16) for f in range(KC)]
        acc = alloc(ctx, nc, "acc", [128, NT * D], F32)
        wslot = [alloc(ctx, nc, f"w{i}", [128, 16384], BF16) for i in range(2)]
        pos_fm = alloc(ctx, nc, "pos_fm", [8, NTT * 128], F32)
        ohkb = alloc(ctx, nc, "ohkb", [8, NTT * 128], BF16)
        cnt_sb = alloc(ctx, nc, "cnt_sb", [8, E], F32)
        pref_sb = alloc(ctx, nc, "pref_sb", [8, 1], F32)
        base_sb = alloc(ctx, nc, "base_sb", [8, 1], F32)
        b2sb = alloc(ctx, nc, "b2sb", [8, D], BF16)
        b1cs = alloc(ctx, nc, "b1cs", [128, E * KC], F32)
        ones8f = alloc(ctx, nc, "ones8f", [8, 1], F32)
        zcol = alloc(ctx, nc, "zcol", [8, 1], F32)
        kg = alloc(ctx, nc, "kg", [128, NTT], F32)

    ps_tr = PsumRing(ctx, nc, "pstr", 2, 512, BF16)
    ps_oh = PsumRing(ctx, nc, "psoh", 1, 512, F32)
    ps_r = PsumRing(ctx, nc, "psr", 1, 512, F32)
    ps_a = PsumRing(ctx, nc, "psa", 2, 512, F32)
    ps_b = PsumRing(ctx, nc, "psb", 2, 512, F32)

    p.on("sync", lambda e: e.dma_start(out=st.idbf[:, :], in_=idnd), dma_chan="init")
    p.on("sync", lambda e: e.dma_start(out=idf[:, :], in_=idfd), dma_chan="init")
    p.on("sync", lambda e: e.dma_start(out=st.dump[0:1, :E], in_=brd), dma_chan="init")
    p.on("gpsimd", lambda e: e.dma_start(
        out=rbf[:, :].rearrange("p (k e) -> p k e", k=KC),
        in_=rTd.rearrange("(k p) e -> p k e", p=128)), dma_chan="initg")
    if mode == "full":
        p.on("sync", lambda e: e.dma_start(out=b1cs[:, :], in_=b1cd), dma_chan="init")
        p.on("sync", lambda e: e.dma_start(out=cnt_sb[:, :], in_=cntd), dma_chan="init")
        p.on("sync", lambda e: e.dma_start(out=pref_sb[:, :], in_=prefd),
             dma_chan="init")
        p.on("gpsimd", lambda e: e.dma_start(out=b2sb[:, :], in_=b2d), dma_chan="initg")
    init_done = p.chan_total("init")
    initg_done = p.chan_total("initg")
    tid = init_done
    p.on("vector", lambda e: e.memset(st.epscol[:, :], 1e-5))
    p.on("vector", lambda e: e.memset(zer8[:, :], 0.0))
    p.on("vector", lambda e: e.memset(oner8[:, :], 1.0))
    if mode == "full":
        p.on("vector", lambda e: e.memset(zcol[:, :], 0.0))
        p.on("vector", lambda e: e.memset(ones8f[:, :], 1.0))
    tbr = p.on("vector", lambda e: e.tensor_copy(brr[:, :], st.dump[0:1, :E]),
               after=[init_done])

    wwar = {0: [], 1: []}
    xt_war = [None, None]

    oh_done = []
    prev_rt_tail = None
    for g in range(NG):
        ut_toks = []
        for i in range(NT):
            sl = i % 2
            tx = p.on("sync", lambda e, g=g, i=i, sl=sl: e.dma_start(
                out=xt[sl][:, :], in_=Xd[(g * NT + i) * 128:(g * NT + i + 1) * 128, :]),
                after=([xt_war[sl]] if xt_war[sl] else []), dma_chan=f"xt{sl}",
                tag=f"lx{g}_{i}")
            st.ln_after = [tx]
            tu = emit_ln_tile(p, st, xt[sl][:, :], u_tm[:, i * D:(i + 1) * D], i,
                              tag=f"ln{g}_{i}")
            xt_war[sl] = tu
            ut_toks.append(tu)
        ufm_done = emit_transpose(p, st, lambda i: u_tm[:, i * D:(i + 1) * D],
                                  u_fm, NT, KC, ut_toks + [tid], ps_tr, tag=f"tr{g}")
        for i in range(NT):
            ti_ = g * NT + i
            ps, war, b = ps_r.get()
            last = None
            for k in range(KC):
                aft = ufm_done + [initg_done] + ([war] if war and k == 0 else [])
                last = p.on("tensor", lambda e, k=k, i=i, ps=ps: e.matmul(
                    ps[:, :E], u_fm[k][:, i * 128:(i + 1) * 128],
                    rbf[:, k * E:(k + 1) * E],
                    start=(k == 0), stop=False),
                    after=(aft if k == 0 else []), inc=0, tag=f"rt{g}_{i}")
            last = p.on("tensor", lambda e, ps=ps: e.matmul(
                ps[:, :E], oner8[0:1, :], brr[:, :], start=False, stop=True),
                after=[tbr], inc=1, tag=f"rtb{g}_{i}")
            tm_ = p.on("vector", lambda e, ps=ps: e.reduce_max(
                c1[:, 0:1], ps[:, :E], AX.X, negate=True), after=[last])
            te = p.on("scalar", lambda e, ps=ps: e.activation(
                prob[:, :], ps[:, :E], AF.Exp, bias=c1[:, 0:1]),
                after=[tm_] + ([prev_rt_tail] if prev_rt_tail else []))
            ps_r.consumed(b, te)
            p.on("vector", lambda e: e.reduce_sum(c1[:, 1:2], prob[:, :], AX.X),
                 after=[te])
            p.on("vector", lambda e, ti_=ti_: e.reciprocal(
                gates[:, ti_:ti_ + 1], c1[:, 1:2]))
            t1_ = p.on("vector", lambda e: e.reduce_max(c1[:, 2:3], prob[:, :], AX.X))
            p.on("vector", lambda e: e.tensor_scalar(eq[:, :], prob[:, :], c1[:, 2:3],
                                                     None, ALU.is_ge), after=[t1_])
            p.on("vector", lambda e: e.tensor_tensor_scan(cum[:, :], eq[:, :],
                                                          zer8[:, :], 0.0, ALU.add,
                                                          ALU.add))
            p.on("vector", lambda e: e.tensor_scalar(cum[:, :], cum[:, :], 1.0, None,
                                                     ALU.is_equal))
            tq = p.on("vector", lambda e, ti_=ti_: e.tensor_mul(
                onehot_tm[:, ti_ * E:(ti_ + 1) * E], eq[:, :], cum[:, :]),
                tag=f"oh{g}_{i}")
            prev_rt_tail = tq
        for i0 in range(0, NT, 4):
            ps, war, b = ps_oh.get()
            lastt = None
            for j in range(4):
                ti_ = g * NT + i0 + j
                aft = ([prev_rt_tail] if j == 0 else []) + ([war] if war and j == 0 else [])
                lastt = p.on("tensor", lambda e, ti_=ti_, j=j, ps=ps: e.transpose(
                    ps[0:8, j * 128:(j + 1) * 128],
                    onehot_tm[:, ti_ * E:(ti_ + 1) * E], idf[:, :]),
                    after=aft, inc=(1 if j == 3 else 0), tag=f"oht{g}_{i0}_{j}")
            tv = p.on("vector", lambda e, g=g, i0=i0, ps=ps: e.tensor_copy(
                oh_fm[:, (g * NT + i0) * 128:(g * NT + i0 + 4) * 128], ps[0:8, :]),
                after=[lastt], tag=f"ohe{g}_{i0}")
            ps_oh.consumed(b, tv)
            oh_done.append(tv)

    tcnt = p.on("vector", lambda e: e.reduce_sum(cntloc[:, :], oh_fm[:, :], AX.X),
                after=oh_done, tag="cnt")
    if mode == "count":
        p.on("sync", lambda e: e.dma_start(out=CNTd, in_=cntloc[:, :]), after=[tcnt],
             dma_chan="st0")
        return p

    psb_, warb, bb_ = ps_oh.get()
    tb_ = p.on("tensor", lambda e: e.matmul(psb_[0:8, 0:1], cnt_sb[:, :], pref_sb[:, :],
                                            start=True, stop=True),
               after=[init_done] + ([warb] if warb else []), tag="base")
    tb_ = p.on("vector", lambda e: e.tensor_copy(base_sb[:, :], psb_[0:8, 0:1]),
               after=[tb_])
    ps_oh.consumed(bb_, tb_)
    tpos = p.on("vector", lambda e: e.tensor_tensor_scan(
        pos_fm[:, :], oh_fm[:, :], zcol[:, 0:1].broadcast_to([8, NTT * 128]),
        base_sb[:, 0:1], ALU.add, ALU.add), after=[tb_], tag="scan")
    tkeep = p.on("vector", lambda e: e.tensor_scalar(pos_fm[:, :], pos_fm[:, :],
                                                     float(C - 1) + 0.5, None,
                                                     ALU.is_le), after=[tpos], tag="keep")
    tohk = p.on("vector", lambda e: e.tensor_mul(oh_fm[:, :], oh_fm[:, :], pos_fm[:, :]),
                after=[tkeep], tag="ohk")
    tohkb = p.on("vector", lambda e: e.tensor_copy(ohkb[:, :], oh_fm[:, :]),
                 after=[tohk], tag="ohkb")

    sel_done = []
    for ti_ in range(NTT):
        ps, war, b = ps_r.get()
        tkk = p.on("tensor", lambda e, ti_=ti_, ps=ps: e.matmul(
            ps[:, 0:1], oh_fm[:, ti_ * 128:(ti_ + 1) * 128], ones8f[:, :],
            start=True, stop=True),
            after=[tohk] + ([war] if war else []), tag=f"kt{ti_}")
        tkg = p.on("vector", lambda e, ti_=ti_, ps=ps: e.tensor_tensor(
            kg[:, ti_:ti_ + 1], ps[:, 0:1], gates[:, ti_:ti_ + 1], ALU.mult),
            after=[tkk], tag=f"kg{ti_}")
        ps_r.consumed(b, tkg)
        tsl = p.on("vector", lambda e, ti_=ti_: e.tensor_scalar(
            onehot_tm[:, ti_ * E:(ti_ + 1) * E], onehot_tm[:, ti_ * E:(ti_ + 1) * E],
            kg[:, ti_:ti_ + 1], None, ALU.mult), tag=f"sel{ti_}")
        sel_done.append(tsl)

    def wpair(slot, e_, tg):
        t1_ = p.on("gpsimd", lambda e, slot=slot, e_=e_: e.dma_start(
            out=wslot[slot][:, :KC * D].rearrange("p (c n) -> p c n", c=KC),
            in_=W1d[e_].rearrange("(c p) n -> p c n", p=128)), dma_chan=f"w{slot}",
            after=wwar[slot], tag=f"w1{tg}")
        t2_ = p.on("gpsimd", lambda e, slot=slot, e_=e_: e.dma_start(
            out=wslot[slot][:, KC * D:2 * KC * D].rearrange("p (c n) -> p c n", c=KC),
            in_=W2d[e_].rearrange("(c p) n -> p c n", p=128)), dma_chan=f"w{slot}",
            tag=f"w2{tg}")
        return (t1_, t2_)

    for g in range(NG):
        ut_toks = []
        for i in range(NT):
            sl = i % 2
            ti_ = g * NT + i
            tx = p.on("sync", lambda e, g=g, i=i, sl=sl: e.dma_start(
                out=xt[sl][:, :], in_=Xd[(g * NT + i) * 128:(g * NT + i + 1) * 128, :]),
                after=([xt_war[sl]] if xt_war[sl] else []), dma_chan=f"xt{sl}",
                tag=f"lxc{g}_{i}")
            st.ln_after = [tx]
            tu = emit_ln_tile(p, st, xt[sl][:, :], u_tm[:, i * D:(i + 1) * D], i,
                              tag=f"lnc{g}_{i}")
            tu = p.on("vector", lambda e, i=i, ti_=ti_: e.tensor_scalar(
                u_tm[:, i * D:(i + 1) * D], u_tm[:, i * D:(i + 1) * D],
                gates[:, ti_:ti_ + 1], None, ALU.mult), after=[tu], tag=f"ug{g}_{i}")
            xt_war[sl] = tu
            ut_toks.append(tu)
        ufm_done = emit_transpose(p, st, lambda i: u_tm[:, i * D:(i + 1) * D],
                                  u_fm, NT, KC, ut_toks + [tid], ps_tr, tag=f"trc{g}")

        p.on("vector", lambda e: e.memset(acc[:, :], 0.0), tag=f"acc0{g}")
        prev_w = wpair(0, 0, f"{g}_0") if g == 0 else prev_w_next
        for e_ in range(E):
            slot = e_ % 2
            wt_ = prev_w
            if e_ + 1 < E:
                prev_w = wpair((e_ + 1) % 2, e_ + 1, f"{g}_{e_+1}")
            elif g + 1 < NG:
                prev_w_next = wpair((e_ + 1) % 2, 0, f"{g+1}_0")
            cons_h = []
            for f in range(KC):
                for tg4 in range(TG // 512):
                    ps, war, b = ps_a.get()
                    last = None
                    for k in range(KC):
                        aft = [wt_[0]] + ufm_done + ([war] if war and k == 0 else [])
                        last = p.on("tensor", lambda e, k=k, f=f, tg4=tg4, ps=ps,
                                    slot=slot: e.matmul(
                            ps[:, :], wslot[slot][:, k * D + f * 128:k * D + (f + 1) * 128],
                            u_fm[k][:, tg4 * 512:(tg4 + 1) * 512],
                            start=(k == 0), stop=(k == KC - 1)),
                            after=(aft if k == 0 else []), inc=(1 if k == KC - 1 else 0),
                            tag=f"h{g}_{e_}_{f}_{tg4}")
                    ev = p.on("scalar", lambda e, f=f, tg4=tg4, e_=e_, ps=ps: e.activation(
                        h_fm[f][:, tg4 * 512:(tg4 + 1) * 512], ps[:, :], AF.Relu,
                        bias=b1cs[:, e_ * KC + f:e_ * KC + f + 1]),
                        after=[last, init_done], tag=f"hr{g}_{e_}_{f}_{tg4}")
                    ps_a.consumed(b, ev)
                    cons_h.append(ev)
            cons_y = []
            for i in range(NT):
                ti_ = g * NT + i
                for dg in range(D // 512):
                    ps, war, b = ps_b.get()
                    last = None
                    for fk in range(KC):
                        aft = [wt_[1]] + cons_h + ([war] if war and fk == 0 else [])
                        last = p.on("tensor", lambda e, fk=fk, i=i, dg=dg, ps=ps,
                                    slot=slot: e.matmul(
                            ps[:, :], h_fm[fk][:, i * 128:(i + 1) * 128],
                            wslot[slot][:, KC * D + fk * D + dg * 512:
                                        KC * D + fk * D + (dg + 1) * 512],
                            start=(fk == 0), stop=(fk == KC - 1)),
                            after=(aft if fk == 0 else []),
                            inc=(1 if fk == KC - 1 else 0), tag=f"y{g}_{e_}_{i}_{dg}")
                    ev = p.on("vector", lambda e, i=i, dg=dg, ti_=ti_, e_=e_, ps=ps:
                              e.scalar_tensor_tensor(
                        acc[:, i * D + dg * 512:i * D + (dg + 1) * 512], ps[:, :],
                        onehot_tm[:, ti_ * E + e_:ti_ * E + e_ + 1],
                        acc[:, i * D + dg * 512:i * D + (dg + 1) * 512],
                        ALU.mult, ALU.add), after=[last] + sel_done,
                        tag=f"ac{g}_{e_}_{i}_{dg}")
                    ps_b.consumed(b, ev)
                    cons_y.append(ev)
            wwar[slot] = cons_h + cons_y

        for i in range(NT):
            ti_ = g * NT + i
            sl = i % 2
            evs = []
            for dg in range(D // 512):
                ps, war, b = ps_b.get()
                tbm = p.on("tensor", lambda e, ti_=ti_, dg=dg, ps=ps: e.matmul(
                    ps[:, :], ohkb[:, ti_ * 128:(ti_ + 1) * 128],
                    b2sb[:, dg * 512:(dg + 1) * 512], start=True, stop=True),
                    after=[tohkb, initg_done] + ([war] if war else []),
                    tag=f"b2{g}_{i}_{dg}")
                ev = p.on("vector", lambda e, i=i, dg=dg, ti_=ti_, ps=ps:
                          e.scalar_tensor_tensor(
                    acc[:, i * D + dg * 512:i * D + (dg + 1) * 512], ps[:, :],
                    gates[:, ti_:ti_ + 1],
                    acc[:, i * D + dg * 512:i * D + (dg + 1) * 512],
                    ALU.mult, ALU.add), after=[tbm], tag=f"b2e{g}_{i}_{dg}")
                ps_b.consumed(b, ev)
                evs.append(ev)
            tx = p.on("sync", lambda e, g=g, i=i, sl=sl: e.dma_start(
                out=xt[sl][:, :], in_=Xd[(g * NT + i) * 128:(g * NT + i + 1) * 128, :]),
                after=[xt_war[sl]] if xt_war[sl] else [], dma_chan=f"xt{sl}",
                tag=f"lxe{g}_{i}")
            ty = p.on("vector", lambda e, i=i, sl=sl: e.scalar_tensor_tensor(
                xt[sl][:, :], acc[:, i * D:(i + 1) * D], 0.5, xt[sl][:, :],
                ALU.mult, ALU.add), after=[tx] + evs, tag=f"xo{g}_{i}")
            ts_ = p.on("sync", lambda e, g=g, i=i, sl=sl: e.dma_start(
                out=XOd[(g * NT + i) * 128:(g * NT + i + 1) * 128, :], in_=xt[sl][:, :]),
                after=[ty], dma_chan=f"st{sl}", tag=f"sx{g}_{i}")
            xt_war[sl] = ts_

    return p


L0ATTN_W = ["nwt", "ipT", "convw", "convb", "dtb", "alog", "dvec", "gnwt",
            "zwT", "owT", "fT", "idn", "idf", "LTi", "UTs", "ALO", "SDG", "selc"]


def emit_l0attn(p, nc, ctx, Xd, XOd, wd, T=4096, S=2048):
    DI, DS, NH, HD = 2048, 128, 32, 64
    CH = 18
    KC = D // 128
    NSEQ = T // S
    GT = 512
    NGS = S // GT
    NC4 = GT // 128
    GW = 516
    NTS = S // 128
    (nwtd, ipTd, cwd, cbd, dtbd, alogd, dvecd, gnwd, zwTd, owTd, fTd, idnd,
     idfd, LTid, UTsd, ALOd, SDGd, seld) = (wd[k] for k in L0ATTN_W)

    st = St()
    st.idbf = alloc(ctx, nc, "idbf", [128, 128], BF16)
    idf = alloc(ctx, nc, "idf_sb", [128, 128], F32)
    st.dump = alloc(ctx, nc, "dump", [128, D], F32)
    st.dump2 = alloc(ctx, nc, "dump2", [128, DI], F32)
    st.epscol = alloc(ctx, nc, "epscol", [128, 1], F32)
    for nm in ("s1", "s2", "mcol", "vcol", "rcol"):
        setattr(st, nm, alloc(ctx, nc, nm, [128, 16], F32))
    onef = alloc(ctx, nc, "onef", [128, 1], F32)
    gst = alloc(ctx, nc, "gst", [128, 8], F32)

    xt = [alloc(ctx, nc, f"xt{i}", [128, D], F32) for i in range(2)]
    xn_tm = alloc(ctx, nc, "xn_tm", [128, D], BF16)
    xn_fm = [alloc(ctx, nc, f"xnfm{k}", [128, S], BF16) for k in range(KC)]
    nwt = alloc(ctx, nc, "nwt_sb", [128, D], F32)
    gnwt = alloc(ctx, nc, "gnwt_sb", [128, DI], BF16)
    LTi = alloc(ctx, nc, "LTi_sb", [128, 128], F32)
    UTs = alloc(ctx, nc, "UTs_sb", [128, 128], F32)
    ALO = alloc(ctx, nc, "ALO_sb", [128, 128], F32)
    SDG = alloc(ctx, nc, "SDG_sb", [128, 128], F32)
    sel = alloc(ctx, nc, "sel_sb", [32, NH * 128], F32)
    cw = alloc(ctx, nc, "cw_sb", [128, CH * 4], F32)
    cb = alloc(ctx, nc, "cb_sb", [128, CH], F32)
    dtw = alloc(ctx, nc, "dtw", [128, KC * NH], BF16)
    rowf = alloc(ctx, nc, "rowf", [1, 3 * NH], F32)
    negexpA = alloc(ctx, nc, "negexpA", [128, NH], F32)
    Dbc = alloc(ctx, nc, "Dbc", [128, NH], F32)
    dtbb = alloc(ctx, nc, "dtbb", [128, NH], F32)
    onerf = alloc(ctx, nc, "onerf", [1, 128], F32)
    dt_tm = alloc(ctx, nc, "dt_tm", [128, NTS * NH], F32)
    hin = alloc(ctx, nc, "hin", [128, DI], BF16)
    xbcg = alloc(ctx, nc, "xbcg", [128, CH * GW], BF16)
    convg = alloc(ctx, nc, "convg", [128, CH * GT], BF16)
    hist = alloc(ctx, nc, "hist", [128, CH * 4], BF16)
    xs_tm = alloc(ctx, nc, "xs_tm", [128, DI], BF16)
    b_tm = alloc(ctx, nc, "b_tm", [128, 128], BF16)
    wx = alloc(ctx, nc, "wx", [128, DI], BF16)
    acol = alloc(ctx, nc, "acol", [128, NH], F32)
    alp = alloc(ctx, nc, "alp", [128, NH], F32)
    nalp = alloc(ctx, nc, "nalp", [128, NH], F32)
    ealp = alloc(ctx, nc, "ealp", [128, NH], F32)
    wti = alloc(ctx, nc, "wti", [128, NH], F32)
    eaL = alloc(ctx, nc, "eaL", [128, NH], F32)
    alpf = alloc(ctx, nc, "alpf", [32, 128], F32)
    esb = alloc(ctx, nc, "esb", [128, 128], F32)
    ssb = alloc(ctx, nc, "ssb", [128, 128], BF16)
    tg1 = alloc(ctx, nc, "tg1", [128, 512], F32)
    tg2 = alloc(ctx, nc, "tg2", [128, 512], F32)
    ygfm = alloc(ctx, nc, "ygfm", [128, 16 * GT], BF16)
    out_acc = alloc(ctx, nc, "out_acc", [128, NC4 * D], F32)
    qtile = alloc(ctx, nc, "qtile", [128, D], F32)
    carry = alloc(ctx, nc, "carry", [1, D], F32)
    wblk = alloc(ctx, nc, "wblk", [128, 2 * 4096], BF16)
    lblk = alloc(ctx, nc, "lblk", [128, 2 * 1024], BF16)

    ps_tr = PsumRing(ctx, nc, "pstr", 2, 512, BF16)
    ps_a = PsumRing(ctx, nc, "psa", 2, 512, F32)
    ps_b = PsumRing(ctx, nc, "psb", 2, 512, F32)
    ps_s = PsumRing(ctx, nc, "pss", 1, 512, F32)
    ps_f = PsumRing(ctx, nc, "psf", 1, 512, F32)

    for dst, srcd in [(st.idbf, idnd), (idf, idfd), (nwt, nwtd), (LTi, LTid),
                      (UTs, UTsd), (ALO, ALOd), (SDG, SDGd), (sel, seld),
                      (cw, cwd), (cb, cbd)]:
        p.on("sync", lambda e, dst=dst, srcd=srcd: e.dma_start(out=dst[:, :], in_=srcd),
             dma_chan="init")
    p.on("sync", lambda e: e.dma_start(out=rowf[:, 0:NH], in_=dtbd), dma_chan="init")
    p.on("sync", lambda e: e.dma_start(out=rowf[:, NH:2 * NH], in_=alogd),
         dma_chan="init")
    p.on("sync", lambda e: e.dma_start(out=rowf[:, 2 * NH:3 * NH], in_=dvecd),
         dma_chan="init")
    p.on("gpsimd", lambda e: e.dma_start(out=gnwt[:, :], in_=gnwd), dma_chan="initg")
    p.on("gpsimd", lambda e: e.dma_start(
        out=dtw[:, :].rearrange("p (k n) -> p k n", k=KC),
        in_=ipTd[:, 4352:4384].rearrange("(k p) n -> p k n", p=128)), dma_chan="initg")
    init_done = p.chan_total("init")
    initg_done = p.chan_total("initg")
    tid = init_done
    p.on("vector", lambda e: e.memset(st.epscol[:, :], 1e-5))
    p.on("vector", lambda e: e.memset(onerf[:, :], 1.0))
    p.on("vector", lambda e: e.memset(onef[:, :], 1.0))
    t_ = p.on("scalar", lambda e: e.activation(rowf[:, NH:2 * NH], rowf[:, NH:2 * NH],
                                               AF.Exp), after=[init_done], tag="expA")
    t_ = p.on("vector", lambda e: e.tensor_scalar(rowf[:, NH:2 * NH],
                                                  rowf[:, NH:2 * NH], -1.0, None,
                                                  ALU.mult), after=[t_])
    bcs = []
    for dst, c0 in [(dtbb, 0), (negexpA, NH), (Dbc, 2 * NH)]:
        ps, war, b = ps_f.get()
        tb = p.on("tensor", lambda e, c0=c0, ps=ps: e.matmul(
            ps[:, :NH], onerf[0:1, :], rowf[:, c0:c0 + NH], start=True, stop=True),
            after=[t_] + ([war] if war else []), tag=f"bc{c0}")
        tb = p.on("vector", lambda e, dst=dst, ps=ps: e.tensor_copy(dst[:, :],
                                                                    ps[:, :NH]),
                  after=[tb])
        ps_f.consumed(b, tb)
        bcs.append(tb)
    bc_done = list(bcs)
    onecol1 = onef

    xt_war = [None, None]
    lblk_war = [[], []]
    wblk_war = [[], []]
    ygrp_war = None
    oa_war = []

    def lblk_load(idx, srcap, tag):
        sl = idx % 2
        return p.on("gpsimd", lambda e, sl=sl, srcap=srcap: e.dma_start(
            out=lblk[:, sl * 1024:(sl + 1) * 1024].rearrange("p (c n) -> p c n", c=8),
            in_=srcap.rearrange("(c p) n -> p c n", p=128)),
            dma_chan="lw", after=(lblk_war[sl] or []) + [p.barrier_token("tensor")],
            tag=tag), sl

    def wblk_load(idx, srcap, tag):
        sl = idx % 2
        return p.on("gpsimd", lambda e, sl=sl, srcap=srcap: e.dma_start(
            out=wblk[:, sl * 4096:(sl + 1) * 4096].rearrange("p (c n) -> p c n", c=4),
            in_=srcap.rearrange("(c p) n -> p c n", p=128)),
            dma_chan="ww", after=(wblk_war[sl] or []) + [p.barrier_token("tensor")],
            tag=tag), sl

    for s in range(NSEQ):
        xn_toks = []
        for i in range(NTS):
            sl = i % 2
            tx = p.on("sync", lambda e, s=s, i=i, sl=sl: e.dma_start(
                out=xt[sl][:, :], in_=Xd[(s * NTS + i) * 128:(s * NTS + i + 1) * 128, :]),
                after=([xt_war[sl]] if xt_war[sl] else []), dma_chan=f"xt{sl}",
                tag=f"lx{s}_{i}")
            st.ln_after = [tx]
            tu = emit_ln_tile(p, st, xt[sl][:, :], st.dump[:, :], i, tag=f"ln{s}_{i}")
            xt_war[sl] = tu
            t2 = p.on("scalar", lambda e, i=i: e.activation(
                st.dump2[:, :D], st.dump[:, :], AF.Square,
                accum_out=st.s2[:, i:i + 1]), after=[tu], tag=f"rs{s}_{i}")
            t3 = p.on("vector", lambda e, i=i: e.tensor_scalar(
                st.vcol[:, i:i + 1], st.s2[:, i:i + 1], 1.0 / D, None, ALU.mult),
                after=[t2])
            t4 = p.on("scalar", lambda e, i=i: e.activation(
                st.vcol[:, i:i + 1], st.vcol[:, i:i + 1], AF.Sqrt,
                bias=st.epscol[:, 0:1]), after=[t3])
            t5 = p.on("vector", lambda e, i=i: e.reciprocal(
                st.s1[:, i:i + 1], st.vcol[:, i:i + 1]), after=[t4])
            t6 = p.on("vector", lambda e, i=i: e.tensor_scalar(
                st.dump[:, :], st.dump[:, :], st.s1[:, i:i + 1], None, ALU.mult),
                after=[t5])
            t7 = p.on("vector", lambda e: e.tensor_mul(xn_tm[:, :], st.dump[:, :],
                                                       nwt[:, :]),
                      after=[init_done, t6], tag=f"xn{s}_{i}")
            tts = emit_transpose(p, st, lambda _i: xn_tm[:, :],
                                 [xn_fm[k][:, i * 128:(i + 1) * 128] for k in range(KC)],
                                 1, KC, [t7, tid], ps_tr, tag=f"xtr{s}_{i}")
            xn_toks.extend(tts)

        dt_toks = []
        for i in range(NTS):
            ps, war, b = ps_f.get()
            last = None
            for k in range(KC):
                aft = xn_toks + [initg_done] + ([war] if war and k == 0 else [])
                last = p.on("tensor", lambda e, k=k, i=i, ps=ps: e.matmul(
                    ps[:, :NH], xn_fm[k][:, i * 128:(i + 1) * 128],
                    dtw[:, k * NH:(k + 1) * NH], start=(k == 0), stop=(k == KC - 1)),
                    after=(aft if k == 0 else []), inc=(1 if k == KC - 1 else 0),
                    tag=f"dt{s}_{i}")
            ta = p.on("vector", lambda e, ps=ps: e.tensor_tensor(
                tg1[:, :NH], ps[:, :NH], dtbb[:, :], ALU.add),
                after=[last] + bc_done, tag=f"sp0{s}_{i}")
            ps_f.consumed(b, ta)
            tb = p.on("scalar", lambda e: e.activation(tg2[:, :NH], tg1[:, :NH],
                                                       AF.Abs), after=[ta])
            tc = p.on("scalar", lambda e: e.activation(tg2[:, :NH], tg2[:, :NH],
                                                       AF.Exp, scale=-1.0), after=[tb])
            td = p.on("scalar", lambda e: e.activation(tg2[:, :NH], tg2[:, :NH],
                                                       AF.Ln, bias=onecol1[:, 0:1]),
                      after=[tc])
            te = p.on("vector", lambda e: e.tensor_scalar_max(tg1[:, :NH], tg1[:, :NH],
                                                              0.0), after=[td])
            tf = p.on("vector", lambda e, i=i: e.tensor_add(
                dt_tm[:, i * NH:(i + 1) * NH], tg1[:, :NH], tg2[:, :NH]), after=[te],
                tag=f"sp1{s}_{i}")
            dt_toks.append(tf)

        th0 = p.on("vector", lambda e: e.memset(hin[:, :], 0.0), tag=f"h0{s}")
        p.on("vector", lambda e: e.memset(hist[:, :], 0.0), tag=f"hi0{s}")
        tcar = p.on("vector", lambda e: e.memset(carry[:, :], 0.0), tag=f"cr0{s}")
        carry_tok = [tcar]

        for g in range(NGS):
            g0 = g * GT
            thc = p.on("vector", lambda e: e.tensor_copy(
                xbcg[:, :].rearrange("p (c w) -> p c w", c=CH)[:, :, 0:3],
                hist[:, :].rearrange("p (c w) -> p c w", c=CH)[:, :, 0:3]),
                after=[ygrp_war] if ygrp_war else [], tag=f"hc{s}_{g}")
            xbc_toks = []
            lw, lsl = lblk_load(0, ipTd[:, 2048:2048 + 128], f"ipw{s}_{g}_0")
            for c in range(CH):
                wt_, wsl_ = lw, lsl
                if c + 1 < CH:
                    lw, lsl = lblk_load(c + 1,
                                        ipTd[:, 2048 + (c + 1) * 128:2048 + (c + 2) * 128],
                                        f"ipw{s}_{g}_{c+1}")
                ps, war, b = ps_a.get()
                last = None
                for k in range(KC):
                    aft = [wt_] + xn_toks + ([war] if war and k == 0 else [])
                    last = p.on("tensor", lambda e, k=k, ps=ps, wsl_=wsl_, g0=g0: e.matmul(
                        ps[:, :], lblk[:, wsl_ * 1024 + k * 128:wsl_ * 1024 + (k + 1) * 128],
                        xn_fm[k][:, g0:g0 + GT],
                        start=(k == 0), stop=(k == KC - 1)),
                        after=(aft if k == 0 else []), inc=(1 if k == KC - 1 else 0),
                        tag=f"xbc{s}_{g}_{c}_{k}")
                lblk_war[wsl_] = [last]
                ev = p.on("vector", lambda e, c=c, ps=ps: e.tensor_copy(
                    xbcg[:, c * GW + 3:c * GW + 3 + GT], ps[:, :]),
                    after=[last, thc], tag=f"xbce{s}_{g}_{c}")
                ps_a.consumed(b, ev)
                xbc_toks.append(ev)
            thn = p.on("vector", lambda e: e.tensor_copy(
                hist[:, :].rearrange("p (c w) -> p c w", c=CH)[:, :, 0:3],
                xbcg[:, :].rearrange("p (c w) -> p c w", c=CH, w=GW)[:, :, GT:GT + 3]),
                after=xbc_toks, tag=f"hn{s}_{g}")

            conv_toks = []
            for c in range(CH):
                t0 = p.on("vector", lambda e, c=c: e.tensor_scalar(
                    tg1[:, :], xbcg[:, c * GW:c * GW + GT], cw[:, c * 4:c * 4 + 1],
                    None, ALU.mult),
                    after=xbc_toks + [init_done] + ([conv_toks[-1]] if conv_toks else []),
                    tag=f"cv0{s}_{g}_{c}")
                for kk in range(1, 4):
                    t0 = p.on("vector", lambda e, c=c, kk=kk: e.scalar_tensor_tensor(
                        tg1[:, :], xbcg[:, c * GW + kk:c * GW + kk + GT],
                        cw[:, c * 4 + kk:c * 4 + kk + 1], tg1[:, :],
                        ALU.mult, ALU.add), after=[t0], tag=f"cv{kk}_{s}_{g}_{c}")
                t1 = p.on("scalar", lambda e, c=c: e.activation(
                    convg[:, c * GT:(c + 1) * GT], tg1[:, :], AF.Silu,
                    bias=cb[:, c:c + 1]), after=[t0], tag=f"cvs{s}_{g}_{c}")
                conv_toks.append(t1)

            ychk_toks = []
            for c4 in range(NC4):
                ci = g * NC4 + c4
                co = c4 * 128
                xs_toks = []
                for cc in range(16):
                    pstile, war, b = ps_tr.get()
                    tt = p.on("tensor", lambda e, cc=cc, co=co, pstile=pstile: e.transpose(
                        pstile[:, 0:128], convg[:, cc * GT + co:cc * GT + co + 128],
                        st.idbf[:, :]),
                        after=conv_toks + ([war] if war else []),
                        tag=f"xst{s}_{g}_{c4}_{cc}")
                    tv = p.on("vector", lambda e, cc=cc, pstile=pstile: e.tensor_copy(
                        xs_tm[:, cc * 128:(cc + 1) * 128], pstile[:, 0:128]),
                        after=[tt], tag=f"xse{s}_{g}_{c4}_{cc}")
                    ps_tr.consumed(b, tv)
                    xs_toks.append(tv)
                pstile, war, b = ps_tr.get()
                tt = p.on("tensor", lambda e, co=co, pstile=pstile: e.transpose(
                    pstile[:, 0:128], convg[:, 16 * GT + co:16 * GT + co + 128],
                    st.idbf[:, :]), after=conv_toks + ([war] if war else []),
                    tag=f"btt{s}_{g}_{c4}")
                tbm = p.on("vector", lambda e, pstile=pstile: e.tensor_copy(
                    b_tm[:, :], pstile[:, 0:128]), after=[tt], tag=f"bte{s}_{g}_{c4}")
                ps_tr.consumed(b, tbm)
                ta0 = p.on("vector", lambda e, ci=ci: e.tensor_mul(
                    acol[:, :], dt_tm[:, ci * NH:(ci + 1) * NH], negexpA[:, :]),
                    after=dt_toks + bc_done, tag=f"a{s}_{ci}")
                psA, warA, bA = ps_f.get()
                tA = p.on("tensor", lambda e, psA=psA: e.matmul(
                    psA[:, :NH], LTi[:, :], acol[:, :], start=True, stop=True),
                    after=[ta0, init_done] + ([warA] if warA else []), tag=f"al{s}_{ci}")
                tA1 = p.on("vector", lambda e, psA=psA: e.tensor_copy(
                    alp[:, :], psA[:, :NH]), after=[tA])
                tA2 = p.on("vector", lambda e, psA=psA: e.tensor_scalar(
                    nalp[:, :], psA[:, :NH], -1.0, None, ALU.mult), after=[tA])
                tA3 = p.on("scalar", lambda e: e.activation(ealp[:, :], alp[:, :],
                                                            AF.Exp), after=[tA1])
                ps_f.consumed(bA, tA2)
                psB, warB, bB = ps_f.get()
                tB = p.on("tensor", lambda e, psB=psB: e.matmul(
                    psB[:, :NH], UTs[:, :], acol[:, :], start=True, stop=True),
                    after=[ta0] + ([warB] if warB else []), tag=f"bml{s}_{ci}")
                tB1 = p.on("scalar", lambda e, psB=psB: e.activation(
                    wti[:, :], psB[:, :NH], AF.Exp), after=[tB])
                tB2 = p.on("vector", lambda e, ci=ci: e.tensor_mul(
                    wti[:, :], wti[:, :], dt_tm[:, ci * NH:(ci + 1) * NH]),
                    after=[tB1])
                ps_f.consumed(bB, tB2)
                psC, warC, bC = ps_f.get()
                tC = p.on("tensor", lambda e, psC=psC: e.matmul(
                    psC[:, :NH], ALO[:, :], acol[:, :], start=True, stop=True),
                    after=[ta0] + ([warC] if warC else []), tag=f"aL{s}_{ci}")
                tC1 = p.on("scalar", lambda e, psC=psC: e.activation(
                    eaL[:, :], psC[:, :NH], AF.Exp), after=[tC])
                ps_f.consumed(bC, tC1)
                psD, warD, bD = ps_f.get()
                tD = p.on("tensor", lambda e, psD=psD: e.transpose(
                    psD[0:32, 0:128], alp[:, :NH], idf[:, :]),
                    after=[tA1] + ([warD] if warD else []), tag=f"att{s}_{ci}")
                tD1 = p.on("vector", lambda e, psD=psD: e.tensor_copy(
                    alpf[:, :], psD[0:32, 0:128]), after=[tD])
                ps_f.consumed(bD, tD1)
                twx = p.on("vector", lambda e: e.tensor_mul(
                    wx[:, :].rearrange("p (h m) -> p h m", m=HD),
                    xs_tm[:, :].rearrange("p (h m) -> p h m", m=HD),
                    wti[:, :].unsqueeze(2).broadcast_to([128, NH, HD])),
                    after=xs_toks + [tB2], tag=f"wx{s}_{ci}")
                psG = ps_a.tiles[0]
                aftG = conv_toks + ([ps_a.war[0]] if ps_a.war[0] else [])
                tG = p.on("tensor", lambda e, co=co, psG=psG: e.matmul(
                    psG[:, 0:128], convg[:, 16 * GT + co:16 * GT + co + 128],
                    convg[:, 17 * GT + co:17 * GT + co + 128], start=True, stop=True),
                    after=aftG, tag=f"G{s}_{ci}")
                psE = ps_a.tiles[1]
                psY1 = ps_b.tiles[0]
                psY2 = ps_b.tiles[1]
                psS_ = ps_s.tiles[0]
                e_war = ps_a.war[1]
                y1_war = ps_b.war[0]
                y2_war = ps_b.war[1]
                s_war = ps_s.war[0]
                lastyi = lastye = lastst = None
                comb = None
                for h in range(NH):
                    g8 = h // 8
                    hm = h % 8
                    aftE = [tD1] + ([init_done] if h == 0 else []) + ([e_war] if e_war else [])
                    tE = p.on("tensor", lambda e, h=h, psE=psE: e.matmul(
                        psE[:, 0:128], sel[:, h * 128:(h + 1) * 128], alpf[:, :],
                        start=True, stop=False), after=aftE, inc=0, tag=f"E{s}_{ci}_{h}")
                    tE = p.on("tensor", lambda e, psE=psE: e.matmul(
                        psE[:, 0:128], LTi[:, :], SDG[:, :], start=False, stop=True),
                        inc=1, tag=f"Et{s}_{ci}_{h}")
                    tEx = p.on("scalar", lambda e, h=h, psE=psE: e.activation(
                        esb[:, :], psE[:, 0:128], AF.Exp, bias=nalp[:, h:h + 1]),
                        after=[tE, tA2], tag=f"Ex{s}_{ci}_{h}")
                    e_war = tEx
                    tS = p.on("vector", lambda e, h=h, ci=ci, psG=psG: e.scalar_tensor_tensor(
                        ssb[:, :], esb[:, :], dt_tm[:, ci * NH + h:ci * NH + h + 1],
                        psG[:, 0:128], ALU.mult, ALU.mult), after=[tEx, tG],
                        tag=f"S{s}_{ci}_{h}")
                    aft = [tS] + ([y1_war] if y1_war and h == 0 else [])
                    lastyi = p.on("tensor", lambda e, h=h, hm=hm, psY1=psY1: e.matmul(
                        psY1[:, hm * 64:(hm + 1) * 64], ssb[:, :],
                        xs_tm[:, h * 64:(h + 1) * 64], start=True, stop=True),
                        after=aft, inc=(1 if hm == 7 else 0), tag=f"yi{s}_{ci}_{h}")
                    aft = ([th0] + ([y2_war] if y2_war else [])) if h == 0 else []
                    lastye = p.on("tensor", lambda e, h=h, hm=hm, co=co, psY2=psY2: e.matmul(
                        psY2[:, hm * 64:(hm + 1) * 64],
                        convg[:, 17 * GT + co:17 * GT + co + 128],
                        hin[:, h * 64:(h + 1) * 64], start=True, stop=True),
                        after=aft, inc=(1 if hm == 7 else 0), tag=f"ye{s}_{ci}_{h}")
                    aft = ([twx, tbm] + ([s_war] if s_war else [])) if h == 0 else []
                    lastst = p.on("tensor", lambda e, h=h, hm=hm, psS_=psS_: e.matmul(
                        psS_[:, hm * 64:(hm + 1) * 64], b_tm[:, :],
                        wx[:, h * 64:(h + 1) * 64], start=True, stop=True),
                        after=aft, inc=(1 if hm == 7 else 0), tag=f"st{s}_{ci}_{h}")
                    if hm == 7:
                        g8o = g8 * 512
                        t1 = p.on("vector", lambda e, g8=g8, psY2=psY2: e.tensor_mul(
                            tg1[:, :].rearrange("p (h m) -> p h m", m=64),
                            psY2[:, :].rearrange("p (h m) -> p h m", m=64),
                            ealp[:, g8 * 8:(g8 + 1) * 8].unsqueeze(2).broadcast_to(
                                [128, 8, 64])), after=[lastye, tA3],
                            tag=f"cb1{s}_{ci}_{g8}")
                        y2_war = t1
                        t2 = p.on("vector", lambda e, g8=g8, g8o=g8o: e.tensor_mul(
                            tg2[:, :].rearrange("p (h m) -> p h m", m=64),
                            xs_tm[:, g8o:g8o + 512].rearrange("p (h m) -> p h m", m=64),
                            Dbc[:, g8 * 8:(g8 + 1) * 8].unsqueeze(2).broadcast_to(
                                [128, 8, 64])), after=xs_toks + bc_done,
                            tag=f"cb2{s}_{ci}_{g8}")
                        t3 = p.on("vector", lambda e: e.tensor_add(
                            tg1[:, :], tg1[:, :], tg2[:, :]), after=[t1, t2])
                        t4 = p.on("vector", lambda e, c4=c4, g8o=g8o, psY1=psY1: e.tensor_tensor(
                            xbcg[:, c4 * DI + g8o:c4 * DI + g8o + 512],
                            tg1[:, :], psY1[:, :], ALU.add),
                            after=[t3, lastyi, thn], tag=f"cb3{s}_{ci}_{g8}")
                        y1_war = t4
                        comb = t4
                        ychk_toks.append(t4)
                        t5 = p.on("vector", lambda e, g8=g8, g8o=g8o: e.tensor_mul(
                            tg2[:, :].rearrange("p (h m) -> p h m", m=64),
                            hin[:, g8o:g8o + 512].rearrange("p (h m) -> p h m", m=64),
                            eaL[:, g8 * 8:(g8 + 1) * 8].unsqueeze(2).broadcast_to(
                                [128, 8, 64])), after=[lastst, tC1],
                            tag=f"sb1{s}_{ci}_{g8}")
                        t6 = p.on("vector", lambda e, g8o=g8o, psS_=psS_: e.tensor_tensor(
                            hin[:, g8o:g8o + 512], tg2[:, :], psS_[:, :], ALU.add),
                            after=[t5], tag=f"sb2{s}_{ci}_{g8}")
                        s_war = t6
                ps_a.war[0] = comb
                ps_a.war[1] = e_war
                ps_b.war[0] = y1_war
                ps_b.war[1] = y2_war
                ps_s.war[0] = s_war

            sz_toks = []
            zw, zsl = wblk_load(100 * g, ipTd[:, 0:512], f"zw{s}_{g}_0")
            for fg in range(4):
                wt_, wsl_ = zw, zsl
                if fg + 1 < 4:
                    zw, zsl = wblk_load(100 * g + fg + 1,
                                        ipTd[:, (fg + 1) * 512:(fg + 2) * 512],
                                        f"zw{s}_{g}_{fg+1}")
                for t4 in range(NC4):
                    ps, war, b = ps_a.get()
                    last = None
                    for k in range(KC):
                        aft = [wt_] + xn_toks + ychk_toks + ([war] if war and k == 0 else [])
                        last = p.on("tensor", lambda e, k=k, t4=t4, g0=g0, ps=ps,
                                    wsl_=wsl_: e.matmul(
                            ps[:, :], xn_fm[k][:, g0 + t4 * 128:g0 + (t4 + 1) * 128],
                            wblk[:, wsl_ * 4096 + k * 512:wsl_ * 4096 + (k + 1) * 512],
                            start=(k == 0), stop=(k == KC - 1)),
                            after=(aft if k == 0 else []), inc=(1 if k == KC - 1 else 0),
                            tag=f"z{s}_{g}_{fg}_{t4}")
                    ev = p.on("scalar", lambda e, t4=t4, fg=fg, ps=ps: e.activation(
                        convg[:, t4 * DI + fg * 512:t4 * DI + (fg + 1) * 512],
                        ps[:, :], AF.Silu), after=[last], tag=f"sz{s}_{g}_{fg}_{t4}")
                    ps_a.consumed(b, ev)
                    sz_toks.append(ev)
                wblk_war[wsl_] = [sz_toks[-1]]

            yg_toks = []
            for t4 in range(NC4):
                t1 = p.on("vector", lambda e, t4=t4: e.tensor_mul(
                    xbcg[:, t4 * DI:(t4 + 1) * DI], xbcg[:, t4 * DI:(t4 + 1) * DI],
                    convg[:, t4 * DI:(t4 + 1) * DI]), after=sz_toks + ychk_toks,
                    tag=f"gt1{s}_{g}_{t4}")
                t2 = p.on("scalar", lambda e, t4=t4: e.activation(
                    st.dump2[:, :DI], xbcg[:, t4 * DI:(t4 + 1) * DI], AF.Square,
                    accum_out=gst[:, t4:t4 + 1]), after=[t1], tag=f"gt2{s}_{g}_{t4}")
                t3 = p.on("vector", lambda e, t4=t4: e.tensor_scalar(
                    gst[:, t4:t4 + 1], gst[:, t4:t4 + 1], 1.0 / DI, None,
                    ALU.mult), after=[t2])
                t4_ = p.on("scalar", lambda e, t4=t4: e.activation(
                    gst[:, t4:t4 + 1], gst[:, t4:t4 + 1], AF.Sqrt,
                    bias=st.epscol[:, 0:1]), after=[t3])
                t5 = p.on("vector", lambda e, t4=t4: e.reciprocal(
                    gst[:, t4:t4 + 1], gst[:, t4:t4 + 1]), after=[t4_])
                t6 = p.on("vector", lambda e, t4=t4: e.tensor_scalar(
                    xbcg[:, t4 * DI:(t4 + 1) * DI], xbcg[:, t4 * DI:(t4 + 1) * DI],
                    gst[:, t4:t4 + 1], None, ALU.mult), after=[t5])
                t7 = p.on("vector", lambda e, t4=t4: e.tensor_mul(
                    xbcg[:, t4 * DI:(t4 + 1) * DI], xbcg[:, t4 * DI:(t4 + 1) * DI],
                    gnwt[:, :]), after=[t6, initg_done], tag=f"yg{s}_{g}_{t4}")
                yg_toks.append(t7)

            ygf_done = []
            for fk in range(16):
                for t4 in range(NC4):
                    pstile, war, b = ps_tr.get()
                    tt = p.on("tensor", lambda e, fk=fk, t4=t4, pstile=pstile: e.transpose(
                        pstile[:, 0:128], xbcg[:, t4 * DI + fk * 128:t4 * DI + (fk + 1) * 128],
                        st.idbf[:, :]), after=yg_toks + ([war] if war else []),
                        tag=f"ygt{s}_{g}_{fk}_{t4}")
                    tv = p.on("vector", lambda e, fk=fk, t4=t4, pstile=pstile: e.tensor_copy(
                        ygfm[:, fk * GT + t4 * 128:fk * GT + (t4 + 1) * 128],
                        pstile[:, 0:128]), after=[tt], tag=f"ygte{s}_{g}_{fk}_{t4}")
                    ps_tr.consumed(b, tv)
                    ygf_done.append(tv)
            ygrp_war = ygf_done[-1]

            def part_mms(src_fm_done, row0, add_x, tag):
                evs = []
                for half in range(2):
                    pw, pwsl = wblk_load(1000 + row0 + half, fTd[row0:row0 + 512, :],
                                         f"fw{tag}_{half}_0")
                    psg = [ps_a.tiles[0], ps_a.tiles[1], ps_b.tiles[0], ps_b.tiles[1]]
                    wars = [ps_a.war[0], ps_a.war[1], ps_b.war[0], ps_b.war[1]]
                    for blk in range(4):
                        wt_, wsl_ = pw, pwsl
                        if blk + 1 < 4:
                            pw, pwsl = wblk_load(1000 + row0 + half + blk + 1,
                                                 fTd[row0 + (blk + 1) * 512:
                                                     row0 + (blk + 2) * 512, :],
                                                 f"fw{tag}_{half}_{blk+1}")
                        for ti in range(2):
                            t4 = half * 2 + ti
                            for dg in range(2):
                                pj = ti * 2 + dg
                                for fk in range(4):
                                    fkg = blk * 4 + fk
                                    aft = [wt_] + src_fm_done
                                    if blk == 0 and fk == 0 and wars[pj]:
                                        aft = aft + [wars[pj]]
                                    p.on("tensor", lambda e, fk=fk, fkg=fkg, t4=t4,
                                         dg=dg, pj=pj, wsl_=wsl_, psg=psg, blk=blk: e.matmul(
                                        psg[pj][:, :],
                                        ygfm[:, fkg * GT + t4 * 128:fkg * GT + (t4 + 1) * 128],
                                        wblk[:, wsl_ * 4096 + fk * 1024 + dg * 512:
                                             wsl_ * 4096 + fk * 1024 + (dg + 1) * 512],
                                        start=(blk == 0 and fk == 0),
                                        stop=(blk == 3 and fk == 3)),
                                        inc=(1 if fk == 3 else 0),
                                        after=aft, tag=f"p{tag}_{half}_{blk}_{t4}_{dg}")
                        wblk_war[wsl_] = [p.barrier_token("tensor")]
                    for ti in range(2):
                        t4 = half * 2 + ti
                        sl = t4 % 2
                        if add_x:
                            tx = p.on("sync", lambda e, s=s, g=g, t4=t4, sl=sl: e.dma_start(
                                out=xt[sl][:, :],
                                in_=Xd[(s * NTS + g * NC4 + t4) * 128:
                                       (s * NTS + g * NC4 + t4 + 1) * 128, :]),
                                after=([xt_war[sl]] if xt_war[sl] else []),
                                dma_chan=f"xt{sl}", tag=f"lxo{tag}_{t4}")
                        for dg in range(2):
                            pj = ti * 2 + dg
                            tok = (
                                "tensor", p.count["tensor"], False)
                            if add_x:
                                gi = g * NC4 + t4
                                ev0 = p.on("vector", lambda e, t4=t4, dg=dg, sl=sl,
                                           gi=gi: e.tensor_scalar(
                                    st.dump[:, dg * 512:(dg + 1) * 512],
                                    xt[sl][:, dg * 512:(dg + 1) * 512],
                                    st.mcol[:, gi:gi + 1], st.rcol[:, gi:gi + 1],
                                    ALU.subtract, ALU.mult),
                                    after=[tx], tag=f"evu{tag}_{t4}_{dg}")
                                ev = p.on("vector", lambda e, t4=t4, dg=dg, pj=pj,
                                          sl=sl, psg=psg: e.tensor_add(
                                    out_acc[:, t4 * D + dg * 512:t4 * D + (dg + 1) * 512],
                                    psg[pj][:, :], xt[sl][:, dg * 512:(dg + 1) * 512]),
                                    after=[tok, tx] + oa_war, tag=f"ev{tag}_{t4}_{dg}")
                                ev = p.on("vector", lambda e, t4=t4, dg=dg: e.tensor_add(
                                    out_acc[:, t4 * D + dg * 512:t4 * D + (dg + 1) * 512],
                                    out_acc[:, t4 * D + dg * 512:t4 * D + (dg + 1) * 512],
                                    st.dump[:, dg * 512:(dg + 1) * 512]),
                                    after=[ev0], tag=f"evv{tag}_{t4}_{dg}")
                                xt_war[sl] = ev
                            else:
                                ev = p.on("vector", lambda e, t4=t4, dg=dg, pj=pj,
                                          psg=psg: e.tensor_add(
                                    out_acc[:, t4 * D + dg * 512:t4 * D + (dg + 1) * 512],
                                    out_acc[:, t4 * D + dg * 512:t4 * D + (dg + 1) * 512],
                                    psg[pj][:, :]), after=[tok], tag=f"ev{tag}_{t4}_{dg}")
                            wars[pj] = ev
                            evs.append(ev)
                    ps_a.war[0], ps_a.war[1] = wars[0], wars[1]
                    ps_b.war[0], ps_b.war[1] = wars[2], wars[3]
                return evs

            evA = part_mms(ygf_done, 0, True, f"A{s}_{g}")

            def fm_proj(wsrc, tag):
                toks = []
                lw2, lsl2 = lblk_load(500, wsrc[:, 0:128], f"{tag}_0")
                for f in range(16):
                    wt_, wsl_ = lw2, lsl2
                    if f + 1 < 16:
                        lw2, lsl2 = lblk_load(500 + f + 1,
                                              wsrc[:, (f + 1) * 128:(f + 2) * 128],
                                              f"{tag}_{f+1}")
                    ps, war, b = ps_s.get()
                    last = None
                    for k in range(KC):
                        aft = [wt_] + xn_toks + ([war] if war and k == 0 else [])
                        last = p.on("tensor", lambda e, k=k, f=f, g0=g0, ps=ps,
                                    wsl_=wsl_: e.matmul(
                            ps[:, :], lblk[:, wsl_ * 1024 + k * 128:wsl_ * 1024 + (k + 1) * 128],
                            xn_fm[k][:, g0:g0 + GT], start=(k == 0), stop=(k == KC - 1)),
                            after=(aft if k == 0 else []), inc=(1 if k == KC - 1 else 0),
                            tag=f"{tag}m{f}_{k}")
                    lblk_war[wsl_] = [last]
                    ev = p.on("vector", lambda e, f=f, ps=ps: e.tensor_copy(
                        ygfm[:, f * GT:(f + 1) * GT], ps[:, :]), after=[last],
                        tag=f"{tag}e{f}")
                    ps_s.consumed(b, ev)
                    toks.append(ev)
                return toks

            y0_done = fm_proj(zwTd, f"y0_{s}_{g}")
            evB = part_mms(y0_done, DI, False, f"B{s}_{g}")
            y1_done = fm_proj(owTd, f"y1_{s}_{g}")

            for t4 in range(NC4):
                pq = [ps_a.tiles[0], ps_a.tiles[1]]
                pqw = [ps_a.war[0], ps_a.war[1]]
                lastq = [None, None]
                for blk in range(4):
                    pw, pwsl = wblk_load(2000 + t4 * 8 + blk,
                                         fTd[2 * DI + blk * 512:2 * DI + (blk + 1) * 512, :],
                                         f"fc{s}_{g}_{t4}_{blk}")
                    for dg in range(2):
                        for fk in range(4):
                            fkg = blk * 4 + fk
                            aft = [pw] + y1_done
                            if blk == 0 and fk == 0 and pqw[dg]:
                                aft = aft + [pqw[dg]]
                            lastq[dg] = p.on("tensor", lambda e, fk=fk, fkg=fkg, t4=t4,
                                             dg=dg, pwsl=pwsl, pq=pq, blk=blk: e.matmul(
                                pq[dg][:, :],
                                ygfm[:, fkg * GT + t4 * 128:fkg * GT + (t4 + 1) * 128],
                                wblk[:, pwsl * 4096 + fk * 1024 + dg * 512:
                                     pwsl * 4096 + fk * 1024 + (dg + 1) * 512],
                                start=(blk == 0 and fk == 0),
                                stop=(blk == 3 and fk == 3)),
                                inc=(1 if fk == 3 else 0),
                                after=aft, tag=f"q{s}_{g}_{t4}_{blk}_{dg}")
                    wblk_war[pwsl] = [lastq[1]]
                tq = [None, None]
                for dg in range(2):
                    tq[dg] = p.on("vector", lambda e, dg=dg, pq=pq: e.tensor_copy(
                        qtile[:, dg * 512:(dg + 1) * 512], pq[dg][:, :]),
                        after=[lastq[dg]], tag=f"qe{s}_{g}_{t4}_{dg}")
                    ps_a.war[dg] = tq[dg]
                for dg in range(2):
                    ps, war, b = ps_b.get()
                    tcs = p.on("tensor", lambda e, dg=dg, ps=ps: e.matmul(
                        ps[:, :], LTi[:, :], qtile[:, dg * 512:(dg + 1) * 512],
                        start=True, stop=False),
                        after=[tq[dg], init_done, tcar] + ([war] if war else []),
                        tag=f"cs{s}_{g}_{t4}_{dg}")
                    tcs = p.on("tensor", lambda e, dg=dg, ps=ps: e.matmul(
                        ps[:, :], onerf[0:1, :], carry[0:1, dg * 512:(dg + 1) * 512],
                        start=False, stop=True), after=carry_tok, inc=1,
                        tag=f"cc{s}_{g}_{t4}_{dg}")
                    ev = p.on("vector", lambda e, t4=t4, dg=dg, ps=ps: e.tensor_add(
                        out_acc[:, t4 * D + dg * 512:t4 * D + (dg + 1) * 512],
                        out_acc[:, t4 * D + dg * 512:t4 * D + (dg + 1) * 512],
                        ps[:, :]), after=[tcs], tag=f"ce{s}_{g}_{t4}_{dg}")
                    ps_b.consumed(b, ev)
                carry_new = []
                for dg in range(2):
                    ps, war, b = ps_b.get()
                    tcl = p.on("tensor", lambda e, dg=dg, ps=ps: e.matmul(
                        ps[0:1, :], ALO[:, 0:1], qtile[:, dg * 512:(dg + 1) * 512],
                        start=True, stop=True),
                        after=[tq[dg]] + ([war] if war else []), inc=1,
                        tag=f"cl{s}_{g}_{t4}_{dg}")
                    tc2 = p.on("vector", lambda e, dg=dg, ps=ps: e.tensor_add(
                        carry[0:1, dg * 512:(dg + 1) * 512],
                        carry[0:1, dg * 512:(dg + 1) * 512], ps[0:1, :]),
                        after=[tcl], tag=f"cu{s}_{g}_{t4}_{dg}")
                    ps_b.consumed(b, tc2)
                    carry_new.append(tc2)
                carry_tok = carry_new

            st_toks = []
            bar = p.barrier_token("vector")
            for t4 in range(NC4):
                ts_ = p.on("sync", lambda e, s=s, g=g, t4=t4: e.dma_start(
                    out=XOd[(s * NTS + g * NC4 + t4) * 128:
                            (s * NTS + g * NC4 + t4 + 1) * 128, :],
                    in_=out_acc[:, t4 * D:(t4 + 1) * D]),
                    after=[bar], dma_chan=f"st{t4 % 2}", tag=f"so{s}_{g}_{t4}")
                st_toks.append(ts_)
            oa_war = st_toks

    return p


def _mk_wrap(nc, ctx, emit_fn, wnames, wshapes, T, **kw):
    Xd = nc.dram_tensor("X", [T, D], F32, kind="ExternalInput").ap()
    XOd = nc.dram_tensor("XO", [T, D], F32, kind="ExternalOutput").ap()
    wd = {}
    for nm in wnames:
        shape, dt_ = wshapes[nm]
        wd[nm] = nc.dram_tensor(nm, shape, dt_, kind="ExternalInput").ap()
    p = Prog(nc)
    emit_fn(p, nc, ctx, Xd, XOd, wd, T=T, **kw)
    p.finalize()
    p.make_sems()
    return p


def l1ffn_shapes():
    return {"rT": ([D, 8], F32), "gT": ([8, D, 2048], F32), "uT": ([8, D, 2048], F32),
            "dT": ([8, 2048, D], F32), "idn": ([128, 128], BF16)}


def l1attn_shapes():
    return {"Wq": ([D, D], F32), "Wk": ([D, D], F32), "Wv": ([D, D], F32),
            "Wo": ([D, D], F32), "W1": ([D, 4096], F32), "W2": ([4096, D], F32),
            "bqc": ([128, 8], F32), "bkr": ([1, D], F32), "bvr": ([1, D], F32),
            "bor": ([1, D], F32), "b1c": ([128, 32], F32), "b2t": ([128, D], F32),
            "g1t": ([128, D], F32), "be1t": ([128, D], F32), "g2t": ([128, D], F32),
            "be2t": ([128, D], F32), "idn": ([128, 128], BF16)}


def l0ffn_shapes():
    return {"rT": ([D, 8], F32), "br": ([1, 8], F32), "idn": ([128, 128], BF16),
            "idf": ([128, 128], F32), "W1": ([8, D, D], F32), "W2": ([8, D, D], F32),
            "b1c": ([128, 64], F32), "b2": ([8, D], F32)}


def l0attn_shapes():
    return {"nwt": ([128, D], F32), "ipT": ([D, 4384], F32),
            "convw": ([128, 72], F32), "convb": ([128, 18], F32),
            "dtb": ([1, 32], F32), "alog": ([1, 32], F32), "dvec": ([1, 32], F32),
            "gnwt": ([128, 2048], F32), "zwT": ([D, 2048], F32),
            "owT": ([D, 2048], F32), "fT": ([6144, D], F32),
            "idn": ([128, 128], BF16), "idf": ([128, 128], F32),
            "LTi": ([128, 128], F32), "UTs": ([128, 128], F32),
            "ALO": ([128, 128], F32), "SDG": ([128, 128], F32),
            "selc": ([32, 4096], F32)}


def build_l1ffn(nc, ctx, T=4096, TG=1024, debug=False):
    return _mk_wrap(nc, ctx, emit_l1ffn, L1FFN_W, l1ffn_shapes(), T, TG=TG,
                    debug=debug)


def build_l1attn(nc, ctx, T=4096, S=2048, stop_after=None):
    return _mk_wrap(nc, ctx, emit_l1attn, L1ATTN_W, l1attn_shapes(), T, S=S,
                    stop_after=stop_after)


def build_l0attn(nc, ctx, T=4096, S=2048):
    return _mk_wrap(nc, ctx, emit_l0attn, L0ATTN_W, l0attn_shapes(), T, S=S)


def build_l0ffn(nc, ctx, T=4096, TG=1024, mode="full", C=4096):
    Xd = nc.dram_tensor("X", [T, D], F32, kind="ExternalInput").ap()
    wd = {}
    for nm, (shape, dt_) in l0ffn_shapes().items():
        if mode == "count" and nm in ("W1", "W2", "b1c", "b2"):
            continue
        wd[nm] = nc.dram_tensor(nm, shape, dt_, kind="ExternalInput").ap()
    p = Prog(nc)
    if mode == "count":
        CNTd = nc.dram_tensor("CNT", [8, 1], F32, kind="ExternalOutput").ap()
        emit_l0ffn(p, nc, ctx, Xd, None, wd, T=T, TG=TG, mode="count", C=C,
                   CNTd=CNTd)
    else:
        XOd = nc.dram_tensor("XO", [T, D], F32, kind="ExternalOutput").ap()
        cntd = nc.dram_tensor("cnt", [8, 8], F32, kind="ExternalInput").ap()
        prefd = nc.dram_tensor("pref", [8, 1], F32, kind="ExternalInput").ap()
        emit_l0ffn(p, nc, ctx, Xd, XOd, wd, T=T, TG=TG, mode="full", C=C,
                   cntd=cntd, prefd=prefd)
    p.finalize()
    p.make_sems()
    return p


from concourse.bass_core_rust import IndirectOffsetOnAxis

L1FFNS_W = ["rT", "gT", "uT", "dT", "idn", "idf", "ecap"]


def l1ffns_shapes():
    d = l1ffn_shapes()
    d["idf"] = ([128, 128], F32)
    d["ecap"] = ([128, 8], F32)
    return d


def emit_l1ffn_sparse(p, nc, ctx, Xd, XOd, wd, Udd, Uedd, EOdd, T=4096, CAP=1280):
    E, FF = 8, 2048
    KC = D // 128
    NT = T // 128
    NCT = CAP // 128
    TGS = []
    c = CAP
    while c > 0:
        TGS.append(min(512, c))
        c -= TGS[-1]
    rTd, gTd, uTd, dTd, idn = (wd[k] for k in L1FFN_W)

    st = St()
    st.idbf = alloc(ctx, nc, "idbf", [128, 128], BF16)
    st.dump = alloc(ctx, nc, "dump", [128, D], F32)
    st.dump2 = alloc(ctx, nc, "dump2", [128, D], F32)
    st.epscol = alloc(ctx, nc, "epscol", [128, 1], F32)
    for nm in ("s1", "s2", "mcol", "vcol", "rcol"):
        setattr(st, nm, alloc(ctx, nc, nm, [128, NT], F32))
    idf = alloc(ctx, nc, "idf_sb", [128, 128], F32)
    xt = [alloc(ctx, nc, f"xt{i}", [128, D], F32) for i in range(2)]
    u_tm = [alloc(ctx, nc, f"utm{i}", [128, D], BF16) for i in range(2)]
    rbf = alloc(ctx, nc, "rbf", [128, KC * E], BF16)
    ecap = alloc(ctx, nc, "ecap", [128, E], F32)
    oh1 = alloc(ctx, nc, "oh1b", [128, NT * E], F32)
    oh2 = alloc(ctx, nc, "oh2b", [128, NT * E], F32)
    w1c = alloc(ctx, nc, "w1c", [128, NT], F32)
    w2c = alloc(ctx, nc, "w2c", [128, NT], F32)
    o1c = alloc(ctx, nc, "o1c", [128, NT], I32)
    o2c = alloc(ctx, nc, "o2c", [128, NT], I32)
    zer8 = alloc(ctx, nc, "zer8", [128, E], F32)
    zcol8 = alloc(ctx, nc, "zcol8", [8, 1], F32)
    prob = alloc(ctx, nc, "prob", [128, E], F32)
    eq = alloc(ctx, nc, "eq", [128, E], F32)
    cum = alloc(ctx, nc, "cum", [128, E], F32)
    c1 = alloc(ctx, nc, "c1", [128, 8], F32)
    eo_tm = alloc(ctx, nc, "eo_tm", [128, D], BF16)
    gth = [alloc(ctx, nc, f"gth{i}", [128, D], BF16) for i in range(2)]
    wslot = [alloc(ctx, nc, f"w{i}", [128, 16384], BF16) for i in range(2)]
    ctx2 = contextlib.ExitStack()
    ufmt = alloc(ctx2, nc, "ufmt", [128, D], BF16)
    ohc_fm = alloc(ctx2, nc, "ohc_fm", [8, NT * 128], F32)
    pos_fm = alloc(ctx2, nc, "pos_fm", [8, NT * 128], F32)

    ps_tr = PsumRing(ctx, nc, "pstr", 2, 512, BF16)
    ps_of = PsumRing(ctx, nc, "psof", 1, 128, F32)
    ps_r = PsumRing(ctx, nc, "psr", 1, E, F32)
    ps_a = PsumRing(ctx, nc, "psa", 2, 512, F32)
    ps_b = PsumRing(ctx, nc, "psb", 2, 512, F32)

    p.on("sync", lambda e: e.dma_start(out=st.idbf[:, :], in_=idn), dma_chan="init")
    p.on("sync", lambda e: e.dma_start(out=idf[:, :], in_=wd["idf"]), dma_chan="init")
    p.on("sync", lambda e: e.dma_start(out=ecap[:, :], in_=wd["ecap"]), dma_chan="init")
    p.on("gpsimd", lambda e: e.dma_start(
        out=rbf[:, :].rearrange("p (k e) -> p k e", k=KC),
        in_=rTd.rearrange("(k p) e -> p k e", p=128)), dma_chan="initg")
    init_done = p.chan_total("init")
    initg_done = p.chan_total("initg")
    p.on("vector", lambda e: e.memset(st.epscol[:, :], 1e-5))
    tz = p.on("vector", lambda e: e.memset(zer8[:, :], 0.0))
    p.on("vector", lambda e: e.memset(zcol8[:, :], 0.0))
    tze = p.on("vector", lambda e: e.memset(eo_tm[:, :], 0.0))
    tzr = p.on("sync", lambda e: e.dma_start(
        out=EOdd[E * CAP:E * CAP + 1, :], in_=eo_tm[0:1, :]), after=[tze],
        dma_chan="ez")
    for r in range(E * CAP // 128):
        p.on("sync", lambda e, r=r: e.dma_start(
            out=Uedd[r * 128:(r + 1) * 128, :], in_=eo_tm[:, :]), after=[tze],
            dma_chan="ez")
    p.on("sync", lambda e: e.dma_start(
        out=Uedd[E * CAP:E * CAP + 1, :], in_=eo_tm[0:1, :]), after=[tze],
        dma_chan="ez")
    ez_all = p.chan_total("ez")

    xt_war = [None, None]
    ud_war = [None, None]
    prev_rt = None
    rt_done = []
    last_tts = None
    for i in range(NT):
        sl = i % 2
        tx = p.on("sync", lambda e, i=i, sl=sl: e.dma_start(
            out=xt[sl][:, :], in_=Xd[i * 128:(i + 1) * 128, :]),
            after=([xt_war[sl]] if xt_war[sl] else []), dma_chan=f"xt{sl}",
            tag=f"lx{i}")
        if ud_war[sl]:
            p.wait_all("vector", [ud_war[sl]], tag=f"uw{i}")
        st.ln_after = [tx]
        tu = emit_ln_tile(p, st, xt[sl][:, :], u_tm[sl][:, :], i, tag=f"ln{i}")
        xt_war[sl] = tu
        ud_war[sl] = p.on("sync", lambda e, i=i, sl=sl: e.dma_start(
            out=Udd[i * 128:(i + 1) * 128, :], in_=u_tm[sl][:, :]), after=[tu],
            dma_chan=f"ud{sl}", tag=f"sud{i}")
        tts = emit_transpose(p, st, lambda _i, sl=sl: u_tm[sl][:, :],
                             [ufmt[:, k * 128:(k + 1) * 128] for k in range(KC)],
                             1, KC, [tu, init_done], ps_tr, tag=f"tr{i}")
        last_tts = tts[-1]
        ps, war, b = ps_r.get()
        last = None
        for k in range(KC):
            aft = tts + [initg_done] + ([war] if war and k == 0 else [])
            last = p.on("tensor", lambda e, k=k, ps=ps: e.matmul(
                ps[:, :], ufmt[:, k * 128:(k + 1) * 128], rbf[:, k * E:(k + 1) * E],
                start=(k == 0), stop=(k == KC - 1)),
                after=(aft if k == 0 else []), inc=(1 if k == KC - 1 else 0),
                tag=f"rt{i}")
        tm_ = p.on("vector", lambda e, ps=ps: e.reduce_max(c1[:, 0:1], ps[:, :],
                                                           AX.X, negate=True),
                   after=[last])
        te = p.on("scalar", lambda e, ps=ps: e.activation(
            prob[:, :], ps[:, :], AF.Exp, bias=c1[:, 0:1]),
            after=[tm_] + ([prev_rt] if prev_rt else []))
        ps_r.consumed(b, te)
        p.on("vector", lambda e: e.reduce_max(c1[:, 1:2], prob[:, :], AX.X),
             after=[te])
        p.on("vector", lambda e: e.tensor_scalar(eq[:, :], prob[:, :], c1[:, 1:2],
                                                 None, ALU.is_ge))
        p.on("vector", lambda e: e.tensor_tensor_scan(cum[:, :], eq[:, :],
                                                      zer8[:, :], 0.0, ALU.add,
                                                      ALU.add), after=[tz])
        p.on("vector", lambda e: e.tensor_scalar(cum[:, :], cum[:, :], 1.0, None,
                                                 ALU.is_equal))
        p.on("vector", lambda e, i=i: e.tensor_mul(oh1[:, i * E:(i + 1) * E],
                                                   eq[:, :], cum[:, :]))
        p.on("vector", lambda e, i=i: e.tensor_scalar(
            cum[:, :], oh1[:, i * E:(i + 1) * E], -1.0, 1.0, ALU.mult, ALU.add))
        p.on("vector", lambda e: e.tensor_mul(eq[:, :], prob[:, :], cum[:, :]))
        p.on("vector", lambda e: e.reduce_max(c1[:, 2:3], eq[:, :], AX.X))
        p.on("vector", lambda e: e.tensor_scalar(cum[:, :], eq[:, :], c1[:, 2:3],
                                                 None, ALU.is_ge))
        p.on("vector", lambda e: e.tensor_tensor_scan(eq[:, :], cum[:, :],
                                                      zer8[:, :], 0.0, ALU.add,
                                                      ALU.add))
        p.on("vector", lambda e: e.tensor_scalar(eq[:, :], eq[:, :], 1.0, None,
                                                 ALU.is_equal))
        p.on("vector", lambda e, i=i: e.tensor_mul(oh2[:, i * E:(i + 1) * E],
                                                   cum[:, :], eq[:, :]))
        p.on("vector", lambda e: e.tensor_add(c1[:, 3:4], c1[:, 1:2], c1[:, 2:3]))
        p.on("vector", lambda e: e.reciprocal(c1[:, 4:5], c1[:, 3:4]))
        p.on("vector", lambda e, i=i: e.tensor_tensor(
            w1c[:, i:i + 1], c1[:, 1:2], c1[:, 4:5], ALU.mult))
        tq = p.on("vector", lambda e, i=i: e.tensor_tensor(
            w2c[:, i:i + 1], c1[:, 2:3], c1[:, 4:5], ALU.mult), tag=f"we{i}")
        prev_rt = tq
        tsum = p.on("vector", lambda e, i=i: e.tensor_add(
            cum[:, :], oh1[:, i * E:(i + 1) * E], oh2[:, i * E:(i + 1) * E]))
        pso_, waro, bo_ = ps_of.get()
        tt = p.on("tensor", lambda e, pso_=pso_: e.matmul(
            pso_[0:8, 0:128], cum[:, :], idf[:, :], start=True, stop=True),
            after=[tsum, init_done] + ([waro] if waro else []), tag=f"ot{i}")
        tv = p.on("vector", lambda e, i=i, pso_=pso_: e.tensor_copy(
            ohc_fm[:, i * 128:(i + 1) * 128], pso_[0:8, 0:128]), after=[tt])
        ps_of.consumed(bo_, tv)
        rt_done.append(tq)
        rt_done.append(tv)
    tpos = p.on("vector", lambda e: e.tensor_tensor_scan(
        pos_fm[:, :], ohc_fm[:, :], zcol8[:, 0:1].broadcast_to([8, NT * 128]),
        0.0, ALU.add, ALU.add), after=rt_done, tag="scan")
    sc_done = []
    for i in range(NT):
        pso_, waro, bo_ = ps_of.get()
        tt = p.on("tensor", lambda e, i=i, pso_=pso_: e.matmul(
            pso_[:, 0:8], pos_fm[:, i * 128:(i + 1) * 128], idf[0:8, 0:8],
            start=True, stop=True),
            after=[tpos] + ([waro] if waro else []), tag=f"pt{i}")
        tlast = None
        for (ohb, oc, sl) in ((oh1, o1c, 0), (oh2, o2c, 1)):
            p.on("vector", lambda e, i=i, ohb=ohb, pso_=pso_: e.tensor_mul(
                eq[:, :], ohb[:, i * E:(i + 1) * E], pso_[:, 0:8]),
                after=[tt])
            p.on("vector", lambda e: e.reduce_sum(c1[:, 5:6], eq[:, :], AX.X))
            p.on("vector", lambda e, i=i, ohb=ohb: e.tensor_mul(
                eq[:, :], ohb[:, i * E:(i + 1) * E], ecap[:, :]),
                after=[init_done])
            p.on("vector", lambda e: e.reduce_sum(c1[:, 6:7], eq[:, :], AX.X))
            p.on("vector", lambda e: e.tensor_add(c1[:, 7:8], c1[:, 5:6],
                                                  c1[:, 6:7]))
            p.on("vector", lambda e: e.tensor_scalar(c1[:, 0:1], c1[:, 5:6],
                                                     CAP + 0.5, None, ALU.is_ge))
            p.on("vector", lambda e: e.tensor_scalar(c1[:, 1:2], c1[:, 7:8],
                                                     -1.0, float(E * CAP),
                                                     ALU.mult, ALU.add))
            p.on("vector", lambda e: e.scalar_tensor_tensor(
                c1[:, 7:8], c1[:, 1:2], c1[:, 0:1], c1[:, 7:8], ALU.mult, ALU.add))
            tlast = p.on("vector", lambda e, i=i, oc=oc: e.tensor_copy(
                oc[:, i:i + 1], c1[:, 7:8]), tag=f"oc{i}_{sl}")
            sc_done.append(tlast)
        ps_of.consumed(bo_, tlast)
    utm_war = [[last_tts], [last_tts]]
    ud_all = [p.chan_total("ud0"), p.chan_total("ud1")]
    scat_last = []
    for i in range(NT):
        sl = i % 2
        tl = p.on("sync", lambda e, i=i, sl=sl: e.dma_start(
            out=u_tm[sl][:, :], in_=Udd[i * 128:(i + 1) * 128, :]),
            after=utm_war[sl] + ud_all + sc_done, dma_chan=f"xu{sl}",
            tag=f"lu{i}")
        s1_ = p.on("gpsimd", lambda e, i=i, sl=sl: e.indirect_dma_start(
            Uedd, IndirectOffsetOnAxis(ap=o1c[:, i:i + 1], axis=0),
            u_tm[sl][:, :], None), after=[tl, ez_all], dma_chan="sc0", tag=f"sca{i}")
        s2_ = p.on("gpsimd", lambda e, i=i, sl=sl: e.indirect_dma_start(
            Uedd, IndirectOffsetOnAxis(ap=o2c[:, i:i + 1], axis=0),
            u_tm[sl][:, :], None), after=[tl, s1_], dma_chan="sc0", tag=f"scb{i}")
        utm_war[sl] = [s1_, s2_]
        scat_last = [s1_, s2_]
    ctx2.close()
    ue_tm = alloc(ctx, nc, "ue_tm", [128, NCT * D], BF16)
    ue_fm = [alloc(ctx, nc, f"uefm{k}", [128, CAP], BF16) for k in range(KC)]
    h_fm = [alloc(ctx, nc, f"hfm{f}", [128, CAP], BF16) for f in range(16)]
    scat_all = [p.chan_total("sc0")]
    wwar = {0: [], 1: []}

    def wload(slot, e_, kind, tg):
        src = {"g": gTd, "u": uTd, "d": dTd}[kind][e_]
        rows, cols = src.shape
        ch = rows // 128
        return p.on("gpsimd", lambda e, slot=slot, src=src, ch=ch, cols=cols:
                    e.dma_start(
            out=wslot[slot][:, :ch * cols].rearrange("p (c n) -> p c n", c=ch),
            in_=src.rearrange("(c p) n -> p c n", p=128)), dma_chan=f"w{slot}",
            after=wwar[slot], tag=tg)

    eo_last = []
    eo_war = [tzr]
    phases = [(e_, m) for e_ in range(E) for m in ("g", "u", "d")]
    wtok = {}
    uef_done = []
    uef_prev = None
    h_g = []
    h_done = []
    for pi, (e_, mat) in enumerate(phases):
        if pi == 0:
            wtok[(e_, mat)] = wload(0, e_, mat, f"wl{e_}{mat}")
        if pi + 1 < len(phases):
            en, mn = phases[pi + 1]
            wtok[(en, mn)] = wload((pi + 1) % 2, en, mn, f"wl{en}{mn}")
        slot = pi % 2
        w = wslot[slot]
        wt = wtok[(e_, mat)]
        cons = []
        if mat == "g":
            lds = []
            for j in range(NCT):
                aft = scat_all if e_ == 0 else ([uef_prev] if j == 0 else [])
                tl = p.on("gpsimd", lambda e, e_=e_, j=j: e.dma_start(
                    out=ue_tm[:, j * D:(j + 1) * D],
                    in_=Uedd[e_ * CAP + j * 128:e_ * CAP + (j + 1) * 128, :]),
                    after=aft, dma_chan="ue", tag=f"ue{e_}_{j}")
                lds.append(tl)
            uef_done = emit_transpose(p, st, lambda j: ue_tm[:, j * D:(j + 1) * D],
                                      ue_fm, NCT, KC, lds + [init_done], ps_tr,
                                      tag=f"uet{e_}")
            uef_prev = uef_done[-1]
            off = 0
            for tg_ in TGS:
                for f in range(16):
                    ps, war, b = ps_a.get()
                    last = None
                    for k in range(KC):
                        aft = [wt] + uef_done + ([war] if war and k == 0 else [])
                        last = p.on("tensor", lambda e, k=k, f=f, off=off,
                                    tg_=tg_, ps=ps, w=w: e.matmul(
                            ps[:, :tg_],
                            w[:, k * 2048 + f * 128:k * 2048 + (f + 1) * 128],
                            ue_fm[k][:, off:off + tg_],
                            start=(k == 0), stop=(k == KC - 1)),
                            after=(aft if k == 0 else []),
                            inc=(1 if k == KC - 1 else 0), tag=f"g{e_}_{f}_{off}")
                    tv = p.on("scalar", lambda e, f=f, off=off, tg_=tg_, ps=ps:
                              e.activation(h_fm[f][:, off:off + tg_], ps[:, :tg_],
                                           AF.Silu), after=[last],
                              tag=f"sl{e_}_{f}_{off}")
                    ps_a.consumed(b, tv)
                    cons.append(tv)
                off += tg_
            h_g = list(cons)
        elif mat == "u":
            off = 0
            for tg_ in TGS:
                for f in range(16):
                    ps, war, b = ps_a.get()
                    last = None
                    for k in range(KC):
                        aft = [wt] + uef_done + ([war] if war and k == 0 else [])
                        last = p.on("tensor", lambda e, k=k, f=f, off=off,
                                    tg_=tg_, ps=ps, w=w: e.matmul(
                            ps[:, :tg_],
                            w[:, k * 2048 + f * 128:k * 2048 + (f + 1) * 128],
                            ue_fm[k][:, off:off + tg_],
                            start=(k == 0), stop=(k == KC - 1)),
                            after=(aft if k == 0 else []),
                            inc=(1 if k == KC - 1 else 0), tag=f"u{e_}_{f}_{off}")
                    tv = p.on("vector", lambda e, f=f, off=off, tg_=tg_, ps=ps:
                              e.tensor_tensor(h_fm[f][:, off:off + tg_],
                                              h_fm[f][:, off:off + tg_],
                                              ps[:, :tg_], ALU.mult),
                              after=[last] + h_g, tag=f"h{e_}_{f}_{off}")
                    ps_a.consumed(b, tv)
                    cons.append(tv)
                off += tg_
            h_done = list(cons)
        else:
            for j in range(NCT):
                evs = []
                for dg in range(2):
                    ps, war, b = ps_b.get()
                    last = None
                    for fk in range(16):
                        aft = [wt] + h_done + ([war] if war and fk == 0 else [])
                        last = p.on("tensor", lambda e, fk=fk, j=j, dg=dg, ps=ps,
                                    w=w: e.matmul(
                            ps[:, :], h_fm[fk][:, j * 128:(j + 1) * 128],
                            w[:, fk * 1024 + dg * 512:fk * 1024 + (dg + 1) * 512],
                            start=(fk == 0), stop=(fk == 15)),
                            after=(aft if fk == 0 else []),
                            inc=(1 if fk == 15 else 0), tag=f"d{e_}_{j}_{dg}")
                    tv = p.on("vector", lambda e, dg=dg, ps=ps: e.tensor_copy(
                        eo_tm[:, dg * 512:(dg + 1) * 512], ps[:, :]),
                        after=[last] + eo_war, tag=f"eoe{e_}_{j}_{dg}")
                    ps_b.consumed(b, tv)
                    evs.append(tv)
                ts_ = p.on("gpsimd", lambda e, e_=e_, j=j: e.dma_start(
                    out=EOdd[e_ * CAP + j * 128:e_ * CAP + (j + 1) * 128, :],
                    in_=eo_tm[:, :]), after=evs, dma_chan="eo", tag=f"eow{e_}_{j}")
                eo_war = [ts_]
                cons.append(ts_)
            eo_last = [cons[-1]]
        wwar[slot] = cons
    eo_all = p.chan_total("eo")
    gth_war = [None, None]
    for i in range(NT):
        sl = i % 2
        tg1_ = p.on("gpsimd", lambda e, i=i: e.indirect_dma_start(
            gth[0][:, :], None, EOdd,
            IndirectOffsetOnAxis(ap=o1c[:, i:i + 1], axis=0)),
            after=[eo_all, tzr] + sc_done + ([gth_war[0]] if gth_war[0] else []),
            dma_chan="gt0", tag=f"g1{i}")
        tg2_ = p.on("gpsimd", lambda e, i=i: e.indirect_dma_start(
            gth[1][:, :], None, EOdd,
            IndirectOffsetOnAxis(ap=o2c[:, i:i + 1], axis=0)),
            after=[eo_all, tzr] + ([gth_war[1]] if gth_war[1] else []),
            dma_chan="gt1", tag=f"g2{i}")
        tx = p.on("sync", lambda e, i=i, sl=sl: e.dma_start(
            out=xt[sl][:, :], in_=Xd[i * 128:(i + 1) * 128, :]),
            after=([xt_war[sl]] if xt_war[sl] else []), dma_chan=f"xt{sl}",
            tag=f"lx3{i}")
        t1_ = p.on("vector", lambda e, i=i: e.tensor_scalar(
            st.dump[:, :], gth[0][:, :], w1c[:, i:i + 1], None, ALU.mult),
            after=[tg1_], tag=f"cmb1{i}")
        t2_ = p.on("vector", lambda e, i=i: e.scalar_tensor_tensor(
            st.dump[:, :], gth[1][:, :], w2c[:, i:i + 1], st.dump[:, :],
            ALU.mult, ALU.add), after=[tg2_], tag=f"cmb2{i}")
        gth_war = [t1_, t2_]
        t3_ = p.on("vector", lambda e, sl=sl: e.scalar_tensor_tensor(
            xt[sl][:, :], st.dump[:, :], 0.5, xt[sl][:, :], ALU.mult, ALU.add),
            after=[tx, t2_], tag=f"out{i}")
        ts_ = p.on("sync", lambda e, i=i, sl=sl: e.dma_start(
            out=XOd[i * 128:(i + 1) * 128, :], in_=xt[sl][:, :]), after=[t3_],
            dma_chan=f"st{sl}", tag=f"sx{i}")
        xt_war[sl] = ts_
    return p


def build_l1ffn_sparse(nc, ctx, T=4096, CAP=1280):
    Xd = nc.dram_tensor("X", [T, D], F32, kind="ExternalInput").ap()
    XOd = nc.dram_tensor("XO", [T, D], F32, kind="ExternalOutput").ap()
    wd = {}
    for nm, (shape, dt_) in l1ffns_shapes().items():
        wd[nm] = nc.dram_tensor(nm, shape, dt_, kind="ExternalInput").ap()
    Udd = nc.dram_tensor("Ud", [T, D], BF16).ap()
    Uedd = nc.dram_tensor("Ued", [8 * CAP + 1, D], BF16).ap()
    EOdd = nc.dram_tensor("EOd", [8 * CAP + 1, D], BF16).ap()
    p = Prog(nc)
    emit_l1ffn_sparse(p, nc, ctx, Xd, XOd, wd, Udd, Uedd, EOdd, T=T, CAP=CAP)
    p.finalize()
    p.make_sems()
    return p


def emit_l0ffn_sparse(p, nc, ctx, Xd, XOd, wd, Udd, Uedd, EOdd, T=4096, TG=1024,
                      C=4096, CL=768, cntd=None, prefd=None):
    E = 8
    KC = D // 128
    NT = TG // 128
    NG = T // TG
    NTT = T // 128
    NCT = CL // 128
    TGS = []
    c = CL
    while c > 0:
        TGS.append(min(512, c))
        c -= TGS[-1]
    rTd, brd, idnd, idfd = wd["rT"], wd["br"], wd["idn"], wd["idf"]
    W1d, W2d, b1cd, b2d = wd["W1"], wd["W2"], wd["b1c"], wd["b2"]

    st = St()
    st.idbf = alloc(ctx, nc, "idbf", [128, 128], BF16)
    idf = alloc(ctx, nc, "idf_sb", [128, 128], F32)
    st.dump = alloc(ctx, nc, "dump", [128, D], F32)
    st.dump2 = alloc(ctx, nc, "dump2", [128, D], F32)
    st.epscol = alloc(ctx, nc, "epscol", [128, 1], F32)
    for nm in ("s1", "s2", "mcol", "vcol", "rcol"):
        setattr(st, nm, alloc(ctx, nc, nm, [128, NT], F32))

    xt = [alloc(ctx, nc, f"xt{i}", [128, D], F32) for i in range(2)]
    u_tm = alloc(ctx, nc, "u_tm", [128, NT * D], BF16)
    ut2 = [alloc(ctx, nc, f"ut2{i}", [128, D], BF16) for i in range(2)]
    rbf = alloc(ctx, nc, "rbf", [128, KC * E], BF16)
    brr = alloc(ctx, nc, "brr", [1, E], BF16)
    oh_fm = alloc(ctx, nc, "oh_fm", [8, NTT * 128], F32)
    onehot_tm = alloc(ctx, nc, "onehot_tm", [128, NTT * E], F32)
    gates = alloc(ctx, nc, "gates", [128, NTT], F32)
    oner8 = alloc(ctx, nc, "oner8", [1, 128], BF16)
    prob = alloc(ctx, nc, "prob", [128, E], F32)
    eq = alloc(ctx, nc, "eq", [128, E], F32)
    cum = alloc(ctx, nc, "cum", [128, E], F32)
    zer8 = alloc(ctx, nc, "zer8", [128, E], F32)
    c1 = alloc(ctx, nc, "c1", [128, 8], F32)
    cnt_sb = alloc(ctx, nc, "cnt_sb", [8, E], F32)
    pref_sb = alloc(ctx, nc, "pref_sb", [8, 1], F32)
    base_sb = alloc(ctx, nc, "base_sb", [8, 1], F32)
    b2sb = alloc(ctx, nc, "b2sb", [1, E * D], BF16)
    b1cs = alloc(ctx, nc, "b1cs", [128, E * KC], F32)
    ones8f = alloc(ctx, nc, "ones8f", [8, 1], F32)
    zcol = alloc(ctx, nc, "zcol", [8, 1], F32)
    kg = alloc(ctx, nc, "kg", [128, NTT], F32)
    ecl = alloc(ctx, nc, "ecl", [128, E], F32)
    ecli = alloc(ctx, nc, "ecli", [128, E], I32)
    oc = alloc(ctx, nc, "ocol", [128, NTT], I32)
    eo_tm = alloc(ctx, nc, "eo_tm", [128, D], BF16)
    gth = alloc(ctx, nc, "gth", [128, D], BF16)
    wslot = [alloc(ctx, nc, f"w{i}", [128, 16384], BF16) for i in range(2)]
    ctx2 = contextlib.ExitStack()
    u_fm = [alloc(ctx2, nc, f"ufm{k}", [128, TG], BF16) for k in range(KC)]
    pos_fm = alloc(ctx2, nc, "pos_fm", [8, NTT * 128], F32)
    kp_fm = alloc(ctx2, nc, "kp_fm", [8, NTT * 128], F32)

    ps_tr = PsumRing(ctx, nc, "pstr", 2, 512, BF16)
    ps_oh = PsumRing(ctx, nc, "psoh", 1, 512, F32)
    ps_r = PsumRing(ctx, nc, "psr", 1, 512, F32)
    ps_a = PsumRing(ctx, nc, "psa", 2, 512, F32)
    ps_b = PsumRing(ctx, nc, "psb", 2, 512, F32)

    p.on("sync", lambda e: e.dma_start(out=st.idbf[:, :], in_=idnd), dma_chan="init")
    p.on("sync", lambda e: e.dma_start(out=idf[:, :], in_=idfd), dma_chan="init")
    p.on("sync", lambda e: e.dma_start(out=st.dump[0:1, :E], in_=brd), dma_chan="init")
    p.on("sync", lambda e: e.dma_start(out=b1cs[:, :], in_=b1cd), dma_chan="init")
    p.on("sync", lambda e: e.dma_start(out=cnt_sb[:, :], in_=cntd), dma_chan="init")
    p.on("sync", lambda e: e.dma_start(out=pref_sb[:, :], in_=prefd), dma_chan="init")
    p.on("gpsimd", lambda e: e.dma_start(
        out=rbf[:, :].rearrange("p (k e) -> p k e", k=KC),
        in_=rTd.rearrange("(k p) e -> p k e", p=128)), dma_chan="initg")
    for e_ in range(E):
        p.on("gpsimd", lambda e, e_=e_: e.dma_start(
            out=b2sb[0:1, e_ * D:(e_ + 1) * D], in_=b2d[e_:e_ + 1, :]),
            dma_chan="initg")
    init_done = p.chan_total("init")
    initg_done = p.chan_total("initg")
    tid = init_done
    p.on("vector", lambda e: e.memset(st.epscol[:, :], 1e-5))
    p.on("vector", lambda e: e.memset(zer8[:, :], 0.0))
    p.on("vector", lambda e: e.memset(oner8[:, :], 1.0))
    p.on("vector", lambda e: e.memset(zcol[:, :], 0.0))
    p.on("vector", lambda e: e.memset(ones8f[:, :], 1.0))
    tio = p.on("gpsimd", lambda e: e.iota(ecli[:, :], pattern=[[CL, E]], base=-1,
                                          channel_multiplier=0))
    p.on("vector", lambda e: e.tensor_copy(ecl[:, :], ecli[:, :]), after=[tio])
    tbr = p.on("vector", lambda e: e.tensor_copy(brr[:, :], st.dump[0:1, :E]),
               after=[init_done])
    tze = p.on("vector", lambda e: e.memset(eo_tm[:, :], 0.0))
    tzr = p.on("sync", lambda e: e.dma_start(
        out=EOdd[E * CL:E * CL + 1, :], in_=eo_tm[0:1, :]), after=[tze],
        dma_chan="ez")
    for r in range(E * CL // 128):
        p.on("sync", lambda e, r=r: e.dma_start(
            out=Uedd[r * 128:(r + 1) * 128, :], in_=eo_tm[:, :]), after=[tze],
            dma_chan="ez")
    p.on("sync", lambda e: e.dma_start(
        out=Uedd[E * CL:E * CL + 1, :], in_=eo_tm[0:1, :]), after=[tze],
        dma_chan="ez")
    ez_all = p.chan_total("ez")

    xt_war = [None, None]
    ud_prev = {}

    oh_done = []
    prev_rt_tail = None
    for g in range(NG):
        ut_toks = []
        for i in range(NT):
            sl = i % 2
            tx = p.on("sync", lambda e, g=g, i=i, sl=sl: e.dma_start(
                out=xt[sl][:, :], in_=Xd[(g * NT + i) * 128:(g * NT + i + 1) * 128, :]),
                after=([xt_war[sl]] if xt_war[sl] else []), dma_chan=f"xt{sl}",
                tag=f"lx{g}_{i}")
            if i in ud_prev:
                p.wait_all("vector", [ud_prev[i]], tag=f"udw{g}_{i}")
            st.ln_after = [tx]
            tu = emit_ln_tile(p, st, xt[sl][:, :], u_tm[:, i * D:(i + 1) * D], i,
                              tag=f"ln{g}_{i}")
            xt_war[sl] = tu
            ud_prev[i] = p.on("sync", lambda e, g=g, i=i: e.dma_start(
                out=Udd[(g * NT + i) * 128:(g * NT + i + 1) * 128, :],
                in_=u_tm[:, i * D:(i + 1) * D]), after=[tu], dma_chan=f"ud{i % 2}",
                tag=f"sud{g}_{i}")
            ut_toks.append(tu)
        ufm_done = emit_transpose(p, st, lambda i: u_tm[:, i * D:(i + 1) * D],
                                  u_fm, NT, KC, ut_toks + [tid], ps_tr, tag=f"tr{g}")
        for i in range(NT):
            ti_ = g * NT + i
            ps, war, b = ps_r.get()
            last = None
            for k in range(KC):
                aft = ufm_done + [initg_done] + ([war] if war and k == 0 else [])
                last = p.on("tensor", lambda e, k=k, i=i, ps=ps: e.matmul(
                    ps[:, :E], u_fm[k][:, i * 128:(i + 1) * 128],
                    rbf[:, k * E:(k + 1) * E],
                    start=(k == 0), stop=False),
                    after=(aft if k == 0 else []), inc=0, tag=f"rt{g}_{i}")
            last = p.on("tensor", lambda e, ps=ps: e.matmul(
                ps[:, :E], oner8[0:1, :], brr[:, :], start=False, stop=True),
                after=[tbr], inc=1, tag=f"rtb{g}_{i}")
            tm_ = p.on("vector", lambda e, ps=ps: e.reduce_max(
                c1[:, 0:1], ps[:, :E], AX.X, negate=True), after=[last])
            te = p.on("scalar", lambda e, ps=ps: e.activation(
                prob[:, :], ps[:, :E], AF.Exp, bias=c1[:, 0:1]),
                after=[tm_] + ([prev_rt_tail] if prev_rt_tail else []))
            ps_r.consumed(b, te)
            p.on("vector", lambda e: e.reduce_sum(c1[:, 1:2], prob[:, :], AX.X),
                 after=[te])
            p.on("vector", lambda e, ti_=ti_: e.reciprocal(
                gates[:, ti_:ti_ + 1], c1[:, 1:2]))
            t1_ = p.on("vector", lambda e: e.reduce_max(c1[:, 2:3], prob[:, :], AX.X))
            p.on("vector", lambda e: e.tensor_scalar(eq[:, :], prob[:, :], c1[:, 2:3],
                                                     None, ALU.is_ge), after=[t1_])
            p.on("vector", lambda e: e.tensor_tensor_scan(cum[:, :], eq[:, :],
                                                          zer8[:, :], 0.0, ALU.add,
                                                          ALU.add))
            p.on("vector", lambda e: e.tensor_scalar(cum[:, :], cum[:, :], 1.0, None,
                                                     ALU.is_equal))
            tq = p.on("vector", lambda e, ti_=ti_: e.tensor_mul(
                onehot_tm[:, ti_ * E:(ti_ + 1) * E], eq[:, :], cum[:, :]),
                tag=f"oh{g}_{i}")
            prev_rt_tail = tq
        for i0 in range(0, NT, 4):
            ps, war, b = ps_oh.get()
            lastt = None
            for j in range(4):
                ti_ = g * NT + i0 + j
                aft = ([prev_rt_tail] if j == 0 else []) + ([war] if war and j == 0 else [])
                lastt = p.on("tensor", lambda e, ti_=ti_, j=j, ps=ps: e.transpose(
                    ps[0:8, j * 128:(j + 1) * 128],
                    onehot_tm[:, ti_ * E:(ti_ + 1) * E], idf[:, :]),
                    after=aft, inc=(1 if j == 3 else 0), tag=f"oht{g}_{i0}_{j}")
            tv = p.on("vector", lambda e, g=g, i0=i0, ps=ps: e.tensor_copy(
                oh_fm[:, (g * NT + i0) * 128:(g * NT + i0 + 4) * 128], ps[0:8, :]),
                after=[lastt], tag=f"ohe{g}_{i0}")
            ps_oh.consumed(b, tv)
            oh_done.append(tv)

    psb_, warb, bb_ = ps_oh.get()
    tb_ = p.on("tensor", lambda e: e.matmul(psb_[0:8, 0:1], cnt_sb[:, :],
                                            pref_sb[:, :], start=True, stop=True),
               after=[init_done] + oh_done + ([warb] if warb else []), tag="base")
    tb_ = p.on("vector", lambda e: e.tensor_copy(base_sb[:, :], psb_[0:8, 0:1]),
               after=[tb_])
    ps_oh.consumed(bb_, tb_)
    tpos = p.on("vector", lambda e: e.tensor_tensor_scan(
        pos_fm[:, :], oh_fm[:, :], zcol[:, 0:1].broadcast_to([8, NTT * 128]),
        base_sb[:, 0:1], ALU.add, ALU.add), after=[tb_] + oh_done, tag="scan")
    tkp = p.on("vector", lambda e: e.tensor_scalar(kp_fm[:, :], pos_fm[:, :],
                                                   float(C - 1) + 0.5, None,
                                                   ALU.is_le), after=[tpos], tag="keep")
    tohk = p.on("vector", lambda e: e.tensor_mul(kp_fm[:, :], kp_fm[:, :],
                                                 oh_fm[:, :]), after=[tkp], tag="ohk")
    tploc = p.on("vector", lambda e: e.tensor_scalar(
        pos_fm[:, :], pos_fm[:, :], base_sb[:, 0:1], None, ALU.subtract),
        after=[tpos], tag="ploc")
    kg_done = []
    sc_done = []
    for ti_ in range(NTT):
        ps, war, b = ps_r.get()
        tkk = p.on("tensor", lambda e, ti_=ti_, ps=ps: e.matmul(
            ps[:, 0:1], kp_fm[:, ti_ * 128:(ti_ + 1) * 128], ones8f[:, :],
            start=True, stop=True),
            after=[tohk] + ([war] if war else []), tag=f"kt{ti_}")
        tkg = p.on("vector", lambda e, ti_=ti_, ps=ps: e.scalar_tensor_tensor(
            kg[:, ti_:ti_ + 1], ps[:, 0:1], 0.5, gates[:, ti_:ti_ + 1],
            ALU.mult, ALU.mult), after=[tkk], tag=f"kg{ti_}")
        ps_r.consumed(b, tkg)
        kg_done.append(tkg)
        pso_, waro, bo_ = ps_oh.get()
        tt = p.on("tensor", lambda e, ti_=ti_, pso_=pso_: e.matmul(
            pso_[:, 0:8], pos_fm[:, ti_ * 128:(ti_ + 1) * 128], idf[0:8, 0:8],
            start=True, stop=True),
            after=[tploc] + ([waro] if waro else []), tag=f"pt{ti_}")
        p.on("vector", lambda e, ti_=ti_, pso_=pso_: e.tensor_mul(
            eq[:, :], onehot_tm[:, ti_ * E:(ti_ + 1) * E], pso_[:, 0:8]),
            after=[tt])
        p.on("vector", lambda e: e.reduce_sum(c1[:, 5:6], eq[:, :], AX.X))
        p.on("vector", lambda e, ti_=ti_: e.tensor_mul(
            eq[:, :], onehot_tm[:, ti_ * E:(ti_ + 1) * E], ecl[:, :]))
        p.on("vector", lambda e: e.reduce_sum(c1[:, 6:7], eq[:, :], AX.X))
        p.on("vector", lambda e: e.tensor_add(c1[:, 7:8], c1[:, 5:6], c1[:, 6:7]))
        p.on("vector", lambda e: e.tensor_scalar(c1[:, 3:4], c1[:, 5:6],
                                                 CL + 0.5, None, ALU.is_ge))
        p.on("vector", lambda e: e.tensor_scalar(c1[:, 4:5], c1[:, 7:8],
                                                 -1.0, float(E * CL),
                                                 ALU.mult, ALU.add))
        p.on("vector", lambda e: e.scalar_tensor_tensor(
            c1[:, 7:8], c1[:, 4:5], c1[:, 3:4], c1[:, 7:8], ALU.mult, ALU.add))
        tlast = p.on("vector", lambda e, ti_=ti_: e.tensor_copy(
            oc[:, ti_:ti_ + 1], c1[:, 7:8]), tag=f"oc{ti_}")
        ps_oh.consumed(bo_, tlast)
        sc_done.append(tlast)
    ud_all = [p.chan_total("ud0"), p.chan_total("ud1")]
    scat_last = None
    utm_war = [None, None]
    for ti_ in range(NTT):
        sl = ti_ % 2
        tl = p.on("sync", lambda e, ti_=ti_, sl=sl: e.dma_start(
            out=ut2[sl][:, :], in_=Udd[ti_ * 128:(ti_ + 1) * 128, :]),
            after=([utm_war[sl]] if utm_war[sl] else []) + ud_all + sc_done,
            dma_chan=f"xu{sl}", tag=f"lu{ti_}")
        tgm = p.on("vector", lambda e, ti_=ti_, sl=sl: e.tensor_scalar(
            ut2[sl][:, :], ut2[sl][:, :], gates[:, ti_:ti_ + 1], None, ALU.mult),
            after=[tl], tag=f"ug{ti_}")
        ts_ = p.on("gpsimd", lambda e, ti_=ti_, sl=sl: e.indirect_dma_start(
            Uedd, IndirectOffsetOnAxis(ap=oc[:, ti_:ti_ + 1], axis=0),
            ut2[sl][:, :], None), after=[tgm, ez_all], dma_chan="sc0",
            tag=f"sca{ti_}")
        utm_war[sl] = ts_
        scat_last = ts_
    ctx2.close()
    ue_tm = alloc(ctx, nc, "ue_tm", [128, NCT * D], BF16)
    ue_fm = [alloc(ctx, nc, f"uefm{k}", [128, CL], BF16) for k in range(KC)]
    h_fm = [alloc(ctx, nc, f"hfm{f}", [128, CL], BF16) for f in range(KC)]
    scat_all = [p.chan_total("sc0")]
    wwar = {0: [], 1: []}

    def wpair(slot, e_, tg):
        t1_ = p.on("gpsimd", lambda e, slot=slot, e_=e_: e.dma_start(
            out=wslot[slot][:, :KC * D].rearrange("p (c n) -> p c n", c=KC),
            in_=W1d[e_].rearrange("(c p) n -> p c n", p=128)), dma_chan=f"w{slot}",
            after=wwar[slot], tag=f"w1{tg}")
        t2_ = p.on("gpsimd", lambda e, slot=slot, e_=e_: e.dma_start(
            out=wslot[slot][:, KC * D:2 * KC * D].rearrange("p (c n) -> p c n", c=KC),
            in_=W2d[e_].rearrange("(c p) n -> p c n", p=128)), dma_chan=f"w{slot}",
            tag=f"w2{tg}")
        return (t1_, t2_)

    eo_war = [tzr]
    uef_prev = None
    prev_w = wpair(0, 0, "e0")
    for e_ in range(E):
        slot = e_ % 2
        wt_ = prev_w
        if e_ + 1 < E:
            prev_w = wpair((e_ + 1) % 2, e_ + 1, f"e{e_ + 1}")
        lds = []
        for j in range(NCT):
            aft = scat_all if e_ == 0 else ([uef_prev] if j == 0 else [])
            tl = p.on("gpsimd", lambda e, e_=e_, j=j: e.dma_start(
                out=ue_tm[:, j * D:(j + 1) * D],
                in_=Uedd[e_ * CL + j * 128:e_ * CL + (j + 1) * 128, :]),
                after=aft, dma_chan="ue", tag=f"ue{e_}_{j}")
            lds.append(tl)
        uef_done = emit_transpose(p, st, lambda j: ue_tm[:, j * D:(j + 1) * D],
                                  ue_fm, NCT, KC, lds + [tid], ps_tr,
                                  tag=f"uet{e_}")
        uef_prev = uef_done[-1]
        cons_h = []
        off = 0
        for tg_ in TGS:
            for f in range(KC):
                ps, war, b = ps_a.get()
                last = None
                for k in range(KC):
                    aft = [wt_[0]] + uef_done + ([war] if war and k == 0 else [])
                    last = p.on("tensor", lambda e, k=k, f=f, off=off, tg_=tg_,
                                ps=ps, slot=slot: e.matmul(
                        ps[:, :tg_],
                        wslot[slot][:, k * D + f * 128:k * D + (f + 1) * 128],
                        ue_fm[k][:, off:off + tg_],
                        start=(k == 0), stop=(k == KC - 1)),
                        after=(aft if k == 0 else []),
                        inc=(1 if k == KC - 1 else 0), tag=f"h{e_}_{f}_{off}")
                ev = p.on("scalar", lambda e, f=f, off=off, tg_=tg_, e_=e_, ps=ps:
                          e.activation(h_fm[f][:, off:off + tg_], ps[:, :tg_],
                                       AF.Relu,
                                       bias=b1cs[:, e_ * KC + f:e_ * KC + f + 1]),
                          after=[last, init_done], tag=f"hr{e_}_{f}_{off}")
                ps_a.consumed(b, ev)
                cons_h.append(ev)
            off += tg_
        cons_y = []
        for j in range(NCT):
            evs = []
            for dg in range(2):
                ps, war, b = ps_b.get()
                last = None
                for fk in range(KC):
                    aft = [wt_[1]] + cons_h + ([war] if war and fk == 0 else [])
                    last = p.on("tensor", lambda e, fk=fk, j=j, dg=dg, ps=ps,
                                slot=slot: e.matmul(
                        ps[:, :], h_fm[fk][:, j * 128:(j + 1) * 128],
                        wslot[slot][:, KC * D + fk * D + dg * 512:
                                    KC * D + fk * D + (dg + 1) * 512],
                        start=(fk == 0), stop=False),
                        after=(aft if fk == 0 else []), inc=0,
                        tag=f"y{e_}_{j}_{dg}")
                last = p.on("tensor", lambda e, e_=e_, dg=dg, ps=ps: e.matmul(
                    ps[:, :], oner8[0:1, :],
                    b2sb[0:1, e_ * D + dg * 512:e_ * D + (dg + 1) * 512],
                    start=False, stop=True), after=[initg_done], inc=1,
                    tag=f"yb{e_}_{j}_{dg}")
                ev = p.on("vector", lambda e, dg=dg, ps=ps: e.tensor_copy(
                    eo_tm[:, dg * 512:(dg + 1) * 512], ps[:, :]),
                    after=[last] + eo_war, tag=f"eoe{e_}_{j}_{dg}")
                ps_b.consumed(b, ev)
                evs.append(ev)
            ts_ = p.on("gpsimd", lambda e, e_=e_, j=j: e.dma_start(
                out=EOdd[e_ * CL + j * 128:e_ * CL + (j + 1) * 128, :],
                in_=eo_tm[:, :]), after=evs, dma_chan="eo", tag=f"eow{e_}_{j}")
            eo_war = [ts_]
            cons_y.append(ts_)
        wwar[slot] = cons_h + cons_y
    eo_all = p.chan_total("eo")
    gth_war = None
    for ti_ in range(NTT):
        sl = ti_ % 2
        tg1_ = p.on("gpsimd", lambda e, ti_=ti_: e.indirect_dma_start(
            gth[:, :], None, EOdd,
            IndirectOffsetOnAxis(ap=oc[:, ti_:ti_ + 1], axis=0)),
            after=[eo_all, tzr] + sc_done + ([gth_war] if gth_war else []),
            dma_chan="gt0", tag=f"g1{ti_}")
        tx = p.on("sync", lambda e, ti_=ti_, sl=sl: e.dma_start(
            out=xt[sl][:, :], in_=Xd[ti_ * 128:(ti_ + 1) * 128, :]),
            after=([xt_war[sl]] if xt_war[sl] else []), dma_chan=f"xt{sl}",
            tag=f"lx3{ti_}")
        t3_ = p.on("vector", lambda e, ti_=ti_, sl=sl: e.scalar_tensor_tensor(
            xt[sl][:, :], gth[:, :], kg[:, ti_:ti_ + 1], xt[sl][:, :],
            ALU.mult, ALU.add), after=[tx, tg1_] + kg_done, tag=f"out{ti_}")
        gth_war = t3_
        ts_ = p.on("sync", lambda e, ti_=ti_, sl=sl: e.dma_start(
            out=XOd[ti_ * 128:(ti_ + 1) * 128, :], in_=xt[sl][:, :]), after=[t3_],
            dma_chan=f"st{sl}", tag=f"sx{ti_}")
        xt_war[sl] = ts_
    return p


def build_l0ffn_sparse(nc, ctx, T=4096, TG=1024, C=4096, CL=768):
    Xd = nc.dram_tensor("X", [T, D], F32, kind="ExternalInput").ap()
    XOd = nc.dram_tensor("XO", [T, D], F32, kind="ExternalOutput").ap()
    wd = {}
    for nm, (shape, dt_) in l0ffn_shapes().items():
        wd[nm] = nc.dram_tensor(nm, shape, dt_, kind="ExternalInput").ap()
    cntd = nc.dram_tensor("cnt", [8, 8], F32, kind="ExternalInput").ap()
    prefd = nc.dram_tensor("pref", [8, 1], F32, kind="ExternalInput").ap()
    Udd = nc.dram_tensor("Ud", [T, D], BF16).ap()
    Uedd = nc.dram_tensor("Ued", [8 * CL + 1, D], BF16).ap()
    EOdd = nc.dram_tensor("EOd", [8 * CL + 1, D], BF16).ap()
    p = Prog(nc)
    emit_l0ffn_sparse(p, nc, ctx, Xd, XOd, wd, Udd, Uedd, EOdd, T=T, TG=TG,
                      C=C, CL=CL, cntd=cntd, prefd=prefd)
    p.finalize()
    p.make_sems()
    return p

NCORES = 8
TPC = 4096
L1CAP = 1280
L0CAP = 768

_cache = {}


def _decl_w(nc, pre, shapes, names):
    wd = {}
    for nm in names:
        shape, dt_ = shapes[nm]
        wd[nm] = nc.dram_tensor(pre + nm, shape, dt_, kind="ExternalInput").ap()
    return wd


def _build_phaseA():
    nc = bass.Bass("TRN2", target_bir_lowering=False, debug=False)
    ctx = contextlib.ExitStack()
    Xd = nc.dram_tensor("X", [TPC, D], F32, kind="ExternalInput").ap()
    CNTd = nc.dram_tensor("CNT", [8, 1], F32, kind="ExternalOutput").ap()
    wd = _decl_w(nc, "c0_", l0ffn_shapes(), ["rT", "br", "idn", "idf"])
    p = Prog(nc)
    with contextlib.ExitStack() as c1:
        emit_l0ffn(p, nc, c1, Xd, None, wd, T=TPC, mode="count", CNTd=CNTd)
    p.finalize()
    p.make_sems()
    with nc.Block() as blk:
        p.emit(blk)
    return nc, ctx


def _build_phaseB():
    nc = bass.Bass("TRN2", target_bir_lowering=False, debug=False)
    ctx = contextlib.ExitStack()
    Xd = nc.dram_tensor("X", [TPC, D], F32, kind="ExternalInput").ap()
    cntd = nc.dram_tensor("cnt", [8, 8], F32, kind="ExternalInput").ap()
    prefd = nc.dram_tensor("pref", [8, 1], F32, kind="ExternalInput").ap()
    X1d = nc.dram_tensor("X1", [TPC, D], F32).ap()
    X2d = nc.dram_tensor("X2", [TPC, D], F32, kind="ExternalOutput").ap()
    CNTd = nc.dram_tensor("CNT", [8, 1], F32, kind="ExternalOutput").ap()
    wd0 = _decl_w(nc, "f0_", l0ffn_shapes(), L0FFN_W)
    wda = _decl_w(nc, "a0_", l0attn_shapes(), L0ATTN_W)
    Udd = nc.dram_tensor("Ud", [TPC, D], BF16).ap()
    Uedd = nc.dram_tensor("Ued", [8 * L0CAP + 1, D], BF16).ap()
    EOdd = nc.dram_tensor("EOd", [8 * L0CAP + 1, D], BF16).ap()
    p = Prog(nc)
    with contextlib.ExitStack() as c1:
        emit_l0ffn_sparse(p, nc, c1, Xd, X1d, wd0, Udd, Uedd, EOdd, T=TPC,
                          C=4096, CL=L0CAP, cntd=cntd, prefd=prefd)
    p.barrier_all()
    with contextlib.ExitStack() as c2:
        emit_l0attn(p, nc, c2, X1d, X2d, wda, T=TPC, S=2048)
    p.barrier_all()
    with contextlib.ExitStack() as c3:
        emit_l0ffn(p, nc, c3, X2d, None, wd0, T=TPC, mode="count", CNTd=CNTd)
    p.finalize()
    p.make_sems()
    with nc.Block() as blk:
        p.emit(blk)
    return nc, ctx


def _build_phaseC(stages=4):
    nc = bass.Bass("TRN2", target_bir_lowering=False, debug=False)
    ctx = contextlib.ExitStack()
    Xd = nc.dram_tensor("X", [TPC, D], F32, kind="ExternalInput").ap()
    cntd = nc.dram_tensor("cnt", [8, 8], F32, kind="ExternalInput").ap()
    prefd = nc.dram_tensor("pref", [8, 1], F32, kind="ExternalInput").ap()
    X3d = nc.dram_tensor("X3", [TPC, D], F32).ap()
    X4d = nc.dram_tensor("X4", [TPC, D], F32).ap()
    X5d = nc.dram_tensor("X5", [TPC, D], F32).ap()
    XOd = nc.dram_tensor("XO", [TPC, D], F32, kind="ExternalOutput").ap()
    wd0 = _decl_w(nc, "f0_", l0ffn_shapes(), L0FFN_W)
    wd1 = _decl_w(nc, "f1_", l1ffns_shapes(), L1FFNS_W)
    wda = _decl_w(nc, "a1_", l1attn_shapes(), L1ATTN_W)
    Udd = nc.dram_tensor("Ud", [TPC, D], BF16).ap()
    Uedd = nc.dram_tensor("Ued", [8 * L1CAP + 1, D], BF16).ap()
    EOdd = nc.dram_tensor("EOd", [8 * L1CAP + 1, D], BF16).ap()
    p = Prog(nc)
    outs = [X3d, X4d, X5d, XOd]
    outs[stages - 1] = XOd
    with contextlib.ExitStack() as c1:
        emit_l0ffn_sparse(p, nc, c1, Xd, outs[0], wd0, Udd, Uedd, EOdd, T=TPC,
                          C=4096, CL=L0CAP, cntd=cntd, prefd=prefd)
    if stages >= 2:
        p.barrier_all()
        with contextlib.ExitStack() as c2:
            emit_l1ffn_sparse(p, nc, c2, outs[0], outs[1], wd1, Udd, Uedd, EOdd,
                              T=TPC, CAP=L1CAP)
    if stages >= 3:
        p.barrier_all()
        with contextlib.ExitStack() as c3:
            emit_l1attn(p, nc, c3, outs[1], outs[2], wda, T=TPC, S=2048)
    if stages >= 4:
        p.barrier_all()
        with contextlib.ExitStack() as c4:
            emit_l1ffn_sparse(p, nc, c4, outs[2], outs[3], wd1, Udd, Uedd, EOdd,
                              T=TPC, CAP=L1CAP)
    p.finalize()
    p.make_sems()
    with nc.Block() as blk:
        p.emit(blk)
    return nc, ctx


def _get(name, **kwb):
    if name not in _cache:
        _cache[name] = {"A": _build_phaseA, "B": _build_phaseB,
                        "C": _build_phaseC}[name](**kwb)
    return _cache[name]


def kernel(**kw):
    f32 = np.float32
    idn = np.eye(128, dtype=ml_dtypes.bfloat16)
    idf = np.eye(128, dtype=f32)
    Xs = [np.ascontiguousarray(kw["L0_attn_x"][2 * c:2 * c + 2].reshape(TPC, 1024))
          for c in range(NCORES)]
    prefs = [(np.arange(8) < c).astype(f32)[:, None] for c in range(NCORES)]
    CH = 18

    w0r = {"rT": np.ascontiguousarray(kw["L0_ffn_Wr"], dtype=f32),
           "br": np.ascontiguousarray(kw["L0_ffn_br"][None, :], dtype=f32),
           "idn": idn, "idf": idf}
    w0f = {**w0r,
           "W1": np.ascontiguousarray(kw["L0_ffn_W1"], dtype=f32),
           "W2": np.ascontiguousarray(kw["L0_ffn_W2"], dtype=f32),
           "b1c": np.ascontiguousarray(
               kw["L0_ffn_b1"].reshape(8, 8, 128).transpose(2, 0, 1).reshape(128, 64)),
           "b2": np.ascontiguousarray(kw["L0_ffn_b2"], dtype=f32)}
    wa0 = {
        "nwt": np.ascontiguousarray(np.tile(kw["L0_attn_norm_w"][None, :], (128, 1)),
                                    dtype=f32),
        "ipT": np.ascontiguousarray(kw["L0_attn_in_proj_w"].T, dtype=f32),
        "convw": np.ascontiguousarray(
            kw["L0_attn_conv_w"].reshape(CH, 128, 4).transpose(1, 0, 2).reshape(128, CH * 4)),
        "convb": np.ascontiguousarray(kw["L0_attn_conv_b"].reshape(CH, 128).T),
        "dtb": np.ascontiguousarray(kw["L0_attn_dt_bias"][None, :], dtype=f32),
        "alog": np.ascontiguousarray(kw["L0_attn_A_log"][None, :], dtype=f32),
        "dvec": np.ascontiguousarray(kw["L0_attn_D"][None, :], dtype=f32),
        "gnwt": np.ascontiguousarray(
            np.tile(kw["L0_attn_gate_norm_w"][None, :], (128, 1)), dtype=f32),
        "zwT": np.ascontiguousarray(kw["L0_attn_zero_w"].T, dtype=f32),
        "owT": np.ascontiguousarray(kw["L0_attn_one_w"].T, dtype=f32),
        "fT": np.ascontiguousarray(kw["L0_attn_fusion_w"].T, dtype=f32),
        "idn": idn, "idf": idf,
        "LTi": np.tril(np.ones((128, 128), f32)).T,
        "UTs": (np.arange(128)[:, None] > np.arange(128)[None, :]).astype(f32),
        "ALO": np.ones((128, 128), f32),
        "SDG": (np.arange(128)[:, None] == np.arange(128)[None, :] + 1).astype(f32) * -1e30,
        "selc": np.concatenate(
            [np.tile((np.arange(32) == h).astype(f32)[:, None], (1, 128))
             for h in range(32)], 1),
    }
    w1f = {"rT": np.ascontiguousarray(kw["L1_ffn_router_w"].T, dtype=f32),
           "gT": np.ascontiguousarray(np.transpose(kw["L1_ffn_gate_w"], (0, 2, 1)),
                                      dtype=f32),
           "uT": np.ascontiguousarray(np.transpose(kw["L1_ffn_up_w"], (0, 2, 1)),
                                      dtype=f32),
           "dT": np.ascontiguousarray(np.transpose(kw["L1_ffn_down_w"], (0, 2, 1)),
                                      dtype=f32),
           "idn": idn, "idf": idf,
           "ecap": np.tile((np.arange(8) * L1CAP - 1).astype(f32), (128, 1))}
    w1a = {"Wq": np.ascontiguousarray(kw["L1_attn_Wq"], dtype=f32),
           "Wk": np.ascontiguousarray(kw["L1_attn_Wk"], dtype=f32),
           "Wv": np.ascontiguousarray(kw["L1_attn_Wv"], dtype=f32),
           "Wo": np.ascontiguousarray(kw["L1_attn_Wo"], dtype=f32),
           "W1": np.ascontiguousarray(kw["L1_attn_W1"], dtype=f32),
           "W2": np.ascontiguousarray(kw["L1_attn_W2"], dtype=f32),
           "bqc": np.ascontiguousarray(kw["L1_attn_bq"].reshape(8, 128).T),
           "bkr": np.ascontiguousarray(kw["L1_attn_bk"][None, :], dtype=f32),
           "bvr": np.ascontiguousarray(kw["L1_attn_bv"][None, :], dtype=f32),
           "bor": np.ascontiguousarray(kw["L1_attn_bo"][None, :], dtype=f32),
           "b1c": np.ascontiguousarray(kw["L1_attn_b1"].reshape(32, 128).T),
           "b2t": np.ascontiguousarray(np.tile(kw["L1_attn_b2"][None, :], (128, 1))),
           "g1t": np.ascontiguousarray(np.tile(kw["L1_attn_g1"][None, :], (128, 1))),
           "be1t": np.ascontiguousarray(np.tile(kw["L1_attn_be1"][None, :], (128, 1))),
           "g2t": np.ascontiguousarray(np.tile(kw["L1_attn_g2"][None, :], (128, 1))),
           "be2t": np.ascontiguousarray(np.tile(kw["L1_attn_be2"][None, :], (128, 1))),
           "idn": idn}

    def px(d, pre):
        return {pre + k: v for k, v in d.items()}

    ncA, _ = _get("A")
    resA = run_bass_kernel_spmd(
        ncA, [{"X": Xs[c], **px(w0r, "c0_")} for c in range(NCORES)],
        core_ids=list(range(NCORES)))
    cnt1 = np.ascontiguousarray(np.stack(
        [np.asarray(resA.results[c]["CNT"])[:, 0] for c in range(NCORES)], 0),
        dtype=f32)

    ncB, _ = _get("B")
    mapsB = [{"X": Xs[c], "cnt": cnt1, "pref": prefs[c],
              **px(w0f, "f0_"), **px(wa0, "a0_")} for c in range(NCORES)]
    resB = run_bass_kernel_spmd(ncB, mapsB, core_ids=list(range(NCORES)))
    X2s = [np.asarray(resB.results[c]["X2"]) for c in range(NCORES)]
    cnt2 = np.ascontiguousarray(np.stack(
        [np.asarray(resB.results[c]["CNT"])[:, 0] for c in range(NCORES)], 0),
        dtype=f32)

    ncC, _ = _get("C")
    mapsC = [{"X": X2s[c], "cnt": cnt2, "pref": prefs[c],
              **px(w0f, "f0_"), **px(w1f, "f1_"), **px(w1a, "a1_")}
             for c in range(NCORES)]
    resC = run_bass_kernel_spmd(ncC, mapsC, core_ids=list(range(NCORES)))

    out = np.empty((16, 2048, 1024), np.float32)
    for c in range(NCORES):
        out[2 * c:2 * c + 2] = np.asarray(resC.results[c]["XO"]).reshape(2, 2048, 1024)
    return out



# version 06
# speedup vs baseline: 1.2843x; 1.0516x over previous
import contextlib
import numpy as np
import ml_dtypes
import concourse.bass as bass
import concourse.mybir as mybir
from concourse.bass_utils import run_bass_kernel_spmd


"""Build kit: program-order recorder with per-engine + per-DMA-channel sems.

Soundness model (validated against the bass simulator's race detector):
- Compute engines complete in order; per-engine semaphore prefix waits are
  sound. Every compute op publishes +1 and all dependencies (same-engine
  included) are enforced with wait_ge (same-engine writeback ordering is
  NOT guaranteed otherwise — probe4).
- DMA completions are NOT ordered within a queue: a prefix wait on a
  shared DMA semaphore is unsound. Every DMA therefore goes to a named
  channel (own semaphore, +16 per DMA) under the discipline that a channel
  has at most one outstanding DMA (enforced structurally: the issuing
  engine waits for the previous consumer first), or is only ever waited
  at its full total with no later issues (init/store channels).
"""
import concourse.bass as bass
import concourse.mybir as mybir

F32 = mybir.dt.float32
BF16 = mybir.dt.bfloat16
I32 = mybir.dt.int32
AF = mybir.ActivationFunctionType
ALU = mybir.AluOpType
AX = mybir.AxisListType

ENGINES = ("sync", "gpsimd", "tensor", "scalar", "vector")
SERIAL = ("gpsimd", "scalar", "vector")


class Prog:
    def __init__(self, nc):
        self.nc = nc
        self.ops = {e: [] for e in ENGINES}
        self.count = {e: 0 for e in ENGINES}
        self.chan_count = {}
        self.sems = {}
        self.last_wait = {e: {} for e in ENGINES}
        self.prev_compute = {e: None for e in ENGINES}
        self._sem_ctxs = []

    def _wait(self, eng, key, val, tag):
        lw = self.last_wait[eng]
        if lw.get(key, 0) < val:
            self.ops[eng].append(("wait", key, val, tag))
            lw[key] = val

    def on(self, eng, fn, after=(), inc=None, tag=None, dma_chan=None):
        if dma_chan is not None:
            inc = 16
        elif inc is None:
            inc = 1
        deps = [t for t in after if t is not None]
        if dma_chan is None and eng in SERIAL and self.prev_compute[eng] is not None:
            deps.append(self.prev_compute[eng])
        if dma_chan is not None:
            prev = self.chan_count.get("c:" + dma_chan, 0)
            if prev:
                deps.append(("c:" + dma_chan, prev, True))
        for key, val, _async in deps:
            if key == eng and self.last_wait[eng].get(key, 0) >= val:
                continue
            self._wait(eng, key, val, tag)
        self.ops[eng].append(("emit", fn, inc, tag, dma_chan))
        if dma_chan is not None:
            key = "c:" + dma_chan
            self.chan_count[key] = self.chan_count.get(key, 0) + 16
            return (key, self.chan_count[key], True)
        if inc:
            self.count[eng] += inc
        tok = (eng, self.count[eng], False)
        if inc:
            self.prev_compute[eng] = tok
        return tok

    def barrier_all(self):
        toks = [(e, self.count[e], False) for e in ENGINES if self.count[e]]
        toks += [(k, v, True) for k, v in self.chan_count.items()]
        for e in ENGINES:
            if self.ops[e]:
                self.wait_all(e, toks, tag="barrier")

    def barrier_token(self, eng):
        return (eng, self.count[eng], False)

    def chan_total(self, chan):
        key = "c:" + chan
        return (key, self.chan_count.get(key, 0), True)

    def wait_all(self, eng, toks, tag=None):
        for tok in toks:
            if tok is None:
                continue
            key, val, _ = tok
            self._wait(eng, key, val, tag)

    def finalize(self, eng="sync"):
        for key, val in sorted(self.chan_count.items()):
            self._wait(eng, key, val, "final")

    def make_sems(self):
        for e in ENGINES:
            ctx = self.nc.semaphore("sem_" + e)
            h = ctx.__enter__()
            self._sem_ctxs.append(ctx)
            self.sems[e] = h
        for key in self.chan_count:
            ctx = self.nc.semaphore("sem_" + key[2:])
            h = ctx.__enter__()
            self._sem_ctxs.append(ctx)
            self.sems[key] = h

    def emit(self, blk):
        def mk(e):
            def body(engobj):
                for item in self.ops[e]:
                    if item[0] == "wait":
                        _, key, val, tag = item
                        engobj.wait_ge(self.sems[key], val)
                    else:
                        _, fn, inc, tag, chan = item
                        inst = fn(engobj)
                        if chan is not None:
                            inst.then_inc(self.sems["c:" + chan], 16)
                        elif inc:
                            assert inst is not None, f"inc={inc} tag={tag} returned None"
                            inst.then_inc(self.sems[e], inc)
            return body
        for e in ENGINES:
            if self.ops[e]:
                getattr(blk, e)(mk(e))

    def stats(self):
        return {e: len(self.ops[e]) for e in ENGINES}

"""Block program builders for the macaron-d2 megakernel (raw Bass).

Home layout: token-major [tokens on partitions, features on free].
Matmul inputs are feature-major bf16 (produced by PE transposes).

Transitive-safety rule used throughout: PE executes in order, so a writer
that waits on a PE token issued after all PE reads of its destination is
WAR-safe without an explicit edge. Same-engine ops are in order.
"""
import contextlib
import numpy as np
import concourse.bass as bass
import concourse.mybir as mybir

D = 1024


class St:
    pass


_UNIQ = [0]


def alloc(ctx, nc, name, shape, dtype):
    _UNIQ[0] += 1
    return ctx.enter_context(nc.sbuf_tensor(f"{name}_u{_UNIQ[0]}", shape, dtype))


def alloc_ps(ctx, nc, name, shape, dtype=F32):
    _UNIQ[0] += 1
    return ctx.enter_context(nc.psum_tensor(f"{name}_u{_UNIQ[0]}", shape, dtype))


class PsumRing:
    def __init__(self, ctx, nc, name, nbanks, width=512, dtype=F32):
        self.tiles = [alloc_ps(ctx, nc, f"{name}{i}", [128, width], dtype)
                      for i in range(nbanks)]
        self.war = [None] * nbanks
        self.i = 0

    def get(self):
        b = self.i % len(self.tiles)
        self.i += 1
        return self.tiles[b], self.war[b], b

    def consumed(self, b, token):
        self.war[b] = token


def emit_ln_tile(p, st, xt, ut, col, tag="ln", gb=None, bb=None):
    s1c = st.s1[:, col:col + 1]
    s2c = st.s2[:, col:col + 1]
    mc = st.mcol[:, col:col + 1]
    vc = st.vcol[:, col:col + 1]
    rc = st.rcol[:, col:col + 1]
    t1 = p.on("vector", lambda e: e.reduce_sum(s1c, xt, AX.X), after=st.ln_after,
              tag=tag + "_s1")
    t2 = p.on("scalar", lambda e: e.activation(st.dump2[:, :D], xt, AF.Square,
                                               accum_out=s2c), after=st.ln_after,
              tag=tag + "_s2")
    t3 = p.on("vector", lambda e: e.tensor_scalar(mc, s1c, 1.0 / D, None, ALU.mult),
              after=[t1], tag=tag + "_m")
    t4 = p.on("scalar", lambda e: e.activation(vc, mc, AF.Square), after=[t2, t3],
              tag=tag + "_m2")
    t5 = p.on("vector", lambda e: e.tensor_scalar(vc, s2c, 1.0 / D, vc, ALU.mult,
                                                  ALU.subtract), after=[t4],
              tag=tag + "_v")
    t6 = p.on("scalar", lambda e: e.activation(vc, vc, AF.Sqrt, bias=st.epscol[:, 0:1]),
              after=[t5], tag=tag + "_sd")
    t7 = p.on("vector", lambda e: e.reciprocal(rc, vc), after=[t6], tag=tag + "_r")
    if gb is None:
        t8 = p.on("vector", lambda e: e.tensor_scalar(ut, xt, mc, rc, ALU.subtract,
                                                      ALU.mult), after=[t7], tag=tag + "_ap")
        return t8
    tmp = st.lntmp[:, :xt.shape[1]]
    t8 = p.on("vector", lambda e: e.tensor_scalar(tmp, xt, mc, rc, ALU.subtract,
                                                  ALU.mult), after=[t7], tag=tag + "_ap")
    t9 = p.on("vector", lambda e: e.tensor_mul(tmp, tmp, gb), after=[t8], tag=tag + "_g")
    ta = p.on("vector", lambda e: e.tensor_add(ut, tmp, bb), after=[t9], tag=tag + "_b")
    return ta


def emit_transpose(p, st, src_tile, dst_fm, ntile, dchunks, after, ps_tr, tag="tr"):
    toks = []
    for k in range(dchunks):
        for i0 in range(0, ntile, 4):
            nb = min(4, ntile - i0)
            ps, war, b = ps_tr.get()
            last = None
            for j in range(nb):
                i = i0 + j
                aft = list(after) + ([war] if (war and j == 0) else [])
                last = p.on("tensor", lambda e, i=i, j=j, k=k, ps=ps: e.transpose(
                    ps[:, j * 128:(j + 1) * 128],
                    src_tile(i)[:, k * 128:(k + 1) * 128],
                    st.idbf[:, :]),
                    after=(aft if j == 0 else []), inc=(1 if j == nb - 1 else 0),
                    tag=f"{tag}_t{k}_{i}")
            tv = p.on("vector", lambda e, k=k, i0=i0, nb=nb, ps=ps: e.tensor_copy(
                dst_fm[k][:, i0 * 128:(i0 + nb) * 128], ps[:, :nb * 128]),
                after=[last], tag=f"{tag}_e{k}_{i0}")
            ps_tr.consumed(b, tv)
            toks.append(tv)
    return toks


L1FFN_W = ["rT", "gT", "uT", "dT", "idn"]


def emit_l1ffn(p, nc, ctx, Xd, XOd, wd, T=4096, TG=1024, debug=False):
    E, FF = 8, 2048
    KC = D // 128
    FC = FF // 128
    NT = TG // 128
    NG = T // TG
    dbg = {}
    rTd, gTd, uTd, dTd, idn = (wd[k] for k in L1FFN_W)

    st = St()
    st.idbf = alloc(ctx, nc, "idbf", [128, 128], BF16)
    st.dump = alloc(ctx, nc, "dump", [128, D], F32)
    st.dump2 = alloc(ctx, nc, "dump2", [128, D], F32)
    for nm in ("s1", "s2", "mcol", "vcol", "rcol"):
        setattr(st, nm, alloc(ctx, nc, nm, [128, NT], F32))

    xt = [alloc(ctx, nc, f"xt{i}", [128, D], F32) for i in range(2)]
    u_tm = alloc(ctx, nc, "u_tm", [128, NT * D], BF16)
    u_fm = [alloc(ctx, nc, f"ufm{k}", [128, TG], BF16) for k in range(KC)]
    h_fm = [alloc(ctx, nc, f"hfm{f}", [128, TG], BF16) for f in range(FC)]
    acc = alloc(ctx, nc, "acc", [128, NT * D], F32)
    wslot = [alloc(ctx, nc, f"w{i}", [128, 16384], BF16) for i in range(2)]
    rbf = alloc(ctx, nc, "rbf", [128, KC * E], BF16)
    zer8 = alloc(ctx, nc, "zer8", [128, E], F32)
    prob = alloc(ctx, nc, "prob", [128, E], F32)
    eq = alloc(ctx, nc, "eq", [128, E], F32)
    cum = alloc(ctx, nc, "cum", [128, E], F32)
    oh1 = alloc(ctx, nc, "oh1", [128, E], F32)
    oh2 = alloc(ctx, nc, "oh2", [128, E], F32)
    wecols = alloc(ctx, nc, "wecols", [128, NT * E], F32)
    c1 = alloc(ctx, nc, "c1", [128, 8], F32)

    ps_tr = PsumRing(ctx, nc, "pstr", 2, 512, BF16)
    ps_r = PsumRing(ctx, nc, "psr", 1, E, F32)
    ps_gu = PsumRing(ctx, nc, "psgu", 2, 512, F32)
    ps_d = PsumRing(ctx, nc, "psd", 2, 512, F32)

    st.epscol = alloc(ctx, nc, "epscol", [128, 1], F32)
    tid0 = p.on("sync", lambda e: e.dma_start(out=st.idbf[:, :], in_=idn),
                dma_chan="init")
    tz = p.on("vector", lambda e: e.memset(zer8[:, :], 0.0))
    p.on("vector", lambda e: e.memset(st.epscol[:, :], 1e-5))
    p.on("gpsimd", lambda e: e.dma_start(
        out=rbf[:, :].rearrange("p (k e) -> p k e", k=KC),
        in_=rTd.rearrange("(k p) e -> p k e", p=128)), dma_chan="initg")
    tid = p.chan_total("init")
    trw = p.chan_total("initg")

    wslot_war = {0: [], 1: []}
    wtok = {}

    def wload(slot, e, kind, tag=""):
        src = {"g": gTd, "u": uTd, "d": dTd}[kind][e]
        rows, cols = src.shape
        ch = rows // 128
        return p.on("gpsimd", lambda _e: _e.dma_start(
            out=wslot[slot][:, :ch * cols].rearrange("p (c n) -> p c n", c=ch),
            in_=src.rearrange("(c p) n -> p c n", p=128)), dma_chan=f"w{slot}",
            after=wslot_war[slot], tag=f"wl{tag}_{e}{kind}")

    xt_war = [None, None]
    prev_router_tail = None

    for g in range(NG):
        ut_toks = []
        for i in range(NT):
            sl = i % 2
            tx = p.on("sync", lambda e, i=i, sl=sl, g=g: e.dma_start(
                out=xt[sl][:, :], in_=Xd[(g * NT + i) * 128:(g * NT + i + 1) * 128, :]),
                after=([xt_war[sl]] if xt_war[sl] else []), dma_chan=f"xt{sl}",
                tag=f"ldx{g}_{i}")
            st.ln_after = [tx]
            tu = emit_ln_tile(p, st, xt[sl][:, :], u_tm[:, i * D:(i + 1) * D], i,
                              tag=f"ln{g}_{i}")
            xt_war[sl] = tu
            ut_toks.append(tu)

        ufm_done = emit_transpose(p, st, lambda i: u_tm[:, i * D:(i + 1) * D],
                                  u_fm, NT, KC, ut_toks + [tid], ps_tr, tag=f"tr{g}")

        if debug and g == NG - 1:
            p.on("sync", lambda e: e.dma_start(out=dbg["dbg_u"], in_=u_tm[:, :]),
                 after=ut_toks, inc=16)
            p.on("sync", lambda e: e.dma_start(out=dbg["dbg_s1"], in_=st.s1[:, :]),
                 after=ut_toks, inc=16)
            p.on("sync", lambda e: e.dma_start(out=dbg["dbg_s2"], in_=st.s2[:, :]),
                 after=ut_toks, inc=16)
            p.on("sync", lambda e: e.dma_start(out=dbg["dbg_m"], in_=st.mcol[:, :]),
                 after=ut_toks, inc=16)
            p.on("sync", lambda e: e.dma_start(out=dbg["dbg_r"], in_=st.rcol[:, :]),
                 after=ut_toks, inc=16)
            p.on("sync", lambda e: e.dma_start(out=dbg["dbg_ufm0"], in_=u_fm[0][:, :]),
                 after=ufm_done, inc=16)

        we_done = []
        for i in range(NT):
            ps, war, b = ps_r.get()
            last = None
            for k in range(KC):
                aft = ufm_done + [trw] + ([war] if war and k == 0 else [])
                last = p.on("tensor", lambda e, k=k, i=i, ps=ps: e.matmul(
                    ps[:, :], u_fm[k][:, i * 128:(i + 1) * 128],
                    rbf[:, k * E:(k + 1) * E],
                    start=(k == 0), stop=(k == KC - 1)),
                    after=(aft if k == 0 else []), inc=(1 if k == KC - 1 else 0),
                    tag=f"rt{g}_{i}")
            tm_ = p.on("vector", lambda e, ps=ps: e.reduce_max(
                c1[:, 0:1], ps[:, :], AX.X, negate=True), after=[last])
            te = p.on("scalar", lambda e, ps=ps: e.activation(
                prob[:, :], ps[:, :], AF.Exp, bias=c1[:, 0:1]),
                after=[tm_] + ([prev_router_tail] if prev_router_tail else []))
            ps_r.consumed(b, te)
            t1_ = p.on("vector", lambda e: e.reduce_max(c1[:, 1:2], prob[:, :], AX.X),
                       after=[te])
            p.on("vector", lambda e: e.tensor_scalar(eq[:, :], prob[:, :], c1[:, 1:2],
                                                     None, ALU.is_ge), after=[t1_])
            p.on("vector", lambda e: e.tensor_tensor_scan(cum[:, :], eq[:, :],
                                                          zer8[:, :], 0.0, ALU.add,
                                                          ALU.add), after=[tz])
            p.on("vector", lambda e: e.tensor_scalar(cum[:, :], cum[:, :], 1.0, None,
                                                     ALU.is_equal))
            p.on("vector", lambda e: e.tensor_mul(oh1[:, :], eq[:, :], cum[:, :]))
            p.on("vector", lambda e: e.tensor_scalar(cum[:, :], oh1[:, :], -1.0, 1.0,
                                                     ALU.mult, ALU.add))
            p.on("vector", lambda e: e.tensor_mul(eq[:, :], prob[:, :], cum[:, :]))
            p.on("vector", lambda e: e.reduce_max(c1[:, 2:3], eq[:, :], AX.X))
            p.on("vector", lambda e: e.tensor_scalar(cum[:, :], eq[:, :], c1[:, 2:3],
                                                     None, ALU.is_ge))
            p.on("vector", lambda e: e.tensor_tensor_scan(oh2[:, :], cum[:, :],
                                                          zer8[:, :], 0.0, ALU.add,
                                                          ALU.add))
            p.on("vector", lambda e: e.tensor_scalar(oh2[:, :], oh2[:, :], 1.0, None,
                                                     ALU.is_equal))
            p.on("vector", lambda e: e.tensor_mul(oh2[:, :], cum[:, :], oh2[:, :]))
            p.on("vector", lambda e: e.tensor_add(c1[:, 3:4], c1[:, 1:2], c1[:, 2:3]))
            p.on("vector", lambda e: e.reciprocal(c1[:, 4:5], c1[:, 3:4]))
            p.on("vector", lambda e, i=i: e.tensor_scalar(
                wecols[:, i * E:(i + 1) * E], oh1[:, :], c1[:, 1:2], None, ALU.mult))
            p.on("vector", lambda e, i=i: e.scalar_tensor_tensor(
                cum[:, :], oh2[:, :], c1[:, 2:3], wecols[:, i * E:(i + 1) * E],
                ALU.mult, ALU.add))
            tq = p.on("vector", lambda e, i=i: e.tensor_scalar(
                wecols[:, i * E:(i + 1) * E], cum[:, :], c1[:, 4:5], None, ALU.mult),
                tag=f"we{g}_{i}")
            prev_router_tail = tq
            we_done.append(tq)

        if debug and g == NG - 1:
            p.on("sync", lambda e: e.dma_start(out=dbg["dbg_we"], in_=wecols[:, :]),
                 after=we_done, inc=16)

        p.on("vector", lambda e: e.memset(acc[:, :], 0.0), tag=f"acc0{g}")

        phases = [(e, k) for e in range(E) for k in ("g", "u", "d")]
        h_g = []
        h_done = []
        for pi, (e, kind) in enumerate(phases):
            key = (g, e, kind)
            if key not in wtok:
                wtok[key] = wload(pi % 2, e, kind, tag=str(g))
            nxt = pi + 1
            if nxt < len(phases):
                en, kn = phases[nxt]
                if (g, en, kn) not in wtok:
                    wtok[(g, en, kn)] = wload(nxt % 2, en, kn, tag=str(g))
            elif g + 1 < NG:
                wtok[(g + 1, 0, "g")] = wload(nxt % 2, 0, "g", tag=str(g + 1))
            slot = pi % 2
            w = wslot[slot]
            wt = wtok[key]
            cons = []
            if debug and g == NG - 1 and e == 0 and kind == "g":
                p.on("sync", lambda _e, w=w: _e.dma_start(out=dbg["dbg_w"],
                     in_=w[:, :]), after=[wt], dma_chan="dbg")
            if kind in ("g", "u"):
                for f in range(FC):
                    for t4 in range(TG // 512):
                        ps, war, b = ps_gu.get()
                        last = None
                        for k in range(KC):
                            aft = [wt] + ufm_done + ([war] if war and k == 0 else [])
                            last = p.on("tensor", lambda _e, k=k, f=f, t4=t4, ps=ps, w=w: _e.matmul(
                                ps[:, :], w[:, k * 2048 + f * 128:k * 2048 + (f + 1) * 128],
                                u_fm[k][:, t4 * 512:(t4 + 1) * 512],
                                start=(k == 0), stop=(k == KC - 1)),
                                after=(aft if k == 0 else []),
                                inc=(1 if k == KC - 1 else 0),
                                tag=f"{kind}{g}_{e}_{f}_{t4}")
                        if kind == "g":
                            tv = p.on("scalar", lambda _e, f=f, t4=t4, ps=ps: _e.activation(
                                h_fm[f][:, t4 * 512:(t4 + 1) * 512], ps[:, :], AF.Silu),
                                after=[last], tag=f"sl{g}_{e}_{f}_{t4}")
                        else:
                            tv = p.on("vector", lambda _e, f=f, t4=t4, ps=ps: _e.tensor_tensor(
                                h_fm[f][:, t4 * 512:(t4 + 1) * 512],
                                h_fm[f][:, t4 * 512:(t4 + 1) * 512], ps[:, :], ALU.mult),
                                after=[last] + h_g, tag=f"h{g}_{e}_{f}_{t4}")
                        ps_gu.consumed(b, tv)
                        cons.append(tv)
                if kind == "g":
                    h_g = list(cons)
                    if debug and g == NG - 1 and e == 0:
                        p.on("sync", lambda _e: _e.dma_start(out=dbg["dbg_g0"],
                             in_=h_fm[0][:, :]), after=h_g, inc=16)
                else:
                    h_done = list(cons)
                    if debug and g == NG - 1 and e == 0:
                        p.on("sync", lambda _e: _e.dma_start(out=dbg["dbg_h0"],
                             in_=h_fm[0][:, :]), after=h_done, inc=16)
            else:
                for i in range(NT):
                    for dg in range(D // 512):
                        ps, war, b = ps_d.get()
                        last = None
                        for fk in range(FC):
                            aft = [wt] + h_done + ([war] if war and fk == 0 else [])
                            last = p.on("tensor", lambda _e, fk=fk, i=i, dg=dg, ps=ps, w=w: _e.matmul(
                                ps[:, :], h_fm[fk][:, i * 128:(i + 1) * 128],
                                w[:, fk * 1024 + dg * 512:fk * 1024 + (dg + 1) * 512],
                                start=(fk == 0), stop=(fk == FC - 1)),
                                after=(aft if fk == 0 else []),
                                inc=(1 if fk == FC - 1 else 0), tag=f"d{g}_{e}_{i}_{dg}")
                        tv = p.on("vector", lambda _e, i=i, dg=dg, e_=e, ps=ps: _e.scalar_tensor_tensor(
                            acc[:, i * D + dg * 512:i * D + (dg + 1) * 512], ps[:, :],
                            wecols[:, i * E + e_:i * E + e_ + 1],
                            acc[:, i * D + dg * 512:i * D + (dg + 1) * 512],
                            ALU.mult, ALU.add), after=[last], tag=f"ac{g}_{e}_{i}_{dg}")
                        ps_d.consumed(b, tv)
                        cons.append(tv)
            wslot_war[slot] = cons

        acc_done = [cons[-1]]
        if debug and g == NG - 1:
            p.on("sync", lambda e: e.dma_start(out=dbg["dbg_ufm0b"], in_=u_fm[0][:, :]),
                 after=acc_done, inc=16)
            p.on("sync", lambda e: e.dma_start(out=dbg["dbg_acc"], in_=acc[:, :]),
                 after=acc_done, inc=16)
        for i in range(NT):
            sl = i % 2
            tx = p.on("sync", lambda e, i=i, sl=sl, g=g: e.dma_start(
                out=xt[sl][:, :], in_=Xd[(g * NT + i) * 128:(g * NT + i + 1) * 128, :]),
                after=[xt_war[sl]] if xt_war[sl] else [], dma_chan=f"xt{sl}",
                tag=f"ldx2{g}_{i}")
            ty = p.on("vector", lambda e, i=i, sl=sl: e.scalar_tensor_tensor(
                xt[sl][:, :], acc[:, i * D:(i + 1) * D], 0.5, xt[sl][:, :],
                ALU.mult, ALU.add), after=[tx] + acc_done, tag=f"xo{g}_{i}")
            ts_ = p.on("sync", lambda e, i=i, sl=sl, g=g: e.dma_start(
                out=XOd[(g * NT + i) * 128:(g * NT + i + 1) * 128, :], in_=xt[sl][:, :]),
                after=[ty], dma_chan=f"st{sl}", tag=f"stx{g}_{i}")
            xt_war[sl] = ts_

    return p


L1ATTN_W = ["Wq", "Wk", "Wv", "Wo", "W1", "W2", "bqc", "bkr", "bvr", "bor",
            "b1c", "b2t", "g1t", "be1t", "g2t", "be2t", "idn"]


def emit_l1attn(p, nc, ctx, Xd, XOd, wd, T=4096, S=2048, stop_after=None):
    H, DH, FF = 16, 64, 4096
    KC = D // 128
    NT = S // 128
    NSEQ = T // S
    NSL = 8
    FSL = FF // NSL
    NHALF = NT // 2
    (Wqd, Wkd, Wvd, Wod, W1d, W2d, bqc, bkr, bvr, bor, b1c, b2t, g1t, be1t,
     g2t, be2t, idn) = (wd[k] for k in L1ATTN_W)

    st = St()
    st.idbf = alloc(ctx, nc, "idbf", [128, 128], BF16)
    st.dump = alloc(ctx, nc, "dump", [128, D], F32)
    st.dump2 = alloc(ctx, nc, "dump2", [128, D], F32)
    st.lntmp = st.dump2
    st.epscol = alloc(ctx, nc, "epscol", [128, 1], F32)
    for nm in ("s1", "s2", "mcol", "vcol", "rcol"):
        setattr(st, nm, alloc(ctx, nc, nm, [128, NT], F32))

    S0 = alloc(ctx, nc, "S0", [128, NT * D], BF16)
    S1 = alloc(ctx, nc, "S1", [128, NT * D], BF16)
    S2 = alloc(ctx, nc, "S2", [128, NT * D], BF16)
    S3 = alloc(ctx, nc, "S3", [128, NT * D], BF16)
    y_acc = S3[:, :].bitcast(F32)
    wsl = [alloc(ctx, nc, f"wsl{i}", [128, KC * D], BF16) for i in range(2)]
    xt = [alloc(ctx, nc, f"xt{i}", [128, D], F32) for i in range(2)]
    scr = [alloc(ctx, nc, "scr0", [128, 512], F32), st.dump2[:, 0:512]]
    kt = alloc(ctx, nc, "kt", [128, D], BF16)
    outt = kt
    vt = alloc(ctx, nc, "vt", [128, D], BF16)
    kv_acc = alloc(ctx, nc, "kv_acc", [128, 512], F32)
    ks_acc = alloc(ctx, nc, "ks_acc", [128, 8], F32)
    ztile = alloc(ctx, nc, "ztile", [128, 16], F32)
    kvsb = alloc(ctx, nc, "kvsb", [128, 1024], BF16)
    ksb = alloc(ctx, nc, "ksb", [128, 16], BF16)
    bq1 = alloc(ctx, nc, "bq1", [128, KC], F32)
    b1cs = alloc(ctx, nc, "b1cs", [128, FF // 128], F32)
    rows = alloc(ctx, nc, "rows", [1, 3 * D], BF16)
    onec = alloc(ctx, nc, "onec", [128, 1], BF16)
    oner = alloc(ctx, nc, "oner", [1, 128], BF16)
    neg1 = alloc(ctx, nc, "neg1", [128, 1], F32)
    b2b = alloc(ctx, nc, "b2b", [128, D], BF16)
    g1b = alloc(ctx, nc, "g1b", [128, D], F32)
    be1b = alloc(ctx, nc, "be1b", [128, D], BF16)
    g2b = alloc(ctx, nc, "g2b", [128, D], F32)
    be2b = alloc(ctx, nc, "be2b", [128, D], BF16)

    ps_tr = PsumRing(ctx, nc, "pstr", 2, 512, BF16)
    ps_a = PsumRing(ctx, nc, "psa", 2, 512, F32)
    ps_b = PsumRing(ctx, nc, "psb", 2, 512, F32)
    kv_ps = alloc_ps(ctx, nc, "kvps", [128, 512], F32)
    ks_ps = alloc_ps(ctx, nc, "ksps", [128, 8], F32)

    p.on("sync", lambda e: e.dma_start(out=st.idbf[:, :], in_=idn), dma_chan="init")
    for dst, srcd in [(g1b, g1t), (g2b, g2t)]:
        p.on("sync", lambda e, dst=dst, srcd=srcd: e.dma_start(out=dst[:, :], in_=srcd),
             dma_chan="init")
    for dst, srcd in [(b2b, b2t), (be1b, be1t), (be2b, be2t)]:
        p.on("gpsimd", lambda e, dst=dst, srcd=srcd: e.dma_start(out=dst[:, :], in_=srcd),
             dma_chan="initg")
    p.on("sync", lambda e: e.dma_start(out=bq1[:, :], in_=bqc), dma_chan="init")
    p.on("sync", lambda e: e.dma_start(out=b1cs[:, :], in_=b1c), dma_chan="init")
    p.on("sync", lambda e: e.dma_start(out=st.dump[0:1, :], in_=bkr), dma_chan="init")
    p.on("gpsimd", lambda e: e.dma_start(out=rows[:, D:2 * D], in_=bvr),
         dma_chan="initg")
    p.on("gpsimd", lambda e: e.dma_start(out=rows[:, 2 * D:3 * D], in_=bor),
         dma_chan="initg")
    init_done = p.chan_total("init")
    initg_done = p.chan_total("initg")
    tid = tb0 = tb1 = init_done
    p.on("vector", lambda e: e.memset(st.epscol[:, :], 1e-5))
    p.on("vector", lambda e: e.memset(onec[:, :], 1.0))
    p.on("vector", lambda e: e.memset(oner[:, :], 1.0))
    tn1 = p.on("vector", lambda e: e.memset(neg1[:, :], -1.0))
    tbq = p.on("vector", lambda e: e.tensor_scalar_add(bq1[:, :], bq1[:, :], 1.0),
               after=[init_done])
    tk0 = p.on("vector", lambda e: e.tensor_scalar_add(rows[:, 0:D], st.dump[0:1, :], 1.0),
               after=[init_done])
    rows_done = [tk0, init_done, initg_done]

    wwar = {0: [], 1: []}

    def wload(slot, srcd, r0, rch, c0, cn, tg):
        return p.on("gpsimd", lambda e, slot=slot, srcd=srcd, r0=r0, rch=rch,
                    c0=c0, cn=cn: e.dma_start(
            out=wsl[slot][:, :rch * cn].rearrange("p (c n) -> p c n", c=rch),
            in_=srcd[r0 * 128:(r0 + rch) * 128, c0:c0 + cn].rearrange(
                "(c p) n -> p c n", p=128)), dma_chan=f"w{slot}",
            after=wwar[slot], tag=tg)

    xt_war = [None, None]
    kvsb_war = None

    for s in range(NSEQ):
        ut_toks = []
        for i in range(NT):
            sl = i % 2
            tx = p.on("sync", lambda e, s=s, i=i, sl=sl: e.dma_start(
                out=xt[sl][:, :], in_=Xd[(s * NT + i) * 128:(s * NT + i + 1) * 128, :]),
                after=([xt_war[sl]] if xt_war[sl] else []), dma_chan=f"xt{sl}",
                tag=f"lx{s}_{i}")
            st.ln_after = [tx]
            tu = emit_ln_tile(p, st, xt[sl][:, :], S0[:, i * D:(i + 1) * D], i,
                              tag=f"ln{s}_{i}")
            xt_war[sl] = tu
            ut_toks.append(tu)

        ufm = [S1[:, k * NT * 128:(k + 1) * NT * 128] for k in range(KC)]
        ufm_done = emit_transpose(p, st, lambda i: S0[:, i * D:(i + 1) * D],
                                  ufm, NT, KC, ut_toks + [tid], ps_tr, tag=f"utr{s}")

        if stop_after == "tr":
            break
        wq = wload(0, Wqd, 0, KC, 0, D, f"wq{s}")
        wk = wload(1, Wkd, 0, KC, 0, D, f"wk{s}")
        qfm = [S2[:, k * NT * 128:(k + 1) * NT * 128] for k in range(KC)]
        cons_q = []
        prev_t4 = None
        for f in range(KC):
            for tg in range(NT * 128 // 512):
                ps, war, b = ps_a.get()
                last = None
                for k in range(KC):
                    aft = [wq] + ufm_done + ([war] if war and k == 0 else [])
                    last = p.on("tensor", lambda e, k=k, f=f, tg=tg, ps=ps: e.matmul(
                        ps[:, :], wsl[0][:, k * D + f * 128:k * D + (f + 1) * 128],
                        ufm[k][:, tg * 512:(tg + 1) * 512],
                        start=(k == 0), stop=(k == KC - 1)),
                        after=(aft if k == 0 else []), inc=(1 if k == KC - 1 else 0),
                        tag=f"q{s}_{f}_{tg}")
                t1 = p.on("vector", lambda e, f=f, ps=ps: e.tensor_scalar_add(
                    scr[0][:, :], ps[:, :], bq1[:, f:f + 1]),
                    after=[last, tbq] + ([prev_t4] if prev_t4 else []),
                    tag=f"qa{s}_{f}_{tg}")
                t2 = p.on("vector", lambda e: e.tensor_scalar_min(
                    scr[1][:, :], scr[0][:, :], 1.0), after=[t1])
                t3 = p.on("scalar", lambda e: e.activation(
                    scr[1][:, :], scr[1][:, :], AF.Exp, bias=neg1[:, 0:1]),
                    after=[t2, tn1])
                t4 = p.on("vector", lambda e, f=f, tg=tg: e.tensor_max(
                    qfm[f][:, tg * 512:(tg + 1) * 512], scr[0][:, :], scr[1][:, :]),
                    after=[t3], tag=f"qm{s}_{f}_{tg}")
                ps_a.consumed(b, t4)
                prev_t4 = t4
                cons_q.append(t4)
        wwar[0] = cons_q

        if stop_after == "q":
            break
        wv = wload(0, Wvd, 0, KC, 0, D, f"wv{s}")
        kv_zero = p.on("vector", lambda e: e.memset(kv_acc[:, :], 0.0), tag=f"kvz{s}")
        ks_zero = p.on("vector", lambda e: e.memset(ks_acc[:, :], 0.0), tag=f"ksz{s}")
        last_kv = None
        last_ks = None
        kv_evac = None
        ks_evac = None
        kt_war = None
        vt_war = None
        for i in range(NT):
            k_toks = []
            for fg in range(2):
                ps, war, b = ps_b.get()
                last = None
                for k in range(KC):
                    aft = [wk] + ufm_done + rows_done + ([war] if war and k == 0 else [])
                    last = p.on("tensor", lambda e, k=k, i=i, fg=fg, ps=ps: e.matmul(
                        ps[:, :], ufm[k][:, i * 128:(i + 1) * 128],
                        wsl[1][:, k * D + fg * 512:k * D + (fg + 1) * 512],
                        start=(k == 0), stop=False),
                        after=(aft if k == 0 else []), inc=0, tag=f"k{s}_{i}_{fg}")
                last = p.on("tensor", lambda e, fg=fg, ps=ps: e.matmul(
                    ps[:, :], oner[0:1, :], rows[:, fg * 512:(fg + 1) * 512],
                    start=False, stop=True), inc=1, tag=f"kb{s}_{i}_{fg}")
                t2 = p.on("vector", lambda e, ps=ps: e.tensor_scalar_min(
                    scr[0][:, :], ps[:, :], 1.0),
                    after=[last] + ([kt_war] if kt_war and fg == 0 else []))
                t3 = p.on("scalar", lambda e: e.activation(
                    scr[0][:, :], scr[0][:, :], AF.Exp, bias=neg1[:, 0:1]),
                    after=[t2, tn1])
                t4 = p.on("vector", lambda e, fg=fg, ps=ps: e.tensor_tensor(
                    kt[:, fg * 512:(fg + 1) * 512], ps[:, :], scr[0][:, :], ALU.max),
                    after=[t3], tag=f"km{s}_{i}_{fg}")
                ps_b.consumed(b, t4)
                k_toks.append(t4)
            v_toks = []
            for fg in range(2):
                ps, war, b = ps_b.get()
                last = None
                for k in range(KC):
                    aft = [wv] + ufm_done + rows_done + ([war] if war and k == 0 else [])
                    last = p.on("tensor", lambda e, k=k, i=i, fg=fg, ps=ps: e.matmul(
                        ps[:, :], ufm[k][:, i * 128:(i + 1) * 128],
                        wsl[0][:, k * D + fg * 512:k * D + (fg + 1) * 512],
                        start=(k == 0), stop=False),
                        after=(aft if k == 0 else []), inc=0, tag=f"v{s}_{i}_{fg}")
                last = p.on("tensor", lambda e, fg=fg, ps=ps: e.matmul(
                    ps[:, :], oner[0:1, :], rows[:, D + fg * 512:D + (fg + 1) * 512],
                    start=False, stop=True), inc=1, tag=f"vb{s}_{i}_{fg}")
                t4 = p.on("scalar", lambda e, fg=fg, ps=ps: e.activation(
                    vt[:, fg * 512:(fg + 1) * 512], ps[:, :], AF.Copy),
                    after=[last] + ([vt_war] if vt_war and fg == 0 else []),
                    tag=f"vc{s}_{i}_{fg}")
                ps_b.consumed(b, t4)
                v_toks.append(t4)
            for h in range(H):
                hc, hp = h // 2, (h % 2) * 64
                aft = (k_toks + v_toks + ([kv_evac] if i else [kv_zero])) if h == 0 else []
                last_kv = p.on("tensor", lambda e, h=h, hc=hc, hp=hp: e.matmul(
                    kv_ps[hp:hp + 64, hc * 64:(hc + 1) * 64],
                    kt[:, h * 64:(h + 1) * 64], vt[:, h * 64:(h + 1) * 64],
                    start=True, stop=True),
                    after=aft, inc=(1 if h == H - 1 else 0), tag=f"kv{s}_{i}_{h}")
            for hc in range(KC):
                aft = (k_toks + ([ks_evac] if i else [ks_zero])) if hc == 0 else []
                last_ks = p.on("tensor", lambda e, hc=hc: e.matmul(
                    ks_ps[:, hc:hc + 1], kt[:, hc * 128:(hc + 1) * 128], onec[:, 0:1],
                    start=True, stop=True),
                    after=aft, inc=(1 if hc == KC - 1 else 0), tag=f"ks{s}_{i}_{hc}")
            kv_evac = p.on("vector", lambda e: e.tensor_add(
                kv_acc[:, :], kv_acc[:, :], kv_ps[:, :]), after=[last_kv],
                tag=f"kva{s}_{i}")
            ks_evac = p.on("vector", lambda e: e.tensor_add(
                ks_acc[:, :], ks_acc[:, :], ks_ps[:, :]), after=[last_ks],
                tag=f"ksa{s}_{i}")
            kt_war = last_ks
            vt_war = last_kv
        p.on("vector", lambda e: e.memset(kvsb[:, :], 0.0),
             after=[kvsb_war] if kvsb_war else [], tag=f"kvz2{s}")
        p.on("vector", lambda e: e.memset(ksb[:, :], 0.0), tag=f"ksz2{s}")
        kv_cp = []
        for hc in range(KC):
            kv_cp.append(p.on("vector", lambda e, hc=hc: e.tensor_copy(
                kvsb[0:64, hc * 128:hc * 128 + 64],
                kv_acc[0:64, hc * 64:(hc + 1) * 64]), after=[kv_evac], tag=f"kbd{s}_{hc}"))
            kv_cp.append(p.on("vector", lambda e, hc=hc: e.tensor_copy(
                kvsb[64:128, hc * 128 + 64:(hc + 1) * 128],
                kv_acc[64:128, hc * 64:(hc + 1) * 64]), tag=f"kbd2{s}_{hc}"))
            kv_cp.append(p.on("vector", lambda e, hc=hc: e.tensor_copy(
                ksb[0:64, 2 * hc:2 * hc + 1], ks_acc[0:64, hc:hc + 1]),
                after=[ks_evac], tag=f"sbd{s}_{hc}"))
            kv_cp.append(p.on("vector", lambda e, hc=hc: e.tensor_copy(
                ksb[64:128, 2 * hc + 1:2 * hc + 2], ks_acc[64:128, hc:hc + 1]),
                tag=f"sbd2{s}_{hc}"))
        tkv = tks = kv_cp[-1]

        if stop_after == "kv":
            break
        zinv_ps = ps_a.tiles[0]
        z_war = ps_a.war[0]
        pso = [ps_b.tiles[0], ps_b.tiles[1]]
        pso_war = [ps_b.war[0], ps_b.war[1]]
        osc_toks = []
        for i in range(NT):
            lastz = None
            for hc in range(KC):
                aft = (cons_q + [tks] + ([z_war] if z_war else [])) if hc == 0 else []
                lastz = p.on("tensor", lambda e, i=i, hc=hc: e.matmul(
                    zinv_ps[:, 2 * hc:2 * hc + 2],
                    qfm[hc][:, i * 128:(i + 1) * 128],
                    ksb[:, 2 * hc:2 * hc + 2],
                    start=True, stop=True),
                    after=aft, inc=(1 if hc == KC - 1 else 0), tag=f"zi{s}_{i}_{hc}")
            tz1 = p.on("vector", lambda e: e.tensor_scalar_add(
                ztile[:, :], zinv_ps[:, :16], 1e-6), after=[lastz], tag=f"z1{s}_{i}")
            tz2 = p.on("vector", lambda e: e.reciprocal(ztile[:, :], ztile[:, :]),
                       after=[tz1], tag=f"z2{s}_{i}")
            z_war = tz2
            lasto = [None, None]
            for hc in range(KC):
                bk_ = hc // 4
                aft = []
                if hc == 0:
                    aft = [tkv] + cons_q + [t for t in pso_war if t]
                lasto[bk_] = p.on("tensor", lambda e, i=i, hc=hc, bk_=bk_: e.matmul(
                    pso[bk_][:, (hc % 4) * 128:(hc % 4 + 1) * 128],
                    qfm[hc][:, i * 128:(i + 1) * 128],
                    kvsb[:, hc * 128:(hc + 1) * 128],
                    start=True, stop=True),
                    after=aft, inc=(1 if hc % 4 == 3 else 0), tag=f"o{s}_{i}_{hc}")
            tsc = None
            for bk_ in range(2):
                tsc = p.on("vector", lambda e, i=i, bk_=bk_: e.tensor_mul(
                    S1[:, i * D + bk_ * 512:i * D + (bk_ + 1) * 512].rearrange(
                        "p (h m) -> p h m", m=64),
                    pso[bk_][:, :].rearrange("p (h m) -> p h m", m=64),
                    ztile[:, bk_ * 8:(bk_ + 1) * 8].unsqueeze(2).broadcast_to(
                        [128, 8, 64])), after=[lasto[bk_], tz2], tag=f"os{s}_{i}_{bk_}")
                pso_war[bk_] = tsc
            osc_toks.append(tsc)
        ps_a.war[0] = z_war
        ps_b.war[0] = pso_war[0]
        ps_b.war[1] = pso_war[1]
        kvsb_war = osc_toks[-1]

        if stop_after == "attn":
            break
        wwar[0] = [tkv]
        wwar[1] = [tkv, tks]
        wo = wload(1, Wod, 0, KC, 0, D, f"wo{s}")
        ofm = [S2[:, k * NT * 128:(k + 1) * NT * 128] for k in range(KC)]
        ofm_done = emit_transpose(p, st, lambda i: S1[:, i * D:(i + 1) * D],
                                  ofm, NT, KC, osc_toks + [tid], ps_tr, tag=f"otr{s}")
        if stop_after == "ofm":
            break
        x1_toks = []
        for i in range(NT):
            evs = []
            for fg in range(D // 512):
                ps, war, b = ps_a.get()
                last = None
                for k in range(KC):
                    aft = ofm_done + [wo] + ([war] if war and k == 0 else [])
                    last = p.on("tensor", lambda e, k=k, i=i, fg=fg, ps=ps: e.matmul(
                        ps[:, :], ofm[k][:, i * 128:(i + 1) * 128],
                        wsl[1][:, k * D + fg * 512:k * D + (fg + 1) * 512],
                        start=(k == 0), stop=False),
                        after=(aft if k == 0 else []), inc=0, tag=f"wo{s}_{i}_{fg}")
                last = p.on("tensor", lambda e, i=i, fg=fg, ps=ps: e.matmul(
                    ps[:, :], oner[0:1, :], rows[:, 2 * D + fg * 512:2 * D + (fg + 1) * 512],
                    start=False, stop=True), inc=1, tag=f"wob{s}_{i}_{fg}")
                ev = p.on("vector", lambda e, i=i, fg=fg, ps=ps: e.tensor_add(
                    st.dump[:, fg * 512:(fg + 1) * 512], ps[:, :],
                    S0[:, i * D + fg * 512:i * D + (fg + 1) * 512]),
                    after=[last], tag=f"x1r{s}_{i}_{fg}")
                ps_a.consumed(b, ev)
                evs.append(ev)
            st.ln_after = evs
            tu = emit_ln_tile(p, st, st.dump[:, :D], S0[:, i * D:(i + 1) * D], i,
                              tag=f"l1n{s}_{i}", gb=g1b[:, :], bb=be1b[:, :])
            x1_toks.append(tu)
        wwar[1] = x1_toks

        if stop_after == "x1":
            break
        x1fm = [S2[:, k * NT * 128:(k + 1) * NT * 128] for k in range(KC)]
        x1fm_done = emit_transpose(p, st, lambda i: S0[:, i * D:(i + 1) * D],
                                   x1fm, NT, KC, x1_toks + [tid], ps_tr, tag=f"x1tr{s}")

        if stop_after == "x1fm":
            break
        h1ch = FSL // 128
        HT = NHALF * 128
        for hf in range(2):
            i0 = hf * NHALF
            tza = p.on("vector", lambda e: e.memset(y_acc[:, :], 0.0), tag=f"ya{s}_{hf}")
            yacc_done = [tza]
            def wpair(slot, sl_, tg):
                t1_ = p.on("gpsimd", lambda e, slot=slot, sl_=sl_: e.dma_start(
                    out=wsl[slot][:, :KC * FSL].rearrange("p (c n) -> p c n", c=KC),
                    in_=W1d[:, sl_ * FSL:(sl_ + 1) * FSL].rearrange(
                        "(c p) n -> p c n", p=128)), dma_chan=f"w{slot}",
                    after=wwar[slot], tag=f"w1{tg}")
                t2_ = p.on("gpsimd", lambda e, slot=slot, sl_=sl_: e.dma_start(
                    out=wsl[slot][:, KC * FSL:KC * FSL + h1ch * D].rearrange(
                        "p (c n) -> p c n", c=h1ch),
                    in_=W2d[sl_ * h1ch * 128:(sl_ + 1) * h1ch * 128, :].rearrange(
                        "(c p) n -> p c n", p=128)), dma_chan=f"w{slot}", tag=f"w2{tg}")
                return (t1_, t2_)

            prev_w = wpair(0, 0, f"{s}{hf}_0")
            y_toks = []
            for sl_ in range(NSL):
                slot = sl_ % 2
                wt_ = prev_w
                if sl_ + 1 < NSL:
                    prev_w = wpair((sl_ + 1) % 2, sl_ + 1, f"{s}{hf}_{sl_+1}")
                cons_h = []
                for fc in range(h1ch):
                    for tg in range(NHALF * 128 // 512):
                        ps, war, b = ps_a.get()
                        last = None
                        for k in range(KC):
                            aft = [wt_[0]] + x1fm_done + ([war] if war and k == 0 else [])
                            last = p.on("tensor", lambda e, k=k, fc=fc, tg=tg, ps=ps,
                                        i0=i0, slot=slot: e.matmul(
                                ps[:, :],
                                wsl[slot][:, k * FSL + fc * 128:k * FSL + (fc + 1) * 128],
                                x1fm[k][:, i0 * 128 + tg * 512:i0 * 128 + (tg + 1) * 512],
                                start=(k == 0), stop=(k == KC - 1)),
                                after=(aft if k == 0 else []),
                                inc=(1 if k == KC - 1 else 0),
                                tag=f"h1{s}_{hf}_{sl_}_{fc}_{tg}")
                        ev = p.on("scalar", lambda e, fc=fc, tg=tg, sl_=sl_, ps=ps,
                                  HT=HT: e.activation(
                            S1[:, fc * HT + tg * 512:fc * HT + (tg + 1) * 512],
                            ps[:, :], AF.Gelu,
                            bias=b1cs[:, sl_ * h1ch + fc:sl_ * h1ch + fc + 1]),
                            after=[last, tb1], tag=f"ge{s}_{hf}_{sl_}_{fc}_{tg}")
                        ps_a.consumed(b, ev)
                        cons_h.append(ev)
                cons_y = []
                for ii in range(NHALF):
                    for dg in range(D // 512):
                        ps, war, b = ps_b.get()
                        last = None
                        for fk in range(h1ch):
                            aft = [wt_[1]] + cons_h + ([war] if war and fk == 0 else [])
                            last = p.on("tensor", lambda e, fk=fk, ii=ii, dg=dg, ps=ps,
                                        HT=HT, slot=slot: e.matmul(
                                ps[:, :], S1[:, fk * HT + ii * 128:fk * HT + (ii + 1) * 128],
                                wsl[slot][:, KC * FSL + fk * D + dg * 512:
                                          KC * FSL + fk * D + (dg + 1) * 512],
                                start=(fk == 0), stop=(fk == h1ch - 1)),
                                after=(aft if fk == 0 else []),
                                inc=(1 if fk == h1ch - 1 else 0),
                                tag=f"y{s}_{hf}_{sl_}_{ii}_{dg}")
                        ev = p.on("vector", lambda e, ii=ii, dg=dg, ps=ps: e.tensor_add(
                            y_acc[:, ii * D + dg * 512:ii * D + (dg + 1) * 512],
                            y_acc[:, ii * D + dg * 512:ii * D + (dg + 1) * 512],
                            ps[:, :]), after=[last] + yacc_done,
                            tag=f"ya{s}_{hf}_{sl_}_{ii}_{dg}")
                        ps_b.consumed(b, ev)
                        cons_y.append(ev)
                wwar[slot] = cons_h + cons_y
                y_toks = cons_y

            for ii in range(NHALF):
                i = i0 + ii
                sl = i % 2
                t1 = p.on("vector", lambda e, ii=ii: e.tensor_add(
                    st.dump[:, :512], y_acc[:, ii * D:ii * D + 512], b2b[:, :512]),
                    after=y_toks, tag=f"x2a{s}_{i}")
                t1b = p.on("vector", lambda e, ii=ii: e.tensor_add(
                    st.dump[:, 512:1024], y_acc[:, ii * D + 512:(ii + 1) * D],
                    b2b[:, 512:]), tag=f"x2b{s}_{i}")
                t2 = p.on("vector", lambda e, i=i: e.tensor_add(
                    st.dump[:, :D], st.dump[:, :D], S0[:, i * D:(i + 1) * D]),
                    after=[t1, t1b], tag=f"x2c{s}_{i}")
                st.ln_after = [t2]
                tu = emit_ln_tile(p, st, st.dump[:, :D], outt[:, :], i,
                                  tag=f"l2n{s}_{i}", gb=g2b[:, :], bb=be2b[:, :])
                tx = p.on("sync", lambda e, s=s, i=i, sl=sl: e.dma_start(
                    out=xt[sl][:, :], in_=Xd[(s * NT + i) * 128:(s * NT + i + 1) * 128, :]),
                    after=([xt_war[sl]] if xt_war[sl] else []), dma_chan=f"xt{sl}",
                    tag=f"lx2{s}_{i}")
                ty = p.on("vector", lambda e, sl=sl: e.tensor_add(
                    xt[sl][:, :], xt[sl][:, :], outt[:, :]),
                    after=[tx, tu], tag=f"xo{s}_{i}")
                ts_ = p.on("sync", lambda e, s=s, i=i, sl=sl: e.dma_start(
                    out=XOd[(s * NT + i) * 128:(s * NT + i + 1) * 128, :], in_=xt[sl][:, :]),
                    after=[ty], dma_chan=f"st{sl}", tag=f"sx{s}_{i}")
                xt_war[sl] = ts_

    return p


L0FFN_W = ["rT", "br", "idn", "idf", "W1", "W2", "b1c", "b2"]


def emit_l0ffn(p, nc, ctx, Xd, XOd, wd, T=4096, TG=1024, mode="full", C=4096,
               CNTd=None, cntd=None, prefd=None):
    E = 8
    KC = D // 128
    NT = TG // 128
    NG = T // TG
    NTT = T // 128
    rTd, brd, idnd, idfd = wd["rT"], wd["br"], wd["idn"], wd["idf"]
    if mode == "full":
        W1d, W2d, b1cd, b2d = wd["W1"], wd["W2"], wd["b1c"], wd["b2"]

    st = St()
    st.idbf = alloc(ctx, nc, "idbf", [128, 128], BF16)
    idf = alloc(ctx, nc, "idf_sb", [128, 128], F32)
    st.dump = alloc(ctx, nc, "dump", [128, D], F32)
    st.dump2 = alloc(ctx, nc, "dump2", [128, D], F32)
    st.epscol = alloc(ctx, nc, "epscol", [128, 1], F32)
    for nm in ("s1", "s2", "mcol", "vcol", "rcol"):
        setattr(st, nm, alloc(ctx, nc, nm, [128, NT], F32))

    xt = [alloc(ctx, nc, f"xt{i}", [128, D], F32) for i in range(2)]
    u_tm = alloc(ctx, nc, "u_tm", [128, NT * D], BF16)
    u_fm = [alloc(ctx, nc, f"ufm{k}", [128, TG], BF16) for k in range(KC)]
    rbf = alloc(ctx, nc, "rbf", [128, KC * E], BF16)
    brr = alloc(ctx, nc, "brr", [1, E], BF16)
    oh_fm = alloc(ctx, nc, "oh_fm", [8, NTT * 128], F32)
    onehot_tm = alloc(ctx, nc, "onehot_tm", [128, NTT * E], F32)
    gates = alloc(ctx, nc, "gates", [128, NTT], F32)
    cntloc = alloc(ctx, nc, "cntloc", [8, 1], F32)
    oner8 = alloc(ctx, nc, "oner8", [1, 128], BF16)
    prob = alloc(ctx, nc, "prob", [128, E], F32)
    eq = alloc(ctx, nc, "eq", [128, E], F32)
    cum = alloc(ctx, nc, "cum", [128, E], F32)
    zer8 = alloc(ctx, nc, "zer8", [128, E], F32)
    c1 = alloc(ctx, nc, "c1", [128, 4], F32)
    if mode == "full":
        h_fm = [alloc(ctx, nc, f"hfm{f}", [128, TG], BF16) for f in range(KC)]
        acc = alloc(ctx, nc, "acc", [128, NT * D], F32)
        wslot = [alloc(ctx, nc, f"w{i}", [128, 16384], BF16) for i in range(2)]
        pos_fm = alloc(ctx, nc, "pos_fm", [8, NTT * 128], F32)
        ohkb = alloc(ctx, nc, "ohkb", [8, NTT * 128], BF16)
        cnt_sb = alloc(ctx, nc, "cnt_sb", [8, E], F32)
        pref_sb = alloc(ctx, nc, "pref_sb", [8, 1], F32)
        base_sb = alloc(ctx, nc, "base_sb", [8, 1], F32)
        b2sb = alloc(ctx, nc, "b2sb", [8, D], BF16)
        b1cs = alloc(ctx, nc, "b1cs", [128, E * KC], F32)
        ones8f = alloc(ctx, nc, "ones8f", [8, 1], F32)
        zcol = alloc(ctx, nc, "zcol", [8, 1], F32)
        kg = alloc(ctx, nc, "kg", [128, NTT], F32)

    ps_tr = PsumRing(ctx, nc, "pstr", 2, 512, BF16)
    ps_oh = PsumRing(ctx, nc, "psoh", 1, 512, F32)
    ps_r = PsumRing(ctx, nc, "psr", 1, 512, F32)
    ps_a = PsumRing(ctx, nc, "psa", 2, 512, F32)
    ps_b = PsumRing(ctx, nc, "psb", 2, 512, F32)

    p.on("sync", lambda e: e.dma_start(out=st.idbf[:, :], in_=idnd), dma_chan="init")
    p.on("sync", lambda e: e.dma_start(out=idf[:, :], in_=idfd), dma_chan="init")
    p.on("sync", lambda e: e.dma_start(out=st.dump[0:1, :E], in_=brd), dma_chan="init")
    p.on("gpsimd", lambda e: e.dma_start(
        out=rbf[:, :].rearrange("p (k e) -> p k e", k=KC),
        in_=rTd.rearrange("(k p) e -> p k e", p=128)), dma_chan="initg")
    if mode == "full":
        p.on("sync", lambda e: e.dma_start(out=b1cs[:, :], in_=b1cd), dma_chan="init")
        p.on("sync", lambda e: e.dma_start(out=cnt_sb[:, :], in_=cntd), dma_chan="init")
        p.on("sync", lambda e: e.dma_start(out=pref_sb[:, :], in_=prefd),
             dma_chan="init")
        p.on("gpsimd", lambda e: e.dma_start(out=b2sb[:, :], in_=b2d), dma_chan="initg")
    init_done = p.chan_total("init")
    initg_done = p.chan_total("initg")
    tid = init_done
    p.on("vector", lambda e: e.memset(st.epscol[:, :], 1e-5))
    p.on("vector", lambda e: e.memset(zer8[:, :], 0.0))
    p.on("vector", lambda e: e.memset(oner8[:, :], 1.0))
    if mode == "full":
        p.on("vector", lambda e: e.memset(zcol[:, :], 0.0))
        p.on("vector", lambda e: e.memset(ones8f[:, :], 1.0))
    tbr = p.on("vector", lambda e: e.tensor_copy(brr[:, :], st.dump[0:1, :E]),
               after=[init_done])

    wwar = {0: [], 1: []}
    xt_war = [None, None]

    oh_done = []
    prev_rt_tail = None
    for g in range(NG):
        ut_toks = []
        for i in range(NT):
            sl = i % 2
            tx = p.on("sync", lambda e, g=g, i=i, sl=sl: e.dma_start(
                out=xt[sl][:, :], in_=Xd[(g * NT + i) * 128:(g * NT + i + 1) * 128, :]),
                after=([xt_war[sl]] if xt_war[sl] else []), dma_chan=f"xt{sl}",
                tag=f"lx{g}_{i}")
            st.ln_after = [tx]
            tu = emit_ln_tile(p, st, xt[sl][:, :], u_tm[:, i * D:(i + 1) * D], i,
                              tag=f"ln{g}_{i}")
            xt_war[sl] = tu
            ut_toks.append(tu)
        ufm_done = emit_transpose(p, st, lambda i: u_tm[:, i * D:(i + 1) * D],
                                  u_fm, NT, KC, ut_toks + [tid], ps_tr, tag=f"tr{g}")
        for i in range(NT):
            ti_ = g * NT + i
            ps, war, b = ps_r.get()
            last = None
            for k in range(KC):
                aft = ufm_done + [initg_done] + ([war] if war and k == 0 else [])
                last = p.on("tensor", lambda e, k=k, i=i, ps=ps: e.matmul(
                    ps[:, :E], u_fm[k][:, i * 128:(i + 1) * 128],
                    rbf[:, k * E:(k + 1) * E],
                    start=(k == 0), stop=False),
                    after=(aft if k == 0 else []), inc=0, tag=f"rt{g}_{i}")
            last = p.on("tensor", lambda e, ps=ps: e.matmul(
                ps[:, :E], oner8[0:1, :], brr[:, :], start=False, stop=True),
                after=[tbr], inc=1, tag=f"rtb{g}_{i}")
            tm_ = p.on("vector", lambda e, ps=ps: e.reduce_max(
                c1[:, 0:1], ps[:, :E], AX.X, negate=True), after=[last])
            te = p.on("scalar", lambda e, ps=ps: e.activation(
                prob[:, :], ps[:, :E], AF.Exp, bias=c1[:, 0:1]),
                after=[tm_] + ([prev_rt_tail] if prev_rt_tail else []))
            ps_r.consumed(b, te)
            p.on("vector", lambda e: e.reduce_sum(c1[:, 1:2], prob[:, :], AX.X),
                 after=[te])
            p.on("vector", lambda e, ti_=ti_: e.reciprocal(
                gates[:, ti_:ti_ + 1], c1[:, 1:2]))
            t1_ = p.on("vector", lambda e: e.reduce_max(c1[:, 2:3], prob[:, :], AX.X))
            p.on("vector", lambda e: e.tensor_scalar(eq[:, :], prob[:, :], c1[:, 2:3],
                                                     None, ALU.is_ge), after=[t1_])
            p.on("vector", lambda e: e.tensor_tensor_scan(cum[:, :], eq[:, :],
                                                          zer8[:, :], 0.0, ALU.add,
                                                          ALU.add))
            p.on("vector", lambda e: e.tensor_scalar(cum[:, :], cum[:, :], 1.0, None,
                                                     ALU.is_equal))
            tq = p.on("vector", lambda e, ti_=ti_: e.tensor_mul(
                onehot_tm[:, ti_ * E:(ti_ + 1) * E], eq[:, :], cum[:, :]),
                tag=f"oh{g}_{i}")
            prev_rt_tail = tq
        for i0 in range(0, NT, 4):
            ps, war, b = ps_oh.get()
            lastt = None
            for j in range(4):
                ti_ = g * NT + i0 + j
                aft = ([prev_rt_tail] if j == 0 else []) + ([war] if war and j == 0 else [])
                lastt = p.on("tensor", lambda e, ti_=ti_, j=j, ps=ps: e.transpose(
                    ps[0:8, j * 128:(j + 1) * 128],
                    onehot_tm[:, ti_ * E:(ti_ + 1) * E], idf[:, :]),
                    after=aft, inc=(1 if j == 3 else 0), tag=f"oht{g}_{i0}_{j}")
            tv = p.on("vector", lambda e, g=g, i0=i0, ps=ps: e.tensor_copy(
                oh_fm[:, (g * NT + i0) * 128:(g * NT + i0 + 4) * 128], ps[0:8, :]),
                after=[lastt], tag=f"ohe{g}_{i0}")
            ps_oh.consumed(b, tv)
            oh_done.append(tv)

    tcnt = p.on("vector", lambda e: e.reduce_sum(cntloc[:, :], oh_fm[:, :], AX.X),
                after=oh_done, tag="cnt")
    if mode == "count":
        p.on("sync", lambda e: e.dma_start(out=CNTd, in_=cntloc[:, :]), after=[tcnt],
             dma_chan="st0")
        return p

    psb_, warb, bb_ = ps_oh.get()
    tb_ = p.on("tensor", lambda e: e.matmul(psb_[0:8, 0:1], cnt_sb[:, :], pref_sb[:, :],
                                            start=True, stop=True),
               after=[init_done] + ([warb] if warb else []), tag="base")
    tb_ = p.on("vector", lambda e: e.tensor_copy(base_sb[:, :], psb_[0:8, 0:1]),
               after=[tb_])
    ps_oh.consumed(bb_, tb_)
    tpos = p.on("vector", lambda e: e.tensor_tensor_scan(
        pos_fm[:, :], oh_fm[:, :], zcol[:, 0:1].broadcast_to([8, NTT * 128]),
        base_sb[:, 0:1], ALU.add, ALU.add), after=[tb_], tag="scan")
    tkeep = p.on("vector", lambda e: e.tensor_scalar(pos_fm[:, :], pos_fm[:, :],
                                                     float(C - 1) + 0.5, None,
                                                     ALU.is_le), after=[tpos], tag="keep")
    tohk = p.on("vector", lambda e: e.tensor_mul(oh_fm[:, :], oh_fm[:, :], pos_fm[:, :]),
                after=[tkeep], tag="ohk")
    tohkb = p.on("vector", lambda e: e.tensor_copy(ohkb[:, :], oh_fm[:, :]),
                 after=[tohk], tag="ohkb")

    sel_done = []
    for ti_ in range(NTT):
        ps, war, b = ps_r.get()
        tkk = p.on("tensor", lambda e, ti_=ti_, ps=ps: e.matmul(
            ps[:, 0:1], oh_fm[:, ti_ * 128:(ti_ + 1) * 128], ones8f[:, :],
            start=True, stop=True),
            after=[tohk] + ([war] if war else []), tag=f"kt{ti_}")
        tkg = p.on("vector", lambda e, ti_=ti_, ps=ps: e.tensor_tensor(
            kg[:, ti_:ti_ + 1], ps[:, 0:1], gates[:, ti_:ti_ + 1], ALU.mult),
            after=[tkk], tag=f"kg{ti_}")
        ps_r.consumed(b, tkg)
        tsl = p.on("vector", lambda e, ti_=ti_: e.tensor_scalar(
            onehot_tm[:, ti_ * E:(ti_ + 1) * E], onehot_tm[:, ti_ * E:(ti_ + 1) * E],
            kg[:, ti_:ti_ + 1], None, ALU.mult), tag=f"sel{ti_}")
        sel_done.append(tsl)

    def wpair(slot, e_, tg):
        t1_ = p.on("gpsimd", lambda e, slot=slot, e_=e_: e.dma_start(
            out=wslot[slot][:, :KC * D].rearrange("p (c n) -> p c n", c=KC),
            in_=W1d[e_].rearrange("(c p) n -> p c n", p=128)), dma_chan=f"w{slot}",
            after=wwar[slot], tag=f"w1{tg}")
        t2_ = p.on("gpsimd", lambda e, slot=slot, e_=e_: e.dma_start(
            out=wslot[slot][:, KC * D:2 * KC * D].rearrange("p (c n) -> p c n", c=KC),
            in_=W2d[e_].rearrange("(c p) n -> p c n", p=128)), dma_chan=f"w{slot}",
            tag=f"w2{tg}")
        return (t1_, t2_)

    for g in range(NG):
        ut_toks = []
        for i in range(NT):
            sl = i % 2
            ti_ = g * NT + i
            tx = p.on("sync", lambda e, g=g, i=i, sl=sl: e.dma_start(
                out=xt[sl][:, :], in_=Xd[(g * NT + i) * 128:(g * NT + i + 1) * 128, :]),
                after=([xt_war[sl]] if xt_war[sl] else []), dma_chan=f"xt{sl}",
                tag=f"lxc{g}_{i}")
            st.ln_after = [tx]
            tu = emit_ln_tile(p, st, xt[sl][:, :], u_tm[:, i * D:(i + 1) * D], i,
                              tag=f"lnc{g}_{i}")
            tu = p.on("vector", lambda e, i=i, ti_=ti_: e.tensor_scalar(
                u_tm[:, i * D:(i + 1) * D], u_tm[:, i * D:(i + 1) * D],
                gates[:, ti_:ti_ + 1], None, ALU.mult), after=[tu], tag=f"ug{g}_{i}")
            xt_war[sl] = tu
            ut_toks.append(tu)
        ufm_done = emit_transpose(p, st, lambda i: u_tm[:, i * D:(i + 1) * D],
                                  u_fm, NT, KC, ut_toks + [tid], ps_tr, tag=f"trc{g}")

        p.on("vector", lambda e: e.memset(acc[:, :], 0.0), tag=f"acc0{g}")
        prev_w = wpair(0, 0, f"{g}_0") if g == 0 else prev_w_next
        for e_ in range(E):
            slot = e_ % 2
            wt_ = prev_w
            if e_ + 1 < E:
                prev_w = wpair((e_ + 1) % 2, e_ + 1, f"{g}_{e_+1}")
            elif g + 1 < NG:
                prev_w_next = wpair((e_ + 1) % 2, 0, f"{g+1}_0")
            cons_h = []
            for f in range(KC):
                for tg4 in range(TG // 512):
                    ps, war, b = ps_a.get()
                    last = None
                    for k in range(KC):
                        aft = [wt_[0]] + ufm_done + ([war] if war and k == 0 else [])
                        last = p.on("tensor", lambda e, k=k, f=f, tg4=tg4, ps=ps,
                                    slot=slot: e.matmul(
                            ps[:, :], wslot[slot][:, k * D + f * 128:k * D + (f + 1) * 128],
                            u_fm[k][:, tg4 * 512:(tg4 + 1) * 512],
                            start=(k == 0), stop=(k == KC - 1)),
                            after=(aft if k == 0 else []), inc=(1 if k == KC - 1 else 0),
                            tag=f"h{g}_{e_}_{f}_{tg4}")
                    ev = p.on("scalar", lambda e, f=f, tg4=tg4, e_=e_, ps=ps: e.activation(
                        h_fm[f][:, tg4 * 512:(tg4 + 1) * 512], ps[:, :], AF.Relu,
                        bias=b1cs[:, e_ * KC + f:e_ * KC + f + 1]),
                        after=[last, init_done], tag=f"hr{g}_{e_}_{f}_{tg4}")
                    ps_a.consumed(b, ev)
                    cons_h.append(ev)
            cons_y = []
            for i in range(NT):
                ti_ = g * NT + i
                for dg in range(D // 512):
                    ps, war, b = ps_b.get()
                    last = None
                    for fk in range(KC):
                        aft = [wt_[1]] + cons_h + ([war] if war and fk == 0 else [])
                        last = p.on("tensor", lambda e, fk=fk, i=i, dg=dg, ps=ps,
                                    slot=slot: e.matmul(
                            ps[:, :], h_fm[fk][:, i * 128:(i + 1) * 128],
                            wslot[slot][:, KC * D + fk * D + dg * 512:
                                        KC * D + fk * D + (dg + 1) * 512],
                            start=(fk == 0), stop=(fk == KC - 1)),
                            after=(aft if fk == 0 else []),
                            inc=(1 if fk == KC - 1 else 0), tag=f"y{g}_{e_}_{i}_{dg}")
                    ev = p.on("vector", lambda e, i=i, dg=dg, ti_=ti_, e_=e_, ps=ps:
                              e.scalar_tensor_tensor(
                        acc[:, i * D + dg * 512:i * D + (dg + 1) * 512], ps[:, :],
                        onehot_tm[:, ti_ * E + e_:ti_ * E + e_ + 1],
                        acc[:, i * D + dg * 512:i * D + (dg + 1) * 512],
                        ALU.mult, ALU.add), after=[last] + sel_done,
                        tag=f"ac{g}_{e_}_{i}_{dg}")
                    ps_b.consumed(b, ev)
                    cons_y.append(ev)
            wwar[slot] = cons_h + cons_y

        for i in range(NT):
            ti_ = g * NT + i
            sl = i % 2
            evs = []
            for dg in range(D // 512):
                ps, war, b = ps_b.get()
                tbm = p.on("tensor", lambda e, ti_=ti_, dg=dg, ps=ps: e.matmul(
                    ps[:, :], ohkb[:, ti_ * 128:(ti_ + 1) * 128],
                    b2sb[:, dg * 512:(dg + 1) * 512], start=True, stop=True),
                    after=[tohkb, initg_done] + ([war] if war else []),
                    tag=f"b2{g}_{i}_{dg}")
                ev = p.on("vector", lambda e, i=i, dg=dg, ti_=ti_, ps=ps:
                          e.scalar_tensor_tensor(
                    acc[:, i * D + dg * 512:i * D + (dg + 1) * 512], ps[:, :],
                    gates[:, ti_:ti_ + 1],
                    acc[:, i * D + dg * 512:i * D + (dg + 1) * 512],
                    ALU.mult, ALU.add), after=[tbm], tag=f"b2e{g}_{i}_{dg}")
                ps_b.consumed(b, ev)
                evs.append(ev)
            tx = p.on("sync", lambda e, g=g, i=i, sl=sl: e.dma_start(
                out=xt[sl][:, :], in_=Xd[(g * NT + i) * 128:(g * NT + i + 1) * 128, :]),
                after=[xt_war[sl]] if xt_war[sl] else [], dma_chan=f"xt{sl}",
                tag=f"lxe{g}_{i}")
            ty = p.on("vector", lambda e, i=i, sl=sl: e.scalar_tensor_tensor(
                xt[sl][:, :], acc[:, i * D:(i + 1) * D], 0.5, xt[sl][:, :],
                ALU.mult, ALU.add), after=[tx] + evs, tag=f"xo{g}_{i}")
            ts_ = p.on("sync", lambda e, g=g, i=i, sl=sl: e.dma_start(
                out=XOd[(g * NT + i) * 128:(g * NT + i + 1) * 128, :], in_=xt[sl][:, :]),
                after=[ty], dma_chan=f"st{sl}", tag=f"sx{g}_{i}")
            xt_war[sl] = ts_

    return p


L0ATTN_W = ["nwt", "ipT", "convw", "convb", "dtb", "alog", "dvec", "gnwt",
            "zwT", "owT", "fT", "idn", "idf", "LTi", "UTs", "ALO", "SDG", "selc"]


def emit_l0attn(p, nc, ctx, Xd, XOd, wd, T=4096, S=2048):
    DI, DS, NH, HD = 2048, 128, 32, 64
    CH = 18
    KC = D // 128
    NSEQ = T // S
    GT = 512
    NGS = S // GT
    NC4 = GT // 128
    GW = 516
    NTS = S // 128
    (nwtd, ipTd, cwd, cbd, dtbd, alogd, dvecd, gnwd, zwTd, owTd, fTd, idnd,
     idfd, LTid, UTsd, ALOd, SDGd, seld) = (wd[k] for k in L0ATTN_W)

    st = St()
    st.idbf = alloc(ctx, nc, "idbf", [128, 128], BF16)
    idf = alloc(ctx, nc, "idf_sb", [128, 128], F32)
    st.dump = alloc(ctx, nc, "dump", [128, D], F32)
    st.dump2 = alloc(ctx, nc, "dump2", [128, DI], F32)
    st.epscol = alloc(ctx, nc, "epscol", [128, 1], F32)
    for nm in ("s1", "s2", "mcol", "vcol", "rcol"):
        setattr(st, nm, alloc(ctx, nc, nm, [128, 16], F32))
    onef = alloc(ctx, nc, "onef", [128, 1], F32)
    gst = alloc(ctx, nc, "gst", [128, 8], F32)

    xt = [alloc(ctx, nc, f"xt{i}", [128, D], F32) for i in range(2)]
    xn_tm = alloc(ctx, nc, "xn_tm", [128, D], BF16)
    xn_fm = [alloc(ctx, nc, f"xnfm{k}", [128, S], BF16) for k in range(KC)]
    nwt = alloc(ctx, nc, "nwt_sb", [128, D], F32)
    gnwt = alloc(ctx, nc, "gnwt_sb", [128, DI], BF16)
    LTi = alloc(ctx, nc, "LTi_sb", [128, 128], F32)
    UTs = alloc(ctx, nc, "UTs_sb", [128, 128], F32)
    ALO = alloc(ctx, nc, "ALO_sb", [128, 128], F32)
    SDG = alloc(ctx, nc, "SDG_sb", [128, 128], F32)
    sel = alloc(ctx, nc, "sel_sb", [32, NH * 128], F32)
    cw = alloc(ctx, nc, "cw_sb", [128, CH * 4], F32)
    cb = alloc(ctx, nc, "cb_sb", [128, CH], F32)
    dtw = alloc(ctx, nc, "dtw", [128, KC * NH], BF16)
    rowf = alloc(ctx, nc, "rowf", [1, 3 * NH], F32)
    negexpA = alloc(ctx, nc, "negexpA", [128, NH], F32)
    Dbc = alloc(ctx, nc, "Dbc", [128, NH], F32)
    dtbb = alloc(ctx, nc, "dtbb", [128, NH], F32)
    onerf = alloc(ctx, nc, "onerf", [1, 128], F32)
    dt_tm = alloc(ctx, nc, "dt_tm", [128, NTS * NH], F32)
    hin = alloc(ctx, nc, "hin", [128, DI], BF16)
    xbcg = alloc(ctx, nc, "xbcg", [128, CH * GW], BF16)
    convg = alloc(ctx, nc, "convg", [128, CH * GT], BF16)
    hist = alloc(ctx, nc, "hist", [128, CH * 4], BF16)
    xs_tm = alloc(ctx, nc, "xs_tm", [128, DI], BF16)
    b_tm = alloc(ctx, nc, "b_tm", [128, 128], BF16)
    wx = alloc(ctx, nc, "wx", [128, DI], BF16)
    acol = alloc(ctx, nc, "acol", [128, NH], F32)
    alp = alloc(ctx, nc, "alp", [128, NH], F32)
    nalp = alloc(ctx, nc, "nalp", [128, NH], F32)
    ealp = alloc(ctx, nc, "ealp", [128, NH], F32)
    wti = alloc(ctx, nc, "wti", [128, NH], F32)
    eaL = alloc(ctx, nc, "eaL", [128, NH], F32)
    alpf = alloc(ctx, nc, "alpf", [32, 128], F32)
    esb = alloc(ctx, nc, "esb", [128, 128], F32)
    ssb = alloc(ctx, nc, "ssb", [128, 128], BF16)
    esb2 = alloc(ctx, nc, "esb2", [128, 128], F32)
    ssb2 = alloc(ctx, nc, "ssb2", [128, 128], BF16)
    tg1 = alloc(ctx, nc, "tg1", [128, 512], F32)
    tg2 = alloc(ctx, nc, "tg2", [128, 512], F32)
    ygfm = alloc(ctx, nc, "ygfm", [128, 16 * GT], BF16)
    out_acc = alloc(ctx, nc, "out_acc", [128, NC4 * D], F32)
    qtile = alloc(ctx, nc, "qtile", [128, D], F32)
    carry = alloc(ctx, nc, "carry", [1, D], F32)
    wblk = alloc(ctx, nc, "wblk", [128, 2 * 4096], BF16)
    lblk = alloc(ctx, nc, "lblk", [128, 2 * 1024], BF16)

    ps_tr = PsumRing(ctx, nc, "pstr", 2, 512, BF16)
    ps_a = PsumRing(ctx, nc, "psa", 2, 512, F32)
    ps_b = PsumRing(ctx, nc, "psb", 2, 512, F32)
    ps_s = PsumRing(ctx, nc, "pss", 1, 512, F32)
    ps_f = PsumRing(ctx, nc, "psf", 1, 512, F32)

    for dst, srcd in [(st.idbf, idnd), (idf, idfd), (nwt, nwtd), (LTi, LTid),
                      (UTs, UTsd), (ALO, ALOd), (SDG, SDGd), (sel, seld),
                      (cw, cwd), (cb, cbd)]:
        p.on("sync", lambda e, dst=dst, srcd=srcd: e.dma_start(out=dst[:, :], in_=srcd),
             dma_chan="init")
    p.on("sync", lambda e: e.dma_start(out=rowf[:, 0:NH], in_=dtbd), dma_chan="init")
    p.on("sync", lambda e: e.dma_start(out=rowf[:, NH:2 * NH], in_=alogd),
         dma_chan="init")
    p.on("sync", lambda e: e.dma_start(out=rowf[:, 2 * NH:3 * NH], in_=dvecd),
         dma_chan="init")
    p.on("gpsimd", lambda e: e.dma_start(out=gnwt[:, :], in_=gnwd), dma_chan="initg")
    p.on("gpsimd", lambda e: e.dma_start(
        out=dtw[:, :].rearrange("p (k n) -> p k n", k=KC),
        in_=ipTd[:, 4352:4384].rearrange("(k p) n -> p k n", p=128)), dma_chan="initg")
    init_done = p.chan_total("init")
    initg_done = p.chan_total("initg")
    tid = init_done
    p.on("vector", lambda e: e.memset(st.epscol[:, :], 1e-5))
    p.on("vector", lambda e: e.memset(onerf[:, :], 1.0))
    p.on("vector", lambda e: e.memset(onef[:, :], 1.0))
    t_ = p.on("scalar", lambda e: e.activation(rowf[:, NH:2 * NH], rowf[:, NH:2 * NH],
                                               AF.Exp), after=[init_done], tag="expA")
    t_ = p.on("vector", lambda e: e.tensor_scalar(rowf[:, NH:2 * NH],
                                                  rowf[:, NH:2 * NH], -1.0, None,
                                                  ALU.mult), after=[t_])
    bcs = []
    for dst, c0 in [(dtbb, 0), (negexpA, NH), (Dbc, 2 * NH)]:
        ps, war, b = ps_f.get()
        tb = p.on("tensor", lambda e, c0=c0, ps=ps: e.matmul(
            ps[:, :NH], onerf[0:1, :], rowf[:, c0:c0 + NH], start=True, stop=True),
            after=[t_] + ([war] if war else []), tag=f"bc{c0}")
        tb = p.on("vector", lambda e, dst=dst, ps=ps: e.tensor_copy(dst[:, :],
                                                                    ps[:, :NH]),
                  after=[tb])
        ps_f.consumed(b, tb)
        bcs.append(tb)
    bc_done = list(bcs)
    onecol1 = onef

    xt_war = [None, None]
    lblk_war = [[], []]
    wblk_war = [[], []]
    ygrp_war = None
    oa_war = []

    def lblk_load(idx, srcap, tag):
        sl = idx % 2
        return p.on("gpsimd", lambda e, sl=sl, srcap=srcap: e.dma_start(
            out=lblk[:, sl * 1024:(sl + 1) * 1024].rearrange("p (c n) -> p c n", c=8),
            in_=srcap.rearrange("(c p) n -> p c n", p=128)),
            dma_chan="lw", after=(lblk_war[sl] or []) + [p.barrier_token("tensor")],
            tag=tag), sl

    def wblk_load(idx, srcap, tag):
        sl = idx % 2
        return p.on("gpsimd", lambda e, sl=sl, srcap=srcap: e.dma_start(
            out=wblk[:, sl * 4096:(sl + 1) * 4096].rearrange("p (c n) -> p c n", c=4),
            in_=srcap.rearrange("(c p) n -> p c n", p=128)),
            dma_chan="ww", after=(wblk_war[sl] or []) + [p.barrier_token("tensor")],
            tag=tag), sl

    for s in range(NSEQ):
        xn_toks = []
        for i in range(NTS):
            sl = i % 2
            tx = p.on("sync", lambda e, s=s, i=i, sl=sl: e.dma_start(
                out=xt[sl][:, :], in_=Xd[(s * NTS + i) * 128:(s * NTS + i + 1) * 128, :]),
                after=([xt_war[sl]] if xt_war[sl] else []), dma_chan=f"xt{sl}",
                tag=f"lx{s}_{i}")
            st.ln_after = [tx]
            tu = emit_ln_tile(p, st, xt[sl][:, :], st.dump[:, :], i, tag=f"ln{s}_{i}")
            xt_war[sl] = tu
            t2 = p.on("scalar", lambda e, i=i: e.activation(
                st.dump2[:, :D], st.dump[:, :], AF.Square,
                accum_out=st.s2[:, i:i + 1]), after=[tu], tag=f"rs{s}_{i}")
            t3 = p.on("vector", lambda e, i=i: e.tensor_scalar(
                st.vcol[:, i:i + 1], st.s2[:, i:i + 1], 1.0 / D, None, ALU.mult),
                after=[t2])
            t4 = p.on("scalar", lambda e, i=i: e.activation(
                st.vcol[:, i:i + 1], st.vcol[:, i:i + 1], AF.Sqrt,
                bias=st.epscol[:, 0:1]), after=[t3])
            t5 = p.on("vector", lambda e, i=i: e.reciprocal(
                st.s1[:, i:i + 1], st.vcol[:, i:i + 1]), after=[t4])
            t6 = p.on("vector", lambda e, i=i: e.tensor_scalar(
                st.dump[:, :], st.dump[:, :], st.s1[:, i:i + 1], None, ALU.mult),
                after=[t5])
            t7 = p.on("vector", lambda e: e.tensor_mul(xn_tm[:, :], st.dump[:, :],
                                                       nwt[:, :]),
                      after=[init_done, t6], tag=f"xn{s}_{i}")
            tts = emit_transpose(p, st, lambda _i: xn_tm[:, :],
                                 [xn_fm[k][:, i * 128:(i + 1) * 128] for k in range(KC)],
                                 1, KC, [t7, tid], ps_tr, tag=f"xtr{s}_{i}")
            xn_toks.extend(tts)

        dt_toks = []
        for i in range(NTS):
            ps, war, b = ps_f.get()
            last = None
            for k in range(KC):
                aft = xn_toks + [initg_done] + ([war] if war and k == 0 else [])
                last = p.on("tensor", lambda e, k=k, i=i, ps=ps: e.matmul(
                    ps[:, :NH], xn_fm[k][:, i * 128:(i + 1) * 128],
                    dtw[:, k * NH:(k + 1) * NH], start=(k == 0), stop=(k == KC - 1)),
                    after=(aft if k == 0 else []), inc=(1 if k == KC - 1 else 0),
                    tag=f"dt{s}_{i}")
            ta = p.on("vector", lambda e, ps=ps: e.tensor_tensor(
                tg1[:, :NH], ps[:, :NH], dtbb[:, :], ALU.add),
                after=[last] + bc_done, tag=f"sp0{s}_{i}")
            ps_f.consumed(b, ta)
            tb = p.on("scalar", lambda e: e.activation(tg2[:, :NH], tg1[:, :NH],
                                                       AF.Abs), after=[ta])
            tc = p.on("scalar", lambda e: e.activation(tg2[:, :NH], tg2[:, :NH],
                                                       AF.Exp, scale=-1.0), after=[tb])
            td = p.on("scalar", lambda e: e.activation(tg2[:, :NH], tg2[:, :NH],
                                                       AF.Ln, bias=onecol1[:, 0:1]),
                      after=[tc])
            te = p.on("vector", lambda e: e.tensor_scalar_max(tg1[:, :NH], tg1[:, :NH],
                                                              0.0), after=[td])
            tf = p.on("vector", lambda e, i=i: e.tensor_add(
                dt_tm[:, i * NH:(i + 1) * NH], tg1[:, :NH], tg2[:, :NH]), after=[te],
                tag=f"sp1{s}_{i}")
            dt_toks.append(tf)

        th0 = p.on("vector", lambda e: e.memset(hin[:, :], 0.0), tag=f"h0{s}")
        p.on("vector", lambda e: e.memset(hist[:, :], 0.0), tag=f"hi0{s}")
        tcar = p.on("vector", lambda e: e.memset(carry[:, :], 0.0), tag=f"cr0{s}")
        carry_tok = [tcar]

        for g in range(NGS):
            g0 = g * GT
            thc = p.on("vector", lambda e: e.tensor_copy(
                xbcg[:, :].rearrange("p (c w) -> p c w", c=CH)[:, :, 0:3],
                hist[:, :].rearrange("p (c w) -> p c w", c=CH)[:, :, 0:3]),
                after=[ygrp_war] if ygrp_war else [], tag=f"hc{s}_{g}")
            xbc_toks = []
            lw, lsl = lblk_load(0, ipTd[:, 2048:2048 + 128], f"ipw{s}_{g}_0")
            for c in range(CH):
                wt_, wsl_ = lw, lsl
                if c + 1 < CH:
                    lw, lsl = lblk_load(c + 1,
                                        ipTd[:, 2048 + (c + 1) * 128:2048 + (c + 2) * 128],
                                        f"ipw{s}_{g}_{c+1}")
                ps, war, b = ps_a.get()
                last = None
                for k in range(KC):
                    aft = [wt_] + xn_toks + ([war] if war and k == 0 else [])
                    last = p.on("tensor", lambda e, k=k, ps=ps, wsl_=wsl_, g0=g0: e.matmul(
                        ps[:, :], lblk[:, wsl_ * 1024 + k * 128:wsl_ * 1024 + (k + 1) * 128],
                        xn_fm[k][:, g0:g0 + GT],
                        start=(k == 0), stop=(k == KC - 1)),
                        after=(aft if k == 0 else []), inc=(1 if k == KC - 1 else 0),
                        tag=f"xbc{s}_{g}_{c}_{k}")
                lblk_war[wsl_] = [last]
                ev = p.on("vector", lambda e, c=c, ps=ps: e.tensor_copy(
                    xbcg[:, c * GW + 3:c * GW + 3 + GT], ps[:, :]),
                    after=[last, thc], tag=f"xbce{s}_{g}_{c}")
                ps_a.consumed(b, ev)
                xbc_toks.append(ev)
            thn = p.on("vector", lambda e: e.tensor_copy(
                hist[:, :].rearrange("p (c w) -> p c w", c=CH)[:, :, 0:3],
                xbcg[:, :].rearrange("p (c w) -> p c w", c=CH, w=GW)[:, :, GT:GT + 3]),
                after=xbc_toks, tag=f"hn{s}_{g}")

            conv_toks = []
            for c in range(CH):
                tgb = tg1 if c % 2 == 0 else tg2
                t0 = p.on("vector", lambda e, c=c, tgb=tgb: e.tensor_scalar(
                    tgb[:, :], xbcg[:, c * GW:c * GW + GT], cw[:, c * 4:c * 4 + 1],
                    None, ALU.mult),
                    after=xbc_toks + [init_done] + ([conv_toks[-2]] if len(conv_toks) > 1 else []),
                    tag=f"cv0{s}_{g}_{c}")
                for kk in range(1, 4):
                    t0 = p.on("vector", lambda e, c=c, kk=kk, tgb=tgb: e.scalar_tensor_tensor(
                        tgb[:, :], xbcg[:, c * GW + kk:c * GW + kk + GT],
                        cw[:, c * 4 + kk:c * 4 + kk + 1], tgb[:, :],
                        ALU.mult, ALU.add), after=[t0], tag=f"cv{kk}_{s}_{g}_{c}")
                t1 = p.on("scalar", lambda e, c=c, tgb=tgb: e.activation(
                    convg[:, c * GT:(c + 1) * GT], tgb[:, :], AF.Silu,
                    bias=cb[:, c:c + 1]), after=[t0], tag=f"cvs{s}_{g}_{c}")
                conv_toks.append(t1)

            ychk_toks = []
            for c4 in range(NC4):
                ci = g * NC4 + c4
                co = c4 * 128
                xs_toks = []
                for cc0 in range(0, 16, 4):
                    pstile, war, b = ps_tr.get()
                    tt = None
                    for j in range(4):
                        cc = cc0 + j
                        aft = conv_toks + ([war] if war and j == 0 else [])
                        tt = p.on("tensor", lambda e, cc=cc, j=j, co=co,
                                  pstile=pstile: e.transpose(
                            pstile[:, j * 128:(j + 1) * 128],
                            convg[:, cc * GT + co:cc * GT + co + 128],
                            st.idbf[:, :]),
                            after=(aft if j == 0 else []), inc=(1 if j == 3 else 0),
                            tag=f"xst{s}_{g}_{c4}_{cc}")
                    tv = p.on("vector", lambda e, cc0=cc0, pstile=pstile: e.tensor_copy(
                        xs_tm[:, cc0 * 128:(cc0 + 4) * 128], pstile[:, :]),
                        after=[tt], tag=f"xse{s}_{g}_{c4}_{cc0}")
                    ps_tr.consumed(b, tv)
                    xs_toks.append(tv)
                pstile, war, b = ps_tr.get()
                tt = p.on("tensor", lambda e, co=co, pstile=pstile: e.transpose(
                    pstile[:, 0:128], convg[:, 16 * GT + co:16 * GT + co + 128],
                    st.idbf[:, :]), after=conv_toks + ([war] if war else []),
                    tag=f"btt{s}_{g}_{c4}")
                tbm = p.on("vector", lambda e, pstile=pstile: e.tensor_copy(
                    b_tm[:, :], pstile[:, 0:128]), after=[tt], tag=f"bte{s}_{g}_{c4}")
                ps_tr.consumed(b, tbm)
                ta0 = p.on("vector", lambda e, ci=ci: e.tensor_mul(
                    acol[:, :], dt_tm[:, ci * NH:(ci + 1) * NH], negexpA[:, :]),
                    after=dt_toks + bc_done, tag=f"a{s}_{ci}")
                psA, warA, bA = ps_f.get()
                tA = p.on("tensor", lambda e, psA=psA: e.matmul(
                    psA[:, :NH], LTi[:, :], acol[:, :], start=True, stop=True),
                    after=[ta0, init_done] + ([warA] if warA else []), tag=f"al{s}_{ci}")
                tA1 = p.on("vector", lambda e, psA=psA: e.tensor_copy(
                    alp[:, :], psA[:, :NH]), after=[tA])
                tA2 = p.on("vector", lambda e, psA=psA: e.tensor_scalar(
                    nalp[:, :], psA[:, :NH], -1.0, None, ALU.mult), after=[tA])
                tA3 = p.on("scalar", lambda e: e.activation(ealp[:, :], alp[:, :],
                                                            AF.Exp), after=[tA1])
                ps_f.consumed(bA, tA2)
                psB, warB, bB = ps_f.get()
                tB = p.on("tensor", lambda e, psB=psB: e.matmul(
                    psB[:, :NH], UTs[:, :], acol[:, :], start=True, stop=True),
                    after=[ta0] + ([warB] if warB else []), tag=f"bml{s}_{ci}")
                tB1 = p.on("scalar", lambda e, psB=psB: e.activation(
                    wti[:, :], psB[:, :NH], AF.Exp), after=[tB])
                tB2 = p.on("vector", lambda e, ci=ci: e.tensor_mul(
                    wti[:, :], wti[:, :], dt_tm[:, ci * NH:(ci + 1) * NH]),
                    after=[tB1])
                ps_f.consumed(bB, tB2)
                psC, warC, bC = ps_f.get()
                tC = p.on("tensor", lambda e, psC=psC: e.matmul(
                    psC[:, :NH], ALO[:, :], acol[:, :], start=True, stop=True),
                    after=[ta0] + ([warC] if warC else []), tag=f"aL{s}_{ci}")
                tC1 = p.on("scalar", lambda e, psC=psC: e.activation(
                    eaL[:, :], psC[:, :NH], AF.Exp), after=[tC])
                ps_f.consumed(bC, tC1)
                psD, warD, bD = ps_f.get()
                tD = p.on("tensor", lambda e, psD=psD: e.transpose(
                    psD[0:32, 0:128], alp[:, :NH], idf[:, :]),
                    after=[tA1] + ([warD] if warD else []), tag=f"att{s}_{ci}")
                tD1 = p.on("vector", lambda e, psD=psD: e.tensor_copy(
                    alpf[:, :], psD[0:32, 0:128]), after=[tD])
                ps_f.consumed(bD, tD1)
                twx = p.on("vector", lambda e: e.tensor_mul(
                    wx[:, :].rearrange("p (h m) -> p h m", m=HD),
                    xs_tm[:, :].rearrange("p (h m) -> p h m", m=HD),
                    wti[:, :].unsqueeze(2).broadcast_to([128, NH, HD])),
                    after=xs_toks + [tB2], tag=f"wx{s}_{ci}")
                psG = ps_a.tiles[0]
                aftG = conv_toks + ([ps_a.war[0]] if ps_a.war[0] else [])
                tG = p.on("tensor", lambda e, co=co, psG=psG: e.matmul(
                    psG[:, 0:128], convg[:, 16 * GT + co:16 * GT + co + 128],
                    convg[:, 17 * GT + co:17 * GT + co + 128], start=True, stop=True),
                    after=aftG, tag=f"G{s}_{ci}")
                psE = ps_a.tiles[1]
                psY1 = ps_b.tiles[0]
                psY2 = ps_b.tiles[1]
                psS_ = ps_s.tiles[0]
                e_war = ps_a.war[1]
                y1_war = ps_b.war[0]
                y2_war = ps_b.war[1]
                s_war = ps_s.war[0]
                esb_ = [esb, esb2]
                ssb_ = [ssb, ssb2]
                psE_ = [psE[:, 0:128], ps_f.tiles[0][:, 128:256]]
                tEx_h = {}
                tS_h = {}
                st_h = {}
                lastyi = lastye = lastst = None
                comb = None
                for h in range(NH + 1):
                  if h < NH:
                    aftE = [tD1] + ([init_done] if h == 0 else [])
                    aftE += [tEx_h[h - 2]] if h >= 2 else ([e_war] if e_war else [])
                    pe = psE_[h % 2]
                    tE = p.on("tensor", lambda e, h=h, pe=pe: e.matmul(
                        pe, sel[:, h * 128:(h + 1) * 128], alpf[:, :],
                        start=True, stop=False), after=aftE, inc=0, tag=f"E{s}_{ci}_{h}")
                    tE = p.on("tensor", lambda e, pe=pe: e.matmul(
                        pe, LTi[:, :], SDG[:, :], start=False, stop=True),
                        inc=1, tag=f"Et{s}_{ci}_{h}")
                    aftX = [tE, tA2] + ([tS_h[h - 2]] if h >= 2 else [])
                    tEx = p.on("scalar", lambda e, h=h, pe=pe: e.activation(
                        esb_[h % 2][:, :], pe, AF.Exp,
                        bias=nalp[:, h:h + 1]),
                        after=aftX, tag=f"Ex{s}_{ci}_{h}")
                    tEx_h[h] = tEx
                    aftS = [tEx, tG] + ([st_h[h - 2]] if h >= 2 else [])
                    tS = p.on("vector", lambda e, h=h, ci=ci, psG=psG: e.scalar_tensor_tensor(
                        ssb_[h % 2][:, :], esb_[h % 2][:, :],
                        dt_tm[:, ci * NH + h:ci * NH + h + 1],
                        psG[:, 0:128], ALU.mult, ALU.mult), after=aftS,
                        tag=f"S{s}_{ci}_{h}")
                    tS_h[h] = tS
                  if h >= 1:
                    hd = h - 1
                    g8 = hd // 8
                    hm = hd % 8
                    sb = ssb_[hd % 2]
                    aft = [tS_h[hd]] + ([y1_war] if y1_war and hm == 0 else [])
                    lastyi = p.on("tensor", lambda e, hd=hd, hm=hm, sb=sb, psY1=psY1: e.matmul(
                        psY1[:, hm * 64:(hm + 1) * 64], sb[:, :],
                        xs_tm[:, hd * 64:(hd + 1) * 64], start=True, stop=True),
                        after=aft, inc=(1 if hm == 7 else 0), tag=f"yi{s}_{ci}_{hd}")
                    aft = ([th0] if hd == 0 else []) + ([y2_war] if y2_war and hm == 0 else [])
                    lastye = p.on("tensor", lambda e, hd=hd, hm=hm, co=co, psY2=psY2: e.matmul(
                        psY2[:, hm * 64:(hm + 1) * 64],
                        convg[:, 17 * GT + co:17 * GT + co + 128],
                        hin[:, hd * 64:(hd + 1) * 64], start=True, stop=True),
                        after=aft, inc=(1 if hm == 7 else 0), tag=f"ye{s}_{ci}_{hd}")
                    aft = ([twx, tbm] if hd == 0 else []) + ([s_war] if s_war and hm == 0 else [])
                    lastst = p.on("tensor", lambda e, hd=hd, hm=hm, psS_=psS_: e.matmul(
                        psS_[:, hm * 64:(hm + 1) * 64], b_tm[:, :],
                        wx[:, hd * 64:(hd + 1) * 64], start=True, stop=True),
                        after=aft, inc=(1 if hm == 7 else 0), tag=f"st{s}_{ci}_{hd}")
                    st_h[hd] = lastst
                    if hm == 7:
                        g8o = g8 * 512
                        t1 = p.on("vector", lambda e, g8=g8, psY2=psY2: e.tensor_mul(
                            tg1[:, :].rearrange("p (h m) -> p h m", m=64),
                            psY2[:, :].rearrange("p (h m) -> p h m", m=64),
                            ealp[:, g8 * 8:(g8 + 1) * 8].unsqueeze(2).broadcast_to(
                                [128, 8, 64])), after=[lastye, tA3],
                            tag=f"cb1{s}_{ci}_{g8}")
                        y2_war = t1
                        t2 = p.on("vector", lambda e, g8=g8, g8o=g8o: e.tensor_mul(
                            tg2[:, :].rearrange("p (h m) -> p h m", m=64),
                            xs_tm[:, g8o:g8o + 512].rearrange("p (h m) -> p h m", m=64),
                            Dbc[:, g8 * 8:(g8 + 1) * 8].unsqueeze(2).broadcast_to(
                                [128, 8, 64])), after=xs_toks + bc_done,
                            tag=f"cb2{s}_{ci}_{g8}")
                        t3 = p.on("vector", lambda e: e.tensor_add(
                            tg1[:, :], tg1[:, :], tg2[:, :]), after=[t1, t2])
                        t4 = p.on("vector", lambda e, c4=c4, g8o=g8o, psY1=psY1: e.tensor_tensor(
                            xbcg[:, c4 * DI + g8o:c4 * DI + g8o + 512],
                            tg1[:, :], psY1[:, :], ALU.add),
                            after=[t3, lastyi, thn], tag=f"cb3{s}_{ci}_{g8}")
                        y1_war = t4
                        comb = t4
                        ychk_toks.append(t4)
                        t5 = p.on("vector", lambda e, g8=g8, g8o=g8o: e.tensor_mul(
                            tg2[:, :].rearrange("p (h m) -> p h m", m=64),
                            hin[:, g8o:g8o + 512].rearrange("p (h m) -> p h m", m=64),
                            eaL[:, g8 * 8:(g8 + 1) * 8].unsqueeze(2).broadcast_to(
                                [128, 8, 64])), after=[lastst, tC1],
                            tag=f"sb1{s}_{ci}_{g8}")
                        t6 = p.on("vector", lambda e, g8o=g8o, psS_=psS_: e.tensor_tensor(
                            hin[:, g8o:g8o + 512], tg2[:, :], psS_[:, :], ALU.add),
                            after=[t5], tag=f"sb2{s}_{ci}_{g8}")
                        s_war = t6
                ps_a.war[0] = comb
                ps_a.war[1] = tEx_h[NH - 2]
                ps_f.war[0] = tEx_h[NH - 1]
                ps_b.war[0] = y1_war
                ps_b.war[1] = y2_war
                ps_s.war[0] = s_war

            sz_toks = []
            zw, zsl = wblk_load(100 * g, ipTd[:, 0:512], f"zw{s}_{g}_0")
            for fg in range(4):
                wt_, wsl_ = zw, zsl
                if fg + 1 < 4:
                    zw, zsl = wblk_load(100 * g + fg + 1,
                                        ipTd[:, (fg + 1) * 512:(fg + 2) * 512],
                                        f"zw{s}_{g}_{fg+1}")
                for t4 in range(NC4):
                    ps, war, b = ps_a.get()
                    last = None
                    for k in range(KC):
                        aft = [wt_] + xn_toks + ychk_toks + ([war] if war and k == 0 else [])
                        last = p.on("tensor", lambda e, k=k, t4=t4, g0=g0, ps=ps,
                                    wsl_=wsl_: e.matmul(
                            ps[:, :], xn_fm[k][:, g0 + t4 * 128:g0 + (t4 + 1) * 128],
                            wblk[:, wsl_ * 4096 + k * 512:wsl_ * 4096 + (k + 1) * 512],
                            start=(k == 0), stop=(k == KC - 1)),
                            after=(aft if k == 0 else []), inc=(1 if k == KC - 1 else 0),
                            tag=f"z{s}_{g}_{fg}_{t4}")
                    ev = p.on("scalar", lambda e, t4=t4, fg=fg, ps=ps: e.activation(
                        convg[:, t4 * DI + fg * 512:t4 * DI + (fg + 1) * 512],
                        ps[:, :], AF.Silu), after=[last], tag=f"sz{s}_{g}_{fg}_{t4}")
                    ps_a.consumed(b, ev)
                    sz_toks.append(ev)
                wblk_war[wsl_] = [sz_toks[-1]]

            yg_toks = []
            for t4 in range(NC4):
                t1 = p.on("vector", lambda e, t4=t4: e.tensor_mul(
                    xbcg[:, t4 * DI:(t4 + 1) * DI], xbcg[:, t4 * DI:(t4 + 1) * DI],
                    convg[:, t4 * DI:(t4 + 1) * DI]), after=sz_toks + ychk_toks,
                    tag=f"gt1{s}_{g}_{t4}")
                t2 = p.on("scalar", lambda e, t4=t4: e.activation(
                    st.dump2[:, :DI], xbcg[:, t4 * DI:(t4 + 1) * DI], AF.Square,
                    accum_out=gst[:, t4:t4 + 1]), after=[t1], tag=f"gt2{s}_{g}_{t4}")
                t3 = p.on("vector", lambda e, t4=t4: e.tensor_scalar(
                    gst[:, t4:t4 + 1], gst[:, t4:t4 + 1], 1.0 / DI, None,
                    ALU.mult), after=[t2])
                t4_ = p.on("scalar", lambda e, t4=t4: e.activation(
                    gst[:, t4:t4 + 1], gst[:, t4:t4 + 1], AF.Sqrt,
                    bias=st.epscol[:, 0:1]), after=[t3])
                t5 = p.on("vector", lambda e, t4=t4: e.reciprocal(
                    gst[:, t4:t4 + 1], gst[:, t4:t4 + 1]), after=[t4_])
                t6 = p.on("vector", lambda e, t4=t4: e.tensor_scalar(
                    xbcg[:, t4 * DI:(t4 + 1) * DI], xbcg[:, t4 * DI:(t4 + 1) * DI],
                    gst[:, t4:t4 + 1], None, ALU.mult), after=[t5])
                t7 = p.on("vector", lambda e, t4=t4: e.tensor_mul(
                    xbcg[:, t4 * DI:(t4 + 1) * DI], xbcg[:, t4 * DI:(t4 + 1) * DI],
                    gnwt[:, :]), after=[t6, initg_done], tag=f"yg{s}_{g}_{t4}")
                yg_toks.append(t7)

            ygf_done = []
            for fk in range(16):
                for t4 in range(NC4):
                    pstile, war, b = ps_tr.get()
                    tt = p.on("tensor", lambda e, fk=fk, t4=t4, pstile=pstile: e.transpose(
                        pstile[:, 0:128], xbcg[:, t4 * DI + fk * 128:t4 * DI + (fk + 1) * 128],
                        st.idbf[:, :]), after=yg_toks + ([war] if war else []),
                        tag=f"ygt{s}_{g}_{fk}_{t4}")
                    tv = p.on("vector", lambda e, fk=fk, t4=t4, pstile=pstile: e.tensor_copy(
                        ygfm[:, fk * GT + t4 * 128:fk * GT + (t4 + 1) * 128],
                        pstile[:, 0:128]), after=[tt], tag=f"ygte{s}_{g}_{fk}_{t4}")
                    ps_tr.consumed(b, tv)
                    ygf_done.append(tv)
            ygrp_war = ygf_done[-1]

            def part_mms(src_fm_done, row0, add_x, tag):
                evs = []
                for half in range(2):
                    pw, pwsl = wblk_load(1000 + row0 + half, fTd[row0:row0 + 512, :],
                                         f"fw{tag}_{half}_0")
                    psg = [ps_a.tiles[0], ps_a.tiles[1], ps_b.tiles[0], ps_b.tiles[1]]
                    wars = [ps_a.war[0], ps_a.war[1], ps_b.war[0], ps_b.war[1]]
                    for blk in range(4):
                        wt_, wsl_ = pw, pwsl
                        if blk + 1 < 4:
                            pw, pwsl = wblk_load(1000 + row0 + half + blk + 1,
                                                 fTd[row0 + (blk + 1) * 512:
                                                     row0 + (blk + 2) * 512, :],
                                                 f"fw{tag}_{half}_{blk+1}")
                        for ti in range(2):
                            t4 = half * 2 + ti
                            for dg in range(2):
                                pj = ti * 2 + dg
                                for fk in range(4):
                                    fkg = blk * 4 + fk
                                    aft = [wt_] + src_fm_done
                                    if blk == 0 and fk == 0 and wars[pj]:
                                        aft = aft + [wars[pj]]
                                    p.on("tensor", lambda e, fk=fk, fkg=fkg, t4=t4,
                                         dg=dg, pj=pj, wsl_=wsl_, psg=psg, blk=blk: e.matmul(
                                        psg[pj][:, :],
                                        ygfm[:, fkg * GT + t4 * 128:fkg * GT + (t4 + 1) * 128],
                                        wblk[:, wsl_ * 4096 + fk * 1024 + dg * 512:
                                             wsl_ * 4096 + fk * 1024 + (dg + 1) * 512],
                                        start=(blk == 0 and fk == 0),
                                        stop=(blk == 3 and fk == 3)),
                                        inc=(1 if fk == 3 else 0),
                                        after=aft, tag=f"p{tag}_{half}_{blk}_{t4}_{dg}")
                        wblk_war[wsl_] = [p.barrier_token("tensor")]
                    for ti in range(2):
                        t4 = half * 2 + ti
                        sl = t4 % 2
                        if add_x:
                            tx = p.on("sync", lambda e, s=s, g=g, t4=t4, sl=sl: e.dma_start(
                                out=xt[sl][:, :],
                                in_=Xd[(s * NTS + g * NC4 + t4) * 128:
                                       (s * NTS + g * NC4 + t4 + 1) * 128, :]),
                                after=([xt_war[sl]] if xt_war[sl] else []),
                                dma_chan=f"xt{sl}", tag=f"lxo{tag}_{t4}")
                        for dg in range(2):
                            pj = ti * 2 + dg
                            tok = (
                                "tensor", p.count["tensor"], False)
                            if add_x:
                                gi = g * NC4 + t4
                                ev0 = p.on("vector", lambda e, t4=t4, dg=dg, sl=sl,
                                           gi=gi: e.tensor_scalar(
                                    st.dump[:, dg * 512:(dg + 1) * 512],
                                    xt[sl][:, dg * 512:(dg + 1) * 512],
                                    st.mcol[:, gi:gi + 1], st.rcol[:, gi:gi + 1],
                                    ALU.subtract, ALU.mult),
                                    after=[tx], tag=f"evu{tag}_{t4}_{dg}")
                                ev = p.on("vector", lambda e, t4=t4, dg=dg, pj=pj,
                                          sl=sl, psg=psg: e.tensor_add(
                                    out_acc[:, t4 * D + dg * 512:t4 * D + (dg + 1) * 512],
                                    psg[pj][:, :], xt[sl][:, dg * 512:(dg + 1) * 512]),
                                    after=[tok, tx] + oa_war, tag=f"ev{tag}_{t4}_{dg}")
                                ev = p.on("vector", lambda e, t4=t4, dg=dg: e.tensor_add(
                                    out_acc[:, t4 * D + dg * 512:t4 * D + (dg + 1) * 512],
                                    out_acc[:, t4 * D + dg * 512:t4 * D + (dg + 1) * 512],
                                    st.dump[:, dg * 512:(dg + 1) * 512]),
                                    after=[ev0], tag=f"evv{tag}_{t4}_{dg}")
                                xt_war[sl] = ev
                            else:
                                ev = p.on("vector", lambda e, t4=t4, dg=dg, pj=pj,
                                          psg=psg: e.tensor_add(
                                    out_acc[:, t4 * D + dg * 512:t4 * D + (dg + 1) * 512],
                                    out_acc[:, t4 * D + dg * 512:t4 * D + (dg + 1) * 512],
                                    psg[pj][:, :]), after=[tok], tag=f"ev{tag}_{t4}_{dg}")
                            wars[pj] = ev
                            evs.append(ev)
                    ps_a.war[0], ps_a.war[1] = wars[0], wars[1]
                    ps_b.war[0], ps_b.war[1] = wars[2], wars[3]
                return evs

            evA = part_mms(ygf_done, 0, True, f"A{s}_{g}")

            def fm_proj(wsrc, tag):
                toks = []
                lw2, lsl2 = lblk_load(500, wsrc[:, 0:128], f"{tag}_0")
                for f in range(16):
                    wt_, wsl_ = lw2, lsl2
                    if f + 1 < 16:
                        lw2, lsl2 = lblk_load(500 + f + 1,
                                              wsrc[:, (f + 1) * 128:(f + 2) * 128],
                                              f"{tag}_{f+1}")
                    ps, war, b = ps_s.get()
                    last = None
                    for k in range(KC):
                        aft = [wt_] + xn_toks + ([war] if war and k == 0 else [])
                        last = p.on("tensor", lambda e, k=k, f=f, g0=g0, ps=ps,
                                    wsl_=wsl_: e.matmul(
                            ps[:, :], lblk[:, wsl_ * 1024 + k * 128:wsl_ * 1024 + (k + 1) * 128],
                            xn_fm[k][:, g0:g0 + GT], start=(k == 0), stop=(k == KC - 1)),
                            after=(aft if k == 0 else []), inc=(1 if k == KC - 1 else 0),
                            tag=f"{tag}m{f}_{k}")
                    lblk_war[wsl_] = [last]
                    ev = p.on("vector", lambda e, f=f, ps=ps: e.tensor_copy(
                        ygfm[:, f * GT:(f + 1) * GT], ps[:, :]), after=[last],
                        tag=f"{tag}e{f}")
                    ps_s.consumed(b, ev)
                    toks.append(ev)
                return toks

            y0_done = fm_proj(zwTd, f"y0_{s}_{g}")
            evB = part_mms(y0_done, DI, False, f"B{s}_{g}")
            y1_done = fm_proj(owTd, f"y1_{s}_{g}")

            for t4 in range(NC4):
                pq = [ps_a.tiles[0], ps_a.tiles[1]]
                pqw = [ps_a.war[0], ps_a.war[1]]
                lastq = [None, None]
                for blk in range(4):
                    pw, pwsl = wblk_load(2000 + t4 * 8 + blk,
                                         fTd[2 * DI + blk * 512:2 * DI + (blk + 1) * 512, :],
                                         f"fc{s}_{g}_{t4}_{blk}")
                    for dg in range(2):
                        for fk in range(4):
                            fkg = blk * 4 + fk
                            aft = [pw] + y1_done
                            if blk == 0 and fk == 0 and pqw[dg]:
                                aft = aft + [pqw[dg]]
                            lastq[dg] = p.on("tensor", lambda e, fk=fk, fkg=fkg, t4=t4,
                                             dg=dg, pwsl=pwsl, pq=pq, blk=blk: e.matmul(
                                pq[dg][:, :],
                                ygfm[:, fkg * GT + t4 * 128:fkg * GT + (t4 + 1) * 128],
                                wblk[:, pwsl * 4096 + fk * 1024 + dg * 512:
                                     pwsl * 4096 + fk * 1024 + (dg + 1) * 512],
                                start=(blk == 0 and fk == 0),
                                stop=(blk == 3 and fk == 3)),
                                inc=(1 if fk == 3 else 0),
                                after=aft, tag=f"q{s}_{g}_{t4}_{blk}_{dg}")
                    wblk_war[pwsl] = [lastq[1]]
                tq = [None, None]
                for dg in range(2):
                    tq[dg] = p.on("vector", lambda e, dg=dg, pq=pq: e.tensor_copy(
                        qtile[:, dg * 512:(dg + 1) * 512], pq[dg][:, :]),
                        after=[lastq[dg]], tag=f"qe{s}_{g}_{t4}_{dg}")
                    ps_a.war[dg] = tq[dg]
                for dg in range(2):
                    ps, war, b = ps_b.get()
                    tcs = p.on("tensor", lambda e, dg=dg, ps=ps: e.matmul(
                        ps[:, :], LTi[:, :], qtile[:, dg * 512:(dg + 1) * 512],
                        start=True, stop=False),
                        after=[tq[dg], init_done, tcar] + ([war] if war else []),
                        tag=f"cs{s}_{g}_{t4}_{dg}")
                    tcs = p.on("tensor", lambda e, dg=dg, ps=ps: e.matmul(
                        ps[:, :], onerf[0:1, :], carry[0:1, dg * 512:(dg + 1) * 512],
                        start=False, stop=True), after=carry_tok, inc=1,
                        tag=f"cc{s}_{g}_{t4}_{dg}")
                    ev = p.on("vector", lambda e, t4=t4, dg=dg, ps=ps: e.tensor_add(
                        out_acc[:, t4 * D + dg * 512:t4 * D + (dg + 1) * 512],
                        out_acc[:, t4 * D + dg * 512:t4 * D + (dg + 1) * 512],
                        ps[:, :]), after=[tcs], tag=f"ce{s}_{g}_{t4}_{dg}")
                    ps_b.consumed(b, ev)
                carry_new = []
                for dg in range(2):
                    ps, war, b = ps_b.get()
                    tcl = p.on("tensor", lambda e, dg=dg, ps=ps: e.matmul(
                        ps[0:1, :], ALO[:, 0:1], qtile[:, dg * 512:(dg + 1) * 512],
                        start=True, stop=True),
                        after=[tq[dg]] + ([war] if war else []), inc=1,
                        tag=f"cl{s}_{g}_{t4}_{dg}")
                    tc2 = p.on("vector", lambda e, dg=dg, ps=ps: e.tensor_add(
                        carry[0:1, dg * 512:(dg + 1) * 512],
                        carry[0:1, dg * 512:(dg + 1) * 512], ps[0:1, :]),
                        after=[tcl], tag=f"cu{s}_{g}_{t4}_{dg}")
                    ps_b.consumed(b, tc2)
                    carry_new.append(tc2)
                carry_tok = carry_new

            st_toks = []
            bar = p.barrier_token("vector")
            for t4 in range(NC4):
                ts_ = p.on("sync", lambda e, s=s, g=g, t4=t4: e.dma_start(
                    out=XOd[(s * NTS + g * NC4 + t4) * 128:
                            (s * NTS + g * NC4 + t4 + 1) * 128, :],
                    in_=out_acc[:, t4 * D:(t4 + 1) * D]),
                    after=[bar], dma_chan=f"st{t4 % 2}", tag=f"so{s}_{g}_{t4}")
                st_toks.append(ts_)
            oa_war = st_toks

    return p


def _mk_wrap(nc, ctx, emit_fn, wnames, wshapes, T, **kw):
    Xd = nc.dram_tensor("X", [T, D], F32, kind="ExternalInput").ap()
    XOd = nc.dram_tensor("XO", [T, D], F32, kind="ExternalOutput").ap()
    wd = {}
    for nm in wnames:
        shape, dt_ = wshapes[nm]
        wd[nm] = nc.dram_tensor(nm, shape, dt_, kind="ExternalInput").ap()
    p = Prog(nc)
    emit_fn(p, nc, ctx, Xd, XOd, wd, T=T, **kw)
    p.finalize()
    p.make_sems()
    return p


def l1ffn_shapes():
    return {"rT": ([D, 8], F32), "gT": ([8, D, 2048], F32), "uT": ([8, D, 2048], F32),
            "dT": ([8, 2048, D], F32), "idn": ([128, 128], BF16)}


def l1attn_shapes():
    return {"Wq": ([D, D], F32), "Wk": ([D, D], F32), "Wv": ([D, D], F32),
            "Wo": ([D, D], F32), "W1": ([D, 4096], F32), "W2": ([4096, D], F32),
            "bqc": ([128, 8], F32), "bkr": ([1, D], F32), "bvr": ([1, D], F32),
            "bor": ([1, D], F32), "b1c": ([128, 32], F32), "b2t": ([128, D], F32),
            "g1t": ([128, D], F32), "be1t": ([128, D], F32), "g2t": ([128, D], F32),
            "be2t": ([128, D], F32), "idn": ([128, 128], BF16)}


def l0ffn_shapes():
    return {"rT": ([D, 8], F32), "br": ([1, 8], F32), "idn": ([128, 128], BF16),
            "idf": ([128, 128], F32), "W1": ([8, D, D], F32), "W2": ([8, D, D], F32),
            "b1c": ([128, 64], F32), "b2": ([8, D], F32)}


def l0attn_shapes():
    return {"nwt": ([128, D], F32), "ipT": ([D, 4384], F32),
            "convw": ([128, 72], F32), "convb": ([128, 18], F32),
            "dtb": ([1, 32], F32), "alog": ([1, 32], F32), "dvec": ([1, 32], F32),
            "gnwt": ([128, 2048], F32), "zwT": ([D, 2048], F32),
            "owT": ([D, 2048], F32), "fT": ([6144, D], F32),
            "idn": ([128, 128], BF16), "idf": ([128, 128], F32),
            "LTi": ([128, 128], F32), "UTs": ([128, 128], F32),
            "ALO": ([128, 128], F32), "SDG": ([128, 128], F32),
            "selc": ([32, 4096], F32)}


def build_l1ffn(nc, ctx, T=4096, TG=1024, debug=False):
    return _mk_wrap(nc, ctx, emit_l1ffn, L1FFN_W, l1ffn_shapes(), T, TG=TG,
                    debug=debug)


def build_l1attn(nc, ctx, T=4096, S=2048, stop_after=None):
    return _mk_wrap(nc, ctx, emit_l1attn, L1ATTN_W, l1attn_shapes(), T, S=S,
                    stop_after=stop_after)


def build_l0attn(nc, ctx, T=4096, S=2048):
    return _mk_wrap(nc, ctx, emit_l0attn, L0ATTN_W, l0attn_shapes(), T, S=S)


def build_l0ffn(nc, ctx, T=4096, TG=1024, mode="full", C=4096):
    Xd = nc.dram_tensor("X", [T, D], F32, kind="ExternalInput").ap()
    wd = {}
    for nm, (shape, dt_) in l0ffn_shapes().items():
        if mode == "count" and nm in ("W1", "W2", "b1c", "b2"):
            continue
        wd[nm] = nc.dram_tensor(nm, shape, dt_, kind="ExternalInput").ap()
    p = Prog(nc)
    if mode == "count":
        CNTd = nc.dram_tensor("CNT", [8, 1], F32, kind="ExternalOutput").ap()
        emit_l0ffn(p, nc, ctx, Xd, None, wd, T=T, TG=TG, mode="count", C=C,
                   CNTd=CNTd)
    else:
        XOd = nc.dram_tensor("XO", [T, D], F32, kind="ExternalOutput").ap()
        cntd = nc.dram_tensor("cnt", [8, 8], F32, kind="ExternalInput").ap()
        prefd = nc.dram_tensor("pref", [8, 1], F32, kind="ExternalInput").ap()
        emit_l0ffn(p, nc, ctx, Xd, XOd, wd, T=T, TG=TG, mode="full", C=C,
                   cntd=cntd, prefd=prefd)
    p.finalize()
    p.make_sems()
    return p


from concourse.bass_core_rust import IndirectOffsetOnAxis

L1FFNS_W = ["rT", "gT", "uT", "dT", "idn", "idf", "ecap"]


def l1ffns_shapes():
    d = l1ffn_shapes()
    d["idf"] = ([128, 128], F32)
    d["ecap"] = ([128, 8], F32)
    return d


def emit_l1ffn_sparse(p, nc, ctx, Xd, XOd, wd, Udd, Uedd, EOdd, T=4096, CAP=1280):
    E, FF = 8, 2048
    KC = D // 128
    NT = T // 128
    NCT = CAP // 128
    TGS = []
    c = CAP
    while c > 0:
        TGS.append(min(512, c))
        c -= TGS[-1]
    rTd, gTd, uTd, dTd, idn = (wd[k] for k in L1FFN_W)

    st = St()
    st.idbf = alloc(ctx, nc, "idbf", [128, 128], BF16)
    st.dump = alloc(ctx, nc, "dump", [128, D], F32)
    st.dump2 = alloc(ctx, nc, "dump2", [128, D], F32)
    st.epscol = alloc(ctx, nc, "epscol", [128, 1], F32)
    for nm in ("s1", "s2", "mcol", "vcol", "rcol"):
        setattr(st, nm, alloc(ctx, nc, nm, [128, NT], F32))
    idf = alloc(ctx, nc, "idf_sb", [128, 128], F32)
    xt = [alloc(ctx, nc, f"xt{i}", [128, D], F32) for i in range(2)]
    u_tm = [alloc(ctx, nc, f"utm{i}", [128, D], BF16) for i in range(2)]
    rbf = alloc(ctx, nc, "rbf", [128, KC * E], BF16)
    ecap = alloc(ctx, nc, "ecap", [128, E], F32)
    oh1 = alloc(ctx, nc, "oh1b", [128, NT * E], F32)
    oh2 = alloc(ctx, nc, "oh2b", [128, NT * E], F32)
    w1c = alloc(ctx, nc, "w1c", [128, NT], F32)
    w2c = alloc(ctx, nc, "w2c", [128, NT], F32)
    o1c = alloc(ctx, nc, "o1c", [128, NT], I32)
    o2c = alloc(ctx, nc, "o2c", [128, NT], I32)
    zer8 = alloc(ctx, nc, "zer8", [128, E], F32)
    zcol8 = alloc(ctx, nc, "zcol8", [8, 1], F32)
    prob = alloc(ctx, nc, "prob", [128, E], F32)
    eq = alloc(ctx, nc, "eq", [128, E], F32)
    cum = alloc(ctx, nc, "cum", [128, E], F32)
    c1 = alloc(ctx, nc, "c1", [128, 8], F32)
    eo_tm = alloc(ctx, nc, "eo_tm", [128, D], BF16)
    gth = [alloc(ctx, nc, f"gth{i}", [128, D], BF16) for i in range(2)]
    wslot = [alloc(ctx, nc, f"w{i}", [128, 16384], BF16) for i in range(2)]
    ctx2 = contextlib.ExitStack()
    ufmt = alloc(ctx2, nc, "ufmt", [128, D], BF16)
    ohc_fm = alloc(ctx2, nc, "ohc_fm", [8, NT * 128], F32)
    pos_fm = alloc(ctx2, nc, "pos_fm", [8, NT * 128], F32)

    ps_tr = PsumRing(ctx, nc, "pstr", 2, 512, BF16)
    ps_of = PsumRing(ctx, nc, "psof", 1, 128, F32)
    ps_r = PsumRing(ctx, nc, "psr", 1, E, F32)
    ps_a = PsumRing(ctx, nc, "psa", 2, 512, F32)
    ps_b = PsumRing(ctx, nc, "psb", 2, 512, F32)

    p.on("sync", lambda e: e.dma_start(out=st.idbf[:, :], in_=idn), dma_chan="init")
    p.on("sync", lambda e: e.dma_start(out=idf[:, :], in_=wd["idf"]), dma_chan="init")
    p.on("sync", lambda e: e.dma_start(out=ecap[:, :], in_=wd["ecap"]), dma_chan="init")
    p.on("gpsimd", lambda e: e.dma_start(
        out=rbf[:, :].rearrange("p (k e) -> p k e", k=KC),
        in_=rTd.rearrange("(k p) e -> p k e", p=128)), dma_chan="initg")
    init_done = p.chan_total("init")
    initg_done = p.chan_total("initg")
    p.on("vector", lambda e: e.memset(st.epscol[:, :], 1e-5))
    tz = p.on("vector", lambda e: e.memset(zer8[:, :], 0.0))
    p.on("vector", lambda e: e.memset(zcol8[:, :], 0.0))
    tze = p.on("vector", lambda e: e.memset(eo_tm[:, :], 0.0))
    tzr = p.on("sync", lambda e: e.dma_start(
        out=EOdd[E * CAP:E * CAP + 1, :], in_=eo_tm[0:1, :]), after=[tze],
        dma_chan="ez")
    for r in range(E * CAP // 128):
        p.on("sync", lambda e, r=r: e.dma_start(
            out=Uedd[r * 128:(r + 1) * 128, :], in_=eo_tm[:, :]), after=[tze],
            dma_chan="ez")
    p.on("sync", lambda e: e.dma_start(
        out=Uedd[E * CAP:E * CAP + 1, :], in_=eo_tm[0:1, :]), after=[tze],
        dma_chan="ez")
    ez_all = p.chan_total("ez")

    xt_war = [None, None]
    ud_war = [None, None]
    prev_rt = None
    rt_done = []
    last_tts = None
    for i in range(NT):
        sl = i % 2
        tx = p.on("sync", lambda e, i=i, sl=sl: e.dma_start(
            out=xt[sl][:, :], in_=Xd[i * 128:(i + 1) * 128, :]),
            after=([xt_war[sl]] if xt_war[sl] else []), dma_chan=f"xt{sl}",
            tag=f"lx{i}")
        if ud_war[sl]:
            p.wait_all("vector", [ud_war[sl]], tag=f"uw{i}")
        st.ln_after = [tx]
        tu = emit_ln_tile(p, st, xt[sl][:, :], u_tm[sl][:, :], i, tag=f"ln{i}")
        xt_war[sl] = tu
        ud_war[sl] = p.on("sync", lambda e, i=i, sl=sl: e.dma_start(
            out=Udd[i * 128:(i + 1) * 128, :], in_=u_tm[sl][:, :]), after=[tu],
            dma_chan=f"ud{sl}", tag=f"sud{i}")
        tts = emit_transpose(p, st, lambda _i, sl=sl: u_tm[sl][:, :],
                             [ufmt[:, k * 128:(k + 1) * 128] for k in range(KC)],
                             1, KC, [tu, init_done], ps_tr, tag=f"tr{i}")
        last_tts = tts[-1]
        ps, war, b = ps_r.get()
        last = None
        for k in range(KC):
            aft = tts + [initg_done] + ([war] if war and k == 0 else [])
            last = p.on("tensor", lambda e, k=k, ps=ps: e.matmul(
                ps[:, :], ufmt[:, k * 128:(k + 1) * 128], rbf[:, k * E:(k + 1) * E],
                start=(k == 0), stop=(k == KC - 1)),
                after=(aft if k == 0 else []), inc=(1 if k == KC - 1 else 0),
                tag=f"rt{i}")
        tm_ = p.on("vector", lambda e, ps=ps: e.reduce_max(c1[:, 0:1], ps[:, :],
                                                           AX.X, negate=True),
                   after=[last])
        te = p.on("scalar", lambda e, ps=ps: e.activation(
            prob[:, :], ps[:, :], AF.Exp, bias=c1[:, 0:1]),
            after=[tm_] + ([prev_rt] if prev_rt else []))
        ps_r.consumed(b, te)
        p.on("vector", lambda e: e.reduce_max(c1[:, 1:2], prob[:, :], AX.X),
             after=[te])
        p.on("vector", lambda e: e.tensor_scalar(eq[:, :], prob[:, :], c1[:, 1:2],
                                                 None, ALU.is_ge))
        p.on("vector", lambda e: e.tensor_tensor_scan(cum[:, :], eq[:, :],
                                                      zer8[:, :], 0.0, ALU.add,
                                                      ALU.add), after=[tz])
        p.on("vector", lambda e: e.tensor_scalar(cum[:, :], cum[:, :], 1.0, None,
                                                 ALU.is_equal))
        p.on("vector", lambda e, i=i: e.tensor_mul(oh1[:, i * E:(i + 1) * E],
                                                   eq[:, :], cum[:, :]))
        p.on("vector", lambda e, i=i: e.tensor_scalar(
            cum[:, :], oh1[:, i * E:(i + 1) * E], -1.0, 1.0, ALU.mult, ALU.add))
        p.on("vector", lambda e: e.tensor_mul(eq[:, :], prob[:, :], cum[:, :]))
        p.on("vector", lambda e: e.reduce_max(c1[:, 2:3], eq[:, :], AX.X))
        p.on("vector", lambda e: e.tensor_scalar(cum[:, :], eq[:, :], c1[:, 2:3],
                                                 None, ALU.is_ge))
        p.on("vector", lambda e: e.tensor_tensor_scan(eq[:, :], cum[:, :],
                                                      zer8[:, :], 0.0, ALU.add,
                                                      ALU.add))
        p.on("vector", lambda e: e.tensor_scalar(eq[:, :], eq[:, :], 1.0, None,
                                                 ALU.is_equal))
        p.on("vector", lambda e, i=i: e.tensor_mul(oh2[:, i * E:(i + 1) * E],
                                                   cum[:, :], eq[:, :]))
        p.on("vector", lambda e: e.tensor_add(c1[:, 3:4], c1[:, 1:2], c1[:, 2:3]))
        p.on("vector", lambda e: e.reciprocal(c1[:, 4:5], c1[:, 3:4]))
        p.on("vector", lambda e, i=i: e.tensor_tensor(
            w1c[:, i:i + 1], c1[:, 1:2], c1[:, 4:5], ALU.mult))
        tq = p.on("vector", lambda e, i=i: e.tensor_tensor(
            w2c[:, i:i + 1], c1[:, 2:3], c1[:, 4:5], ALU.mult), tag=f"we{i}")
        prev_rt = tq
        tsum = p.on("vector", lambda e, i=i: e.tensor_add(
            cum[:, :], oh1[:, i * E:(i + 1) * E], oh2[:, i * E:(i + 1) * E]))
        pso_, waro, bo_ = ps_of.get()
        tt = p.on("tensor", lambda e, pso_=pso_: e.matmul(
            pso_[0:8, 0:128], cum[:, :], idf[:, :], start=True, stop=True),
            after=[tsum, init_done] + ([waro] if waro else []), tag=f"ot{i}")
        tv = p.on("vector", lambda e, i=i, pso_=pso_: e.tensor_copy(
            ohc_fm[:, i * 128:(i + 1) * 128], pso_[0:8, 0:128]), after=[tt])
        ps_of.consumed(bo_, tv)
        rt_done.append(tq)
        rt_done.append(tv)
    tpos = p.on("vector", lambda e: e.tensor_tensor_scan(
        pos_fm[:, :], ohc_fm[:, :], zcol8[:, 0:1].broadcast_to([8, NT * 128]),
        0.0, ALU.add, ALU.add), after=rt_done, tag="scan")
    sc_done = []
    for i in range(NT):
        pso_, waro, bo_ = ps_of.get()
        tt = p.on("tensor", lambda e, i=i, pso_=pso_: e.matmul(
            pso_[:, 0:8], pos_fm[:, i * 128:(i + 1) * 128], idf[0:8, 0:8],
            start=True, stop=True),
            after=[tpos] + ([waro] if waro else []), tag=f"pt{i}")
        tlast = None
        for (ohb, oc, sl) in ((oh1, o1c, 0), (oh2, o2c, 1)):
            p.on("vector", lambda e, i=i, ohb=ohb, pso_=pso_: e.tensor_mul(
                eq[:, :], ohb[:, i * E:(i + 1) * E], pso_[:, 0:8]),
                after=[tt])
            p.on("vector", lambda e: e.reduce_sum(c1[:, 5:6], eq[:, :], AX.X))
            p.on("vector", lambda e, i=i, ohb=ohb: e.tensor_mul(
                eq[:, :], ohb[:, i * E:(i + 1) * E], ecap[:, :]),
                after=[init_done])
            p.on("vector", lambda e: e.reduce_sum(c1[:, 6:7], eq[:, :], AX.X))
            p.on("vector", lambda e: e.tensor_add(c1[:, 7:8], c1[:, 5:6],
                                                  c1[:, 6:7]))
            p.on("vector", lambda e: e.tensor_scalar(c1[:, 0:1], c1[:, 5:6],
                                                     CAP + 0.5, None, ALU.is_ge))
            p.on("vector", lambda e: e.tensor_scalar(c1[:, 1:2], c1[:, 7:8],
                                                     -1.0, float(E * CAP),
                                                     ALU.mult, ALU.add))
            p.on("vector", lambda e: e.scalar_tensor_tensor(
                c1[:, 7:8], c1[:, 1:2], c1[:, 0:1], c1[:, 7:8], ALU.mult, ALU.add))
            tlast = p.on("vector", lambda e, i=i, oc=oc: e.tensor_copy(
                oc[:, i:i + 1], c1[:, 7:8]), tag=f"oc{i}_{sl}")
            sc_done.append(tlast)
        ps_of.consumed(bo_, tlast)
    utm_war = [[last_tts], [last_tts]]
    ud_all = [p.chan_total("ud0"), p.chan_total("ud1")]
    scat_last = []
    for i in range(NT):
        sl = i % 2
        tl = p.on("sync", lambda e, i=i, sl=sl: e.dma_start(
            out=u_tm[sl][:, :], in_=Udd[i * 128:(i + 1) * 128, :]),
            after=utm_war[sl] + ud_all + sc_done, dma_chan=f"xu{sl}",
            tag=f"lu{i}")
        s1_ = p.on("gpsimd", lambda e, i=i, sl=sl: e.indirect_dma_start(
            Uedd, IndirectOffsetOnAxis(ap=o1c[:, i:i + 1], axis=0),
            u_tm[sl][:, :], None), after=[tl, ez_all], dma_chan="sc0", tag=f"sca{i}")
        s2_ = p.on("gpsimd", lambda e, i=i, sl=sl: e.indirect_dma_start(
            Uedd, IndirectOffsetOnAxis(ap=o2c[:, i:i + 1], axis=0),
            u_tm[sl][:, :], None), after=[tl, s1_], dma_chan="sc0", tag=f"scb{i}")
        utm_war[sl] = [s1_, s2_]
        scat_last = [s1_, s2_]
    ctx2.close()
    ue_tm = alloc(ctx, nc, "ue_tm", [128, NCT * D], BF16)
    ue_fm = [alloc(ctx, nc, f"uefm{k}", [128, CAP], BF16) for k in range(KC)]
    h_fm = [alloc(ctx, nc, f"hfm{f}", [128, CAP], BF16) for f in range(16)]
    scat_all = [p.chan_total("sc0")]
    wwar = {0: [], 1: []}

    def wload(slot, e_, kind, tg):
        src = {"g": gTd, "u": uTd, "d": dTd}[kind][e_]
        rows, cols = src.shape
        ch = rows // 128
        return p.on("gpsimd", lambda e, slot=slot, src=src, ch=ch, cols=cols:
                    e.dma_start(
            out=wslot[slot][:, :ch * cols].rearrange("p (c n) -> p c n", c=ch),
            in_=src.rearrange("(c p) n -> p c n", p=128)), dma_chan=f"w{slot}",
            after=wwar[slot], tag=tg)

    eo_last = []
    eo_war = [tzr]
    phases = [(e_, m) for e_ in range(E) for m in ("g", "u", "d")]
    wtok = {}
    uef_done = []
    uef_prev = None
    h_g = []
    h_done = []
    for pi, (e_, mat) in enumerate(phases):
        if pi == 0:
            wtok[(e_, mat)] = wload(0, e_, mat, f"wl{e_}{mat}")
        if pi + 1 < len(phases):
            en, mn = phases[pi + 1]
            wtok[(en, mn)] = wload((pi + 1) % 2, en, mn, f"wl{en}{mn}")
        slot = pi % 2
        w = wslot[slot]
        wt = wtok[(e_, mat)]
        cons = []
        if mat == "g":
            lds = []
            for j in range(NCT):
                aft = scat_all if e_ == 0 else ([uef_prev] if j == 0 else [])
                tl = p.on("gpsimd", lambda e, e_=e_, j=j: e.dma_start(
                    out=ue_tm[:, j * D:(j + 1) * D],
                    in_=Uedd[e_ * CAP + j * 128:e_ * CAP + (j + 1) * 128, :]),
                    after=aft, dma_chan="ue", tag=f"ue{e_}_{j}")
                lds.append(tl)
            uef_done = emit_transpose(p, st, lambda j: ue_tm[:, j * D:(j + 1) * D],
                                      ue_fm, NCT, KC, lds + [init_done], ps_tr,
                                      tag=f"uet{e_}")
            uef_prev = uef_done[-1]
            off = 0
            for tg_ in TGS:
                for f in range(16):
                    ps, war, b = ps_a.get()
                    last = None
                    for k in range(KC):
                        aft = [wt] + uef_done + ([war] if war and k == 0 else [])
                        last = p.on("tensor", lambda e, k=k, f=f, off=off,
                                    tg_=tg_, ps=ps, w=w: e.matmul(
                            ps[:, :tg_],
                            w[:, k * 2048 + f * 128:k * 2048 + (f + 1) * 128],
                            ue_fm[k][:, off:off + tg_],
                            start=(k == 0), stop=(k == KC - 1)),
                            after=(aft if k == 0 else []),
                            inc=(1 if k == KC - 1 else 0), tag=f"g{e_}_{f}_{off}")
                    tv = p.on("scalar", lambda e, f=f, off=off, tg_=tg_, ps=ps:
                              e.activation(h_fm[f][:, off:off + tg_], ps[:, :tg_],
                                           AF.Silu), after=[last],
                              tag=f"sl{e_}_{f}_{off}")
                    ps_a.consumed(b, tv)
                    cons.append(tv)
                off += tg_
            h_g = list(cons)
        elif mat == "u":
            off = 0
            for tg_ in TGS:
                for f in range(16):
                    ps, war, b = ps_a.get()
                    last = None
                    for k in range(KC):
                        aft = [wt] + uef_done + ([war] if war and k == 0 else [])
                        last = p.on("tensor", lambda e, k=k, f=f, off=off,
                                    tg_=tg_, ps=ps, w=w: e.matmul(
                            ps[:, :tg_],
                            w[:, k * 2048 + f * 128:k * 2048 + (f + 1) * 128],
                            ue_fm[k][:, off:off + tg_],
                            start=(k == 0), stop=(k == KC - 1)),
                            after=(aft if k == 0 else []),
                            inc=(1 if k == KC - 1 else 0), tag=f"u{e_}_{f}_{off}")
                    tv = p.on("vector", lambda e, f=f, off=off, tg_=tg_, ps=ps:
                              e.tensor_tensor(h_fm[f][:, off:off + tg_],
                                              h_fm[f][:, off:off + tg_],
                                              ps[:, :tg_], ALU.mult),
                              after=[last] + h_g, tag=f"h{e_}_{f}_{off}")
                    ps_a.consumed(b, tv)
                    cons.append(tv)
                off += tg_
            h_done = list(cons)
        else:
            for j in range(NCT):
                evs = []
                for dg in range(2):
                    ps, war, b = ps_b.get()
                    last = None
                    for fk in range(16):
                        aft = [wt] + h_done + ([war] if war and fk == 0 else [])
                        last = p.on("tensor", lambda e, fk=fk, j=j, dg=dg, ps=ps,
                                    w=w: e.matmul(
                            ps[:, :], h_fm[fk][:, j * 128:(j + 1) * 128],
                            w[:, fk * 1024 + dg * 512:fk * 1024 + (dg + 1) * 512],
                            start=(fk == 0), stop=(fk == 15)),
                            after=(aft if fk == 0 else []),
                            inc=(1 if fk == 15 else 0), tag=f"d{e_}_{j}_{dg}")
                    tv = p.on("vector", lambda e, dg=dg, ps=ps: e.tensor_copy(
                        eo_tm[:, dg * 512:(dg + 1) * 512], ps[:, :]),
                        after=[last] + eo_war, tag=f"eoe{e_}_{j}_{dg}")
                    ps_b.consumed(b, tv)
                    evs.append(tv)
                ts_ = p.on("gpsimd", lambda e, e_=e_, j=j: e.dma_start(
                    out=EOdd[e_ * CAP + j * 128:e_ * CAP + (j + 1) * 128, :],
                    in_=eo_tm[:, :]), after=evs, dma_chan="eo", tag=f"eow{e_}_{j}")
                eo_war = [ts_]
                cons.append(ts_)
            eo_last = [cons[-1]]
        wwar[slot] = cons
    eo_all = p.chan_total("eo")
    gth_war = [None, None]
    for i in range(NT):
        sl = i % 2
        tg1_ = p.on("gpsimd", lambda e, i=i: e.indirect_dma_start(
            gth[0][:, :], None, EOdd,
            IndirectOffsetOnAxis(ap=o1c[:, i:i + 1], axis=0)),
            after=[eo_all, tzr] + sc_done + ([gth_war[0]] if gth_war[0] else []),
            dma_chan="gt0", tag=f"g1{i}")
        tg2_ = p.on("gpsimd", lambda e, i=i: e.indirect_dma_start(
            gth[1][:, :], None, EOdd,
            IndirectOffsetOnAxis(ap=o2c[:, i:i + 1], axis=0)),
            after=[eo_all, tzr] + ([gth_war[1]] if gth_war[1] else []),
            dma_chan="gt1", tag=f"g2{i}")
        tx = p.on("sync", lambda e, i=i, sl=sl: e.dma_start(
            out=xt[sl][:, :], in_=Xd[i * 128:(i + 1) * 128, :]),
            after=([xt_war[sl]] if xt_war[sl] else []), dma_chan=f"xt{sl}",
            tag=f"lx3{i}")
        t1_ = p.on("vector", lambda e, i=i: e.tensor_scalar(
            st.dump[:, :], gth[0][:, :], w1c[:, i:i + 1], None, ALU.mult),
            after=[tg1_], tag=f"cmb1{i}")
        t2_ = p.on("vector", lambda e, i=i: e.scalar_tensor_tensor(
            st.dump[:, :], gth[1][:, :], w2c[:, i:i + 1], st.dump[:, :],
            ALU.mult, ALU.add), after=[tg2_], tag=f"cmb2{i}")
        gth_war = [t1_, t2_]
        t3_ = p.on("vector", lambda e, sl=sl: e.scalar_tensor_tensor(
            xt[sl][:, :], st.dump[:, :], 0.5, xt[sl][:, :], ALU.mult, ALU.add),
            after=[tx, t2_], tag=f"out{i}")
        ts_ = p.on("sync", lambda e, i=i, sl=sl: e.dma_start(
            out=XOd[i * 128:(i + 1) * 128, :], in_=xt[sl][:, :]), after=[t3_],
            dma_chan=f"st{sl}", tag=f"sx{i}")
        xt_war[sl] = ts_
    return p


def build_l1ffn_sparse(nc, ctx, T=4096, CAP=1280):
    Xd = nc.dram_tensor("X", [T, D], F32, kind="ExternalInput").ap()
    XOd = nc.dram_tensor("XO", [T, D], F32, kind="ExternalOutput").ap()
    wd = {}
    for nm, (shape, dt_) in l1ffns_shapes().items():
        wd[nm] = nc.dram_tensor(nm, shape, dt_, kind="ExternalInput").ap()
    Udd = nc.dram_tensor("Ud", [T, D], BF16).ap()
    Uedd = nc.dram_tensor("Ued", [8 * CAP + 1, D], BF16).ap()
    EOdd = nc.dram_tensor("EOd", [8 * CAP + 1, D], BF16).ap()
    p = Prog(nc)
    emit_l1ffn_sparse(p, nc, ctx, Xd, XOd, wd, Udd, Uedd, EOdd, T=T, CAP=CAP)
    p.finalize()
    p.make_sems()
    return p


def emit_l0ffn_sparse(p, nc, ctx, Xd, XOd, wd, Udd, Uedd, EOdd, T=4096, TG=1024,
                      C=4096, CL=768, cntd=None, prefd=None):
    E = 8
    KC = D // 128
    NT = TG // 128
    NG = T // TG
    NTT = T // 128
    NCT = CL // 128
    TGS = []
    c = CL
    while c > 0:
        TGS.append(min(512, c))
        c -= TGS[-1]
    rTd, brd, idnd, idfd = wd["rT"], wd["br"], wd["idn"], wd["idf"]
    W1d, W2d, b1cd, b2d = wd["W1"], wd["W2"], wd["b1c"], wd["b2"]

    st = St()
    st.idbf = alloc(ctx, nc, "idbf", [128, 128], BF16)
    idf = alloc(ctx, nc, "idf_sb", [128, 128], F32)
    st.dump = alloc(ctx, nc, "dump", [128, D], F32)
    st.dump2 = alloc(ctx, nc, "dump2", [128, D], F32)
    st.epscol = alloc(ctx, nc, "epscol", [128, 1], F32)
    for nm in ("s1", "s2", "mcol", "vcol", "rcol"):
        setattr(st, nm, alloc(ctx, nc, nm, [128, NT], F32))

    xt = [alloc(ctx, nc, f"xt{i}", [128, D], F32) for i in range(2)]
    u_tm = alloc(ctx, nc, "u_tm", [128, NT * D], BF16)
    ut2 = [alloc(ctx, nc, f"ut2{i}", [128, D], BF16) for i in range(2)]
    rbf = alloc(ctx, nc, "rbf", [128, KC * E], BF16)
    brr = alloc(ctx, nc, "brr", [1, E], BF16)
    oh_fm = alloc(ctx, nc, "oh_fm", [8, NTT * 128], F32)
    onehot_tm = alloc(ctx, nc, "onehot_tm", [128, NTT * E], F32)
    gates = alloc(ctx, nc, "gates", [128, NTT], F32)
    oner8 = alloc(ctx, nc, "oner8", [1, 128], BF16)
    prob = alloc(ctx, nc, "prob", [128, E], F32)
    eq = alloc(ctx, nc, "eq", [128, E], F32)
    cum = alloc(ctx, nc, "cum", [128, E], F32)
    zer8 = alloc(ctx, nc, "zer8", [128, E], F32)
    c1 = alloc(ctx, nc, "c1", [128, 8], F32)
    cnt_sb = alloc(ctx, nc, "cnt_sb", [8, E], F32)
    pref_sb = alloc(ctx, nc, "pref_sb", [8, 1], F32)
    base_sb = alloc(ctx, nc, "base_sb", [8, 1], F32)
    b2sb = alloc(ctx, nc, "b2sb", [1, E * D], BF16)
    b1cs = alloc(ctx, nc, "b1cs", [128, E * KC], F32)
    ones8f = alloc(ctx, nc, "ones8f", [8, 1], F32)
    zcol = alloc(ctx, nc, "zcol", [8, 1], F32)
    kg = alloc(ctx, nc, "kg", [128, NTT], F32)
    ecl = alloc(ctx, nc, "ecl", [128, E], F32)
    ecli = alloc(ctx, nc, "ecli", [128, E], I32)
    oc = alloc(ctx, nc, "ocol", [128, NTT], I32)
    eo_tm = alloc(ctx, nc, "eo_tm", [128, D], BF16)
    gth = alloc(ctx, nc, "gth", [128, D], BF16)
    wslot = [alloc(ctx, nc, f"w{i}", [128, 16384], BF16) for i in range(2)]
    ctx2 = contextlib.ExitStack()
    u_fm = [alloc(ctx2, nc, f"ufm{k}", [128, TG], BF16) for k in range(KC)]
    pos_fm = alloc(ctx2, nc, "pos_fm", [8, NTT * 128], F32)
    kp_fm = alloc(ctx2, nc, "kp_fm", [8, NTT * 128], F32)

    ps_tr = PsumRing(ctx, nc, "pstr", 2, 512, BF16)
    ps_oh = PsumRing(ctx, nc, "psoh", 1, 512, F32)
    ps_r = PsumRing(ctx, nc, "psr", 1, 512, F32)
    ps_a = PsumRing(ctx, nc, "psa", 2, 512, F32)
    ps_b = PsumRing(ctx, nc, "psb", 2, 512, F32)

    p.on("sync", lambda e: e.dma_start(out=st.idbf[:, :], in_=idnd), dma_chan="init")
    p.on("sync", lambda e: e.dma_start(out=idf[:, :], in_=idfd), dma_chan="init")
    p.on("sync", lambda e: e.dma_start(out=st.dump[0:1, :E], in_=brd), dma_chan="init")
    p.on("sync", lambda e: e.dma_start(out=b1cs[:, :], in_=b1cd), dma_chan="init")
    p.on("sync", lambda e: e.dma_start(out=cnt_sb[:, :], in_=cntd), dma_chan="init")
    p.on("sync", lambda e: e.dma_start(out=pref_sb[:, :], in_=prefd), dma_chan="init")
    p.on("gpsimd", lambda e: e.dma_start(
        out=rbf[:, :].rearrange("p (k e) -> p k e", k=KC),
        in_=rTd.rearrange("(k p) e -> p k e", p=128)), dma_chan="initg")
    for e_ in range(E):
        p.on("gpsimd", lambda e, e_=e_: e.dma_start(
            out=b2sb[0:1, e_ * D:(e_ + 1) * D], in_=b2d[e_:e_ + 1, :]),
            dma_chan="initg")
    init_done = p.chan_total("init")
    initg_done = p.chan_total("initg")
    tid = init_done
    p.on("vector", lambda e: e.memset(st.epscol[:, :], 1e-5))
    p.on("vector", lambda e: e.memset(zer8[:, :], 0.0))
    p.on("vector", lambda e: e.memset(oner8[:, :], 1.0))
    p.on("vector", lambda e: e.memset(zcol[:, :], 0.0))
    p.on("vector", lambda e: e.memset(ones8f[:, :], 1.0))
    tio = p.on("gpsimd", lambda e: e.iota(ecli[:, :], pattern=[[CL, E]], base=-1,
                                          channel_multiplier=0))
    p.on("vector", lambda e: e.tensor_copy(ecl[:, :], ecli[:, :]), after=[tio])
    tbr = p.on("vector", lambda e: e.tensor_copy(brr[:, :], st.dump[0:1, :E]),
               after=[init_done])
    tze = p.on("vector", lambda e: e.memset(eo_tm[:, :], 0.0))
    tzr = p.on("sync", lambda e: e.dma_start(
        out=EOdd[E * CL:E * CL + 1, :], in_=eo_tm[0:1, :]), after=[tze],
        dma_chan="ez")
    for r in range(E * CL // 128):
        p.on("sync", lambda e, r=r: e.dma_start(
            out=Uedd[r * 128:(r + 1) * 128, :], in_=eo_tm[:, :]), after=[tze],
            dma_chan="ez")
    p.on("sync", lambda e: e.dma_start(
        out=Uedd[E * CL:E * CL + 1, :], in_=eo_tm[0:1, :]), after=[tze],
        dma_chan="ez")
    ez_all = p.chan_total("ez")

    xt_war = [None, None]
    ud_prev = {}

    oh_done = []
    prev_rt_tail = None
    for g in range(NG):
        ut_toks = []
        for i in range(NT):
            sl = i % 2
            tx = p.on("sync", lambda e, g=g, i=i, sl=sl: e.dma_start(
                out=xt[sl][:, :], in_=Xd[(g * NT + i) * 128:(g * NT + i + 1) * 128, :]),
                after=([xt_war[sl]] if xt_war[sl] else []), dma_chan=f"xt{sl}",
                tag=f"lx{g}_{i}")
            if i in ud_prev:
                p.wait_all("vector", [ud_prev[i]], tag=f"udw{g}_{i}")
            st.ln_after = [tx]
            tu = emit_ln_tile(p, st, xt[sl][:, :], u_tm[:, i * D:(i + 1) * D], i,
                              tag=f"ln{g}_{i}")
            xt_war[sl] = tu
            ud_prev[i] = p.on("sync", lambda e, g=g, i=i: e.dma_start(
                out=Udd[(g * NT + i) * 128:(g * NT + i + 1) * 128, :],
                in_=u_tm[:, i * D:(i + 1) * D]), after=[tu], dma_chan=f"ud{i % 2}",
                tag=f"sud{g}_{i}")
            ut_toks.append(tu)
        ufm_done = emit_transpose(p, st, lambda i: u_tm[:, i * D:(i + 1) * D],
                                  u_fm, NT, KC, ut_toks + [tid], ps_tr, tag=f"tr{g}")
        for i in range(NT):
            ti_ = g * NT + i
            ps, war, b = ps_r.get()
            last = None
            for k in range(KC):
                aft = ufm_done + [initg_done] + ([war] if war and k == 0 else [])
                last = p.on("tensor", lambda e, k=k, i=i, ps=ps: e.matmul(
                    ps[:, :E], u_fm[k][:, i * 128:(i + 1) * 128],
                    rbf[:, k * E:(k + 1) * E],
                    start=(k == 0), stop=False),
                    after=(aft if k == 0 else []), inc=0, tag=f"rt{g}_{i}")
            last = p.on("tensor", lambda e, ps=ps: e.matmul(
                ps[:, :E], oner8[0:1, :], brr[:, :], start=False, stop=True),
                after=[tbr], inc=1, tag=f"rtb{g}_{i}")
            tm_ = p.on("vector", lambda e, ps=ps: e.reduce_max(
                c1[:, 0:1], ps[:, :E], AX.X, negate=True), after=[last])
            te = p.on("scalar", lambda e, ps=ps: e.activation(
                prob[:, :], ps[:, :E], AF.Exp, bias=c1[:, 0:1]),
                after=[tm_] + ([prev_rt_tail] if prev_rt_tail else []))
            ps_r.consumed(b, te)
            p.on("vector", lambda e: e.reduce_sum(c1[:, 1:2], prob[:, :], AX.X),
                 after=[te])
            p.on("vector", lambda e, ti_=ti_: e.reciprocal(
                gates[:, ti_:ti_ + 1], c1[:, 1:2]))
            t1_ = p.on("vector", lambda e: e.reduce_max(c1[:, 2:3], prob[:, :], AX.X))
            p.on("vector", lambda e: e.tensor_scalar(eq[:, :], prob[:, :], c1[:, 2:3],
                                                     None, ALU.is_ge), after=[t1_])
            p.on("vector", lambda e: e.tensor_tensor_scan(cum[:, :], eq[:, :],
                                                          zer8[:, :], 0.0, ALU.add,
                                                          ALU.add))
            p.on("vector", lambda e: e.tensor_scalar(cum[:, :], cum[:, :], 1.0, None,
                                                     ALU.is_equal))
            tq = p.on("vector", lambda e, ti_=ti_: e.tensor_mul(
                onehot_tm[:, ti_ * E:(ti_ + 1) * E], eq[:, :], cum[:, :]),
                tag=f"oh{g}_{i}")
            prev_rt_tail = tq
        for i0 in range(0, NT, 4):
            ps, war, b = ps_oh.get()
            lastt = None
            for j in range(4):
                ti_ = g * NT + i0 + j
                aft = ([prev_rt_tail] if j == 0 else []) + ([war] if war and j == 0 else [])
                lastt = p.on("tensor", lambda e, ti_=ti_, j=j, ps=ps: e.transpose(
                    ps[0:8, j * 128:(j + 1) * 128],
                    onehot_tm[:, ti_ * E:(ti_ + 1) * E], idf[:, :]),
                    after=aft, inc=(1 if j == 3 else 0), tag=f"oht{g}_{i0}_{j}")
            tv = p.on("vector", lambda e, g=g, i0=i0, ps=ps: e.tensor_copy(
                oh_fm[:, (g * NT + i0) * 128:(g * NT + i0 + 4) * 128], ps[0:8, :]),
                after=[lastt], tag=f"ohe{g}_{i0}")
            ps_oh.consumed(b, tv)
            oh_done.append(tv)

    psb_, warb, bb_ = ps_oh.get()
    tb_ = p.on("tensor", lambda e: e.matmul(psb_[0:8, 0:1], cnt_sb[:, :],
                                            pref_sb[:, :], start=True, stop=True),
               after=[init_done] + oh_done + ([warb] if warb else []), tag="base")
    tb_ = p.on("vector", lambda e: e.tensor_copy(base_sb[:, :], psb_[0:8, 0:1]),
               after=[tb_])
    ps_oh.consumed(bb_, tb_)
    tpos = p.on("vector", lambda e: e.tensor_tensor_scan(
        pos_fm[:, :], oh_fm[:, :], zcol[:, 0:1].broadcast_to([8, NTT * 128]),
        base_sb[:, 0:1], ALU.add, ALU.add), after=[tb_] + oh_done, tag="scan")
    tkp = p.on("vector", lambda e: e.tensor_scalar(kp_fm[:, :], pos_fm[:, :],
                                                   float(C - 1) + 0.5, None,
                                                   ALU.is_le), after=[tpos], tag="keep")
    tohk = p.on("vector", lambda e: e.tensor_mul(kp_fm[:, :], kp_fm[:, :],
                                                 oh_fm[:, :]), after=[tkp], tag="ohk")
    tploc = p.on("vector", lambda e: e.tensor_scalar(
        pos_fm[:, :], pos_fm[:, :], base_sb[:, 0:1], None, ALU.subtract),
        after=[tpos], tag="ploc")
    kg_done = []
    sc_done = []
    for ti_ in range(NTT):
        ps, war, b = ps_r.get()
        tkk = p.on("tensor", lambda e, ti_=ti_, ps=ps: e.matmul(
            ps[:, 0:1], kp_fm[:, ti_ * 128:(ti_ + 1) * 128], ones8f[:, :],
            start=True, stop=True),
            after=[tohk] + ([war] if war else []), tag=f"kt{ti_}")
        tkg = p.on("vector", lambda e, ti_=ti_, ps=ps: e.scalar_tensor_tensor(
            kg[:, ti_:ti_ + 1], ps[:, 0:1], 0.5, gates[:, ti_:ti_ + 1],
            ALU.mult, ALU.mult), after=[tkk], tag=f"kg{ti_}")
        ps_r.consumed(b, tkg)
        kg_done.append(tkg)
        pso_, waro, bo_ = ps_oh.get()
        tt = p.on("tensor", lambda e, ti_=ti_, pso_=pso_: e.matmul(
            pso_[:, 0:8], pos_fm[:, ti_ * 128:(ti_ + 1) * 128], idf[0:8, 0:8],
            start=True, stop=True),
            after=[tploc] + ([waro] if waro else []), tag=f"pt{ti_}")
        p.on("vector", lambda e, ti_=ti_, pso_=pso_: e.tensor_mul(
            eq[:, :], onehot_tm[:, ti_ * E:(ti_ + 1) * E], pso_[:, 0:8]),
            after=[tt])
        p.on("vector", lambda e: e.reduce_sum(c1[:, 5:6], eq[:, :], AX.X))
        p.on("vector", lambda e, ti_=ti_: e.tensor_mul(
            eq[:, :], onehot_tm[:, ti_ * E:(ti_ + 1) * E], ecl[:, :]))
        p.on("vector", lambda e: e.reduce_sum(c1[:, 6:7], eq[:, :], AX.X))
        p.on("vector", lambda e: e.tensor_add(c1[:, 7:8], c1[:, 5:6], c1[:, 6:7]))
        p.on("vector", lambda e: e.tensor_scalar(c1[:, 3:4], c1[:, 5:6],
                                                 CL + 0.5, None, ALU.is_ge))
        p.on("vector", lambda e: e.tensor_scalar(c1[:, 4:5], c1[:, 7:8],
                                                 -1.0, float(E * CL),
                                                 ALU.mult, ALU.add))
        p.on("vector", lambda e: e.scalar_tensor_tensor(
            c1[:, 7:8], c1[:, 4:5], c1[:, 3:4], c1[:, 7:8], ALU.mult, ALU.add))
        tlast = p.on("vector", lambda e, ti_=ti_: e.tensor_copy(
            oc[:, ti_:ti_ + 1], c1[:, 7:8]), tag=f"oc{ti_}")
        ps_oh.consumed(bo_, tlast)
        sc_done.append(tlast)
    ud_all = [p.chan_total("ud0"), p.chan_total("ud1")]
    scat_last = None
    utm_war = [None, None]
    for ti_ in range(NTT):
        sl = ti_ % 2
        tl = p.on("sync", lambda e, ti_=ti_, sl=sl: e.dma_start(
            out=ut2[sl][:, :], in_=Udd[ti_ * 128:(ti_ + 1) * 128, :]),
            after=([utm_war[sl]] if utm_war[sl] else []) + ud_all + sc_done,
            dma_chan=f"xu{sl}", tag=f"lu{ti_}")
        tgm = p.on("vector", lambda e, ti_=ti_, sl=sl: e.tensor_scalar(
            ut2[sl][:, :], ut2[sl][:, :], gates[:, ti_:ti_ + 1], None, ALU.mult),
            after=[tl], tag=f"ug{ti_}")
        ts_ = p.on("gpsimd", lambda e, ti_=ti_, sl=sl: e.indirect_dma_start(
            Uedd, IndirectOffsetOnAxis(ap=oc[:, ti_:ti_ + 1], axis=0),
            ut2[sl][:, :], None), after=[tgm, ez_all], dma_chan="sc0",
            tag=f"sca{ti_}")
        utm_war[sl] = ts_
        scat_last = ts_
    ctx2.close()
    ue_tm = alloc(ctx, nc, "ue_tm", [128, NCT * D], BF16)
    ue_fm = [alloc(ctx, nc, f"uefm{k}", [128, CL], BF16) for k in range(KC)]
    h_fm = [alloc(ctx, nc, f"hfm{f}", [128, CL], BF16) for f in range(KC)]
    scat_all = [p.chan_total("sc0")]
    wwar = {0: [], 1: []}

    def wpair(slot, e_, tg):
        t1_ = p.on("gpsimd", lambda e, slot=slot, e_=e_: e.dma_start(
            out=wslot[slot][:, :KC * D].rearrange("p (c n) -> p c n", c=KC),
            in_=W1d[e_].rearrange("(c p) n -> p c n", p=128)), dma_chan=f"w{slot}",
            after=wwar[slot], tag=f"w1{tg}")
        t2_ = p.on("gpsimd", lambda e, slot=slot, e_=e_: e.dma_start(
            out=wslot[slot][:, KC * D:2 * KC * D].rearrange("p (c n) -> p c n", c=KC),
            in_=W2d[e_].rearrange("(c p) n -> p c n", p=128)), dma_chan=f"w{slot}",
            tag=f"w2{tg}")
        return (t1_, t2_)

    eo_war = [tzr]
    uef_prev = None
    prev_w = wpair(0, 0, "e0")
    for e_ in range(E):
        slot = e_ % 2
        wt_ = prev_w
        if e_ + 1 < E:
            prev_w = wpair((e_ + 1) % 2, e_ + 1, f"e{e_ + 1}")
        lds = []
        for j in range(NCT):
            aft = scat_all if e_ == 0 else ([uef_prev] if j == 0 else [])
            tl = p.on("gpsimd", lambda e, e_=e_, j=j: e.dma_start(
                out=ue_tm[:, j * D:(j + 1) * D],
                in_=Uedd[e_ * CL + j * 128:e_ * CL + (j + 1) * 128, :]),
                after=aft, dma_chan="ue", tag=f"ue{e_}_{j}")
            lds.append(tl)
        uef_done = emit_transpose(p, st, lambda j: ue_tm[:, j * D:(j + 1) * D],
                                  ue_fm, NCT, KC, lds + [tid], ps_tr,
                                  tag=f"uet{e_}")
        uef_prev = uef_done[-1]
        cons_h = []
        off = 0
        for tg_ in TGS:
            for f in range(KC):
                ps, war, b = ps_a.get()
                last = None
                for k in range(KC):
                    aft = [wt_[0]] + uef_done + ([war] if war and k == 0 else [])
                    last = p.on("tensor", lambda e, k=k, f=f, off=off, tg_=tg_,
                                ps=ps, slot=slot: e.matmul(
                        ps[:, :tg_],
                        wslot[slot][:, k * D + f * 128:k * D + (f + 1) * 128],
                        ue_fm[k][:, off:off + tg_],
                        start=(k == 0), stop=(k == KC - 1)),
                        after=(aft if k == 0 else []),
                        inc=(1 if k == KC - 1 else 0), tag=f"h{e_}_{f}_{off}")
                ev = p.on("scalar", lambda e, f=f, off=off, tg_=tg_, e_=e_, ps=ps:
                          e.activation(h_fm[f][:, off:off + tg_], ps[:, :tg_],
                                       AF.Relu,
                                       bias=b1cs[:, e_ * KC + f:e_ * KC + f + 1]),
                          after=[last, init_done], tag=f"hr{e_}_{f}_{off}")
                ps_a.consumed(b, ev)
                cons_h.append(ev)
            off += tg_
        cons_y = []
        for j in range(NCT):
            evs = []
            for dg in range(2):
                ps, war, b = ps_b.get()
                last = None
                for fk in range(KC):
                    aft = [wt_[1]] + cons_h + ([war] if war and fk == 0 else [])
                    last = p.on("tensor", lambda e, fk=fk, j=j, dg=dg, ps=ps,
                                slot=slot: e.matmul(
                        ps[:, :], h_fm[fk][:, j * 128:(j + 1) * 128],
                        wslot[slot][:, KC * D + fk * D + dg * 512:
                                    KC * D + fk * D + (dg + 1) * 512],
                        start=(fk == 0), stop=False),
                        after=(aft if fk == 0 else []), inc=0,
                        tag=f"y{e_}_{j}_{dg}")
                last = p.on("tensor", lambda e, e_=e_, dg=dg, ps=ps: e.matmul(
                    ps[:, :], oner8[0:1, :],
                    b2sb[0:1, e_ * D + dg * 512:e_ * D + (dg + 1) * 512],
                    start=False, stop=True), after=[initg_done], inc=1,
                    tag=f"yb{e_}_{j}_{dg}")
                ev = p.on("vector", lambda e, dg=dg, ps=ps: e.tensor_copy(
                    eo_tm[:, dg * 512:(dg + 1) * 512], ps[:, :]),
                    after=[last] + eo_war, tag=f"eoe{e_}_{j}_{dg}")
                ps_b.consumed(b, ev)
                evs.append(ev)
            ts_ = p.on("gpsimd", lambda e, e_=e_, j=j: e.dma_start(
                out=EOdd[e_ * CL + j * 128:e_ * CL + (j + 1) * 128, :],
                in_=eo_tm[:, :]), after=evs, dma_chan="eo", tag=f"eow{e_}_{j}")
            eo_war = [ts_]
            cons_y.append(ts_)
        wwar[slot] = cons_h + cons_y
    eo_all = p.chan_total("eo")
    gth_war = None
    for ti_ in range(NTT):
        sl = ti_ % 2
        tg1_ = p.on("gpsimd", lambda e, ti_=ti_: e.indirect_dma_start(
            gth[:, :], None, EOdd,
            IndirectOffsetOnAxis(ap=oc[:, ti_:ti_ + 1], axis=0)),
            after=[eo_all, tzr] + sc_done + ([gth_war] if gth_war else []),
            dma_chan="gt0", tag=f"g1{ti_}")
        tx = p.on("sync", lambda e, ti_=ti_, sl=sl: e.dma_start(
            out=xt[sl][:, :], in_=Xd[ti_ * 128:(ti_ + 1) * 128, :]),
            after=([xt_war[sl]] if xt_war[sl] else []), dma_chan=f"xt{sl}",
            tag=f"lx3{ti_}")
        t3_ = p.on("vector", lambda e, ti_=ti_, sl=sl: e.scalar_tensor_tensor(
            xt[sl][:, :], gth[:, :], kg[:, ti_:ti_ + 1], xt[sl][:, :],
            ALU.mult, ALU.add), after=[tx, tg1_] + kg_done, tag=f"out{ti_}")
        gth_war = t3_
        ts_ = p.on("sync", lambda e, ti_=ti_, sl=sl: e.dma_start(
            out=XOd[ti_ * 128:(ti_ + 1) * 128, :], in_=xt[sl][:, :]), after=[t3_],
            dma_chan=f"st{sl}", tag=f"sx{ti_}")
        xt_war[sl] = ts_
    return p


def build_l0ffn_sparse(nc, ctx, T=4096, TG=1024, C=4096, CL=768):
    Xd = nc.dram_tensor("X", [T, D], F32, kind="ExternalInput").ap()
    XOd = nc.dram_tensor("XO", [T, D], F32, kind="ExternalOutput").ap()
    wd = {}
    for nm, (shape, dt_) in l0ffn_shapes().items():
        wd[nm] = nc.dram_tensor(nm, shape, dt_, kind="ExternalInput").ap()
    cntd = nc.dram_tensor("cnt", [8, 8], F32, kind="ExternalInput").ap()
    prefd = nc.dram_tensor("pref", [8, 1], F32, kind="ExternalInput").ap()
    Udd = nc.dram_tensor("Ud", [T, D], BF16).ap()
    Uedd = nc.dram_tensor("Ued", [8 * CL + 1, D], BF16).ap()
    EOdd = nc.dram_tensor("EOd", [8 * CL + 1, D], BF16).ap()
    p = Prog(nc)
    emit_l0ffn_sparse(p, nc, ctx, Xd, XOd, wd, Udd, Uedd, EOdd, T=T, TG=TG,
                      C=C, CL=CL, cntd=cntd, prefd=prefd)
    p.finalize()
    p.make_sems()
    return p

NCORES = 8
TPC = 4096
L1CAP = 1280
L0CAP = 768

_cache = {}


def _decl_w(nc, pre, shapes, names):
    wd = {}
    for nm in names:
        shape, dt_ = shapes[nm]
        wd[nm] = nc.dram_tensor(pre + nm, shape, dt_, kind="ExternalInput").ap()
    return wd


def _build_phaseA():
    nc = bass.Bass("TRN2", target_bir_lowering=False, debug=False)
    ctx = contextlib.ExitStack()
    Xd = nc.dram_tensor("X", [TPC, D], F32, kind="ExternalInput").ap()
    CNTd = nc.dram_tensor("CNT", [8, 1], F32, kind="ExternalOutput").ap()
    wd = _decl_w(nc, "c0_", l0ffn_shapes(), ["rT", "br", "idn", "idf"])
    p = Prog(nc)
    with contextlib.ExitStack() as c1:
        emit_l0ffn(p, nc, c1, Xd, None, wd, T=TPC, mode="count", CNTd=CNTd)
    p.finalize()
    p.make_sems()
    with nc.Block() as blk:
        p.emit(blk)
    return nc, ctx


def _build_phaseB():
    nc = bass.Bass("TRN2", target_bir_lowering=False, debug=False)
    ctx = contextlib.ExitStack()
    Xd = nc.dram_tensor("X", [TPC, D], F32, kind="ExternalInput").ap()
    cntd = nc.dram_tensor("cnt", [8, 8], F32, kind="ExternalInput").ap()
    prefd = nc.dram_tensor("pref", [8, 1], F32, kind="ExternalInput").ap()
    X1d = nc.dram_tensor("X1", [TPC, D], F32).ap()
    X2d = nc.dram_tensor("X2", [TPC, D], F32, kind="ExternalOutput").ap()
    CNTd = nc.dram_tensor("CNT", [8, 1], F32, kind="ExternalOutput").ap()
    wd0 = _decl_w(nc, "f0_", l0ffn_shapes(), L0FFN_W)
    wda = _decl_w(nc, "a0_", l0attn_shapes(), L0ATTN_W)
    Udd = nc.dram_tensor("Ud", [TPC, D], BF16).ap()
    Uedd = nc.dram_tensor("Ued", [8 * L0CAP + 1, D], BF16).ap()
    EOdd = nc.dram_tensor("EOd", [8 * L0CAP + 1, D], BF16).ap()
    p = Prog(nc)
    with contextlib.ExitStack() as c1:
        emit_l0ffn_sparse(p, nc, c1, Xd, X1d, wd0, Udd, Uedd, EOdd, T=TPC,
                          C=4096, CL=L0CAP, cntd=cntd, prefd=prefd)
    p.barrier_all()
    with contextlib.ExitStack() as c2:
        emit_l0attn(p, nc, c2, X1d, X2d, wda, T=TPC, S=2048)
    p.barrier_all()
    with contextlib.ExitStack() as c3:
        emit_l0ffn(p, nc, c3, X2d, None, wd0, T=TPC, mode="count", CNTd=CNTd)
    p.finalize()
    p.make_sems()
    with nc.Block() as blk:
        p.emit(blk)
    return nc, ctx


def _build_phaseC(stages=4):
    nc = bass.Bass("TRN2", target_bir_lowering=False, debug=False)
    ctx = contextlib.ExitStack()
    Xd = nc.dram_tensor("X", [TPC, D], F32, kind="ExternalInput").ap()
    cntd = nc.dram_tensor("cnt", [8, 8], F32, kind="ExternalInput").ap()
    prefd = nc.dram_tensor("pref", [8, 1], F32, kind="ExternalInput").ap()
    X3d = nc.dram_tensor("X3", [TPC, D], F32).ap()
    X4d = nc.dram_tensor("X4", [TPC, D], F32).ap()
    X5d = nc.dram_tensor("X5", [TPC, D], F32).ap()
    XOd = nc.dram_tensor("XO", [TPC, D], F32, kind="ExternalOutput").ap()
    wd0 = _decl_w(nc, "f0_", l0ffn_shapes(), L0FFN_W)
    wd1 = _decl_w(nc, "f1_", l1ffns_shapes(), L1FFNS_W)
    wda = _decl_w(nc, "a1_", l1attn_shapes(), L1ATTN_W)
    Udd = nc.dram_tensor("Ud", [TPC, D], BF16).ap()
    Uedd = nc.dram_tensor("Ued", [8 * L1CAP + 1, D], BF16).ap()
    EOdd = nc.dram_tensor("EOd", [8 * L1CAP + 1, D], BF16).ap()
    p = Prog(nc)
    outs = [X3d, X4d, X5d, XOd]
    outs[stages - 1] = XOd
    with contextlib.ExitStack() as c1:
        emit_l0ffn_sparse(p, nc, c1, Xd, outs[0], wd0, Udd, Uedd, EOdd, T=TPC,
                          C=4096, CL=L0CAP, cntd=cntd, prefd=prefd)
    if stages >= 2:
        p.barrier_all()
        with contextlib.ExitStack() as c2:
            emit_l1ffn_sparse(p, nc, c2, outs[0], outs[1], wd1, Udd, Uedd, EOdd,
                              T=TPC, CAP=L1CAP)
    if stages >= 3:
        p.barrier_all()
        with contextlib.ExitStack() as c3:
            emit_l1attn(p, nc, c3, outs[1], outs[2], wda, T=TPC, S=2048)
    if stages >= 4:
        p.barrier_all()
        with contextlib.ExitStack() as c4:
            emit_l1ffn_sparse(p, nc, c4, outs[2], outs[3], wd1, Udd, Uedd, EOdd,
                              T=TPC, CAP=L1CAP)
    p.finalize()
    p.make_sems()
    with nc.Block() as blk:
        p.emit(blk)
    return nc, ctx


def _get(name, **kwb):
    if name not in _cache:
        _cache[name] = {"A": _build_phaseA, "B": _build_phaseB,
                        "C": _build_phaseC}[name](**kwb)
    return _cache[name]


def kernel(**kw):
    f32 = np.float32
    idn = np.eye(128, dtype=ml_dtypes.bfloat16)
    idf = np.eye(128, dtype=f32)
    Xs = [np.ascontiguousarray(kw["L0_attn_x"][2 * c:2 * c + 2].reshape(TPC, 1024))
          for c in range(NCORES)]
    prefs = [(np.arange(8) < c).astype(f32)[:, None] for c in range(NCORES)]
    CH = 18

    w0r = {"rT": np.ascontiguousarray(kw["L0_ffn_Wr"], dtype=f32),
           "br": np.ascontiguousarray(kw["L0_ffn_br"][None, :], dtype=f32),
           "idn": idn, "idf": idf}
    w0f = {**w0r,
           "W1": np.ascontiguousarray(kw["L0_ffn_W1"], dtype=f32),
           "W2": np.ascontiguousarray(kw["L0_ffn_W2"], dtype=f32),
           "b1c": np.ascontiguousarray(
               kw["L0_ffn_b1"].reshape(8, 8, 128).transpose(2, 0, 1).reshape(128, 64)),
           "b2": np.ascontiguousarray(kw["L0_ffn_b2"], dtype=f32)}
    wa0 = {
        "nwt": np.ascontiguousarray(np.tile(kw["L0_attn_norm_w"][None, :], (128, 1)),
                                    dtype=f32),
        "ipT": np.ascontiguousarray(kw["L0_attn_in_proj_w"].T, dtype=f32),
        "convw": np.ascontiguousarray(
            kw["L0_attn_conv_w"].reshape(CH, 128, 4).transpose(1, 0, 2).reshape(128, CH * 4)),
        "convb": np.ascontiguousarray(kw["L0_attn_conv_b"].reshape(CH, 128).T),
        "dtb": np.ascontiguousarray(kw["L0_attn_dt_bias"][None, :], dtype=f32),
        "alog": np.ascontiguousarray(kw["L0_attn_A_log"][None, :], dtype=f32),
        "dvec": np.ascontiguousarray(kw["L0_attn_D"][None, :], dtype=f32),
        "gnwt": np.ascontiguousarray(
            np.tile(kw["L0_attn_gate_norm_w"][None, :], (128, 1)), dtype=f32),
        "zwT": np.ascontiguousarray(kw["L0_attn_zero_w"].T, dtype=f32),
        "owT": np.ascontiguousarray(kw["L0_attn_one_w"].T, dtype=f32),
        "fT": np.ascontiguousarray(kw["L0_attn_fusion_w"].T, dtype=f32),
        "idn": idn, "idf": idf,
        "LTi": np.tril(np.ones((128, 128), f32)).T,
        "UTs": (np.arange(128)[:, None] > np.arange(128)[None, :]).astype(f32),
        "ALO": np.ones((128, 128), f32),
        "SDG": (np.arange(128)[:, None] == np.arange(128)[None, :] + 1).astype(f32) * -1e30,
        "selc": np.concatenate(
            [np.tile((np.arange(32) == h).astype(f32)[:, None], (1, 128))
             for h in range(32)], 1),
    }
    w1f = {"rT": np.ascontiguousarray(kw["L1_ffn_router_w"].T, dtype=f32),
           "gT": np.ascontiguousarray(np.transpose(kw["L1_ffn_gate_w"], (0, 2, 1)),
                                      dtype=f32),
           "uT": np.ascontiguousarray(np.transpose(kw["L1_ffn_up_w"], (0, 2, 1)),
                                      dtype=f32),
           "dT": np.ascontiguousarray(np.transpose(kw["L1_ffn_down_w"], (0, 2, 1)),
                                      dtype=f32),
           "idn": idn, "idf": idf,
           "ecap": np.tile((np.arange(8) * L1CAP - 1).astype(f32), (128, 1))}
    w1a = {"Wq": np.ascontiguousarray(kw["L1_attn_Wq"], dtype=f32),
           "Wk": np.ascontiguousarray(kw["L1_attn_Wk"], dtype=f32),
           "Wv": np.ascontiguousarray(kw["L1_attn_Wv"], dtype=f32),
           "Wo": np.ascontiguousarray(kw["L1_attn_Wo"], dtype=f32),
           "W1": np.ascontiguousarray(kw["L1_attn_W1"], dtype=f32),
           "W2": np.ascontiguousarray(kw["L1_attn_W2"], dtype=f32),
           "bqc": np.ascontiguousarray(kw["L1_attn_bq"].reshape(8, 128).T),
           "bkr": np.ascontiguousarray(kw["L1_attn_bk"][None, :], dtype=f32),
           "bvr": np.ascontiguousarray(kw["L1_attn_bv"][None, :], dtype=f32),
           "bor": np.ascontiguousarray(kw["L1_attn_bo"][None, :], dtype=f32),
           "b1c": np.ascontiguousarray(kw["L1_attn_b1"].reshape(32, 128).T),
           "b2t": np.ascontiguousarray(np.tile(kw["L1_attn_b2"][None, :], (128, 1))),
           "g1t": np.ascontiguousarray(np.tile(kw["L1_attn_g1"][None, :], (128, 1))),
           "be1t": np.ascontiguousarray(np.tile(kw["L1_attn_be1"][None, :], (128, 1))),
           "g2t": np.ascontiguousarray(np.tile(kw["L1_attn_g2"][None, :], (128, 1))),
           "be2t": np.ascontiguousarray(np.tile(kw["L1_attn_be2"][None, :], (128, 1))),
           "idn": idn}

    def px(d, pre):
        return {pre + k: v for k, v in d.items()}

    ncA, _ = _get("A")
    resA = run_bass_kernel_spmd(
        ncA, [{"X": Xs[c], **px(w0r, "c0_")} for c in range(NCORES)],
        core_ids=list(range(NCORES)))
    cnt1 = np.ascontiguousarray(np.stack(
        [np.asarray(resA.results[c]["CNT"])[:, 0] for c in range(NCORES)], 0),
        dtype=f32)

    ncB, _ = _get("B")
    mapsB = [{"X": Xs[c], "cnt": cnt1, "pref": prefs[c],
              **px(w0f, "f0_"), **px(wa0, "a0_")} for c in range(NCORES)]
    resB = run_bass_kernel_spmd(ncB, mapsB, core_ids=list(range(NCORES)))
    X2s = [np.asarray(resB.results[c]["X2"]) for c in range(NCORES)]
    cnt2 = np.ascontiguousarray(np.stack(
        [np.asarray(resB.results[c]["CNT"])[:, 0] for c in range(NCORES)], 0),
        dtype=f32)

    ncC, _ = _get("C")
    mapsC = [{"X": X2s[c], "cnt": cnt2, "pref": prefs[c],
              **px(w0f, "f0_"), **px(w1f, "f1_"), **px(w1a, "a1_")}
             for c in range(NCORES)]
    resC = run_bass_kernel_spmd(ncC, mapsC, core_ids=list(range(NCORES)))

    out = np.empty((16, 2048, 1024), np.float32)
    for c in range(NCORES):
        out[2 * c:2 * c + 2] = np.asarray(resC.results[c]["XO"]).reshape(2, 2048, 1024)
    return out

